# version 01
# baseline (speedup 1.0000x reference)
from collections import defaultdict
from contextlib import ExitStack
import numpy as np
import ml_dtypes
import concourse.bass as bass
import concourse.mybir as mybir
from concourse.bass_utils import run_bass_kernel_spmd

F32 = mybir.dt.float32
BF16 = mybir.dt.bfloat16
I32 = mybir.dt.int32
AF = mybir.ActivationFunctionType
ALU = mybir.AluOpType
AX = mybir.AxisListType

NCORE = 8
T = 2048
NT = T // 128
D = 1024
ND = D // 128
NQC = 4


def set_dims(t):
    global T, NT, NQC
    T = t
    NT = T // 128
    NQC = max(1, T // 512)

IDENT_BF = np.eye(128, dtype=ml_dtypes.bfloat16)


class Sem:
    def __init__(self, handle, name):
        self.h = handle
        self.name = name
        self.v = 0

    def add(self, n=1):
        self.v += n
        return self.v


class Prog:
    def __init__(self, nc, es):
        self.nc = nc
        self.es = es
        self.streams = defaultdict(list)
        self._n = 0
        self.sbuf_bytes = 0
        self.cpool_vals = []
        self.cpool_sb = None
        self.s_cpool = None
        self.flu = {}

    def constf(self, val):
        val = float(val)
        if val not in self.cpool_vals:
            assert len(self.cpool_vals) < 32
            self.cpool_vals.append(val)
        k = self.cpool_vals.index(val)
        return self.cpool_sb[:, k:k + 1]

    def sem(self, name=None):
        self._n += 1
        h = self.es.enter_context(self.nc.semaphore(name or f"s{self._n}"))
        return Sem(h, name or f"s{self._n}")

    def sb(self, shape, dtype, name=None):
        self._n += 1
        self.sbuf_bytes += int(np.prod(shape[1:])) * mybir.dt.size(dtype)
        return self.es.enter_context(
            self.nc.sbuf_tensor(name or f"t{self._n}", list(shape), dtype))

    def ps(self, shape, dtype=F32, name=None):
        self._n += 1
        return self.es.enter_context(
            self.nc.psum_tensor(name or f"p{self._n}", list(shape), dtype))

    def scope(self):
        return ExitStack()

    def sb_in(self, scope, shape, dtype, name=None):
        self._n += 1
        return scope.enter_context(
            self.nc.sbuf_tensor(name or f"t{self._n}", list(shape), dtype))

    def ps_in(self, scope, shape, dtype=F32, name=None):
        self._n += 1
        return scope.enter_context(
            self.nc.psum_tensor(name or f"p{self._n}", list(shape), dtype))

    def on(self, eng):
        def deco(fn):
            self.streams[eng].append(fn)
        return deco

    def finalize(self, block):
        for eng in ("sync", "vector", "scalar", "gpsimd", "tensor"):
            fns = list(self.streams[eng])
            def body(e, fns=fns):
                for fn in fns:
                    fn(e)
            getattr(block, eng)(body)


def chain(p, eng, waits, steps, flush="auto"):
    n = len(steps)
    if flush == "all":
        fl = [True] * n
    elif flush == "auto":
        fl = [True] * n
    else:
        fl = list(flush) + [True]
        fl = fl[:n - 1] + [True]
    sem = p.flu[eng]
    base = sem.v
    incs = [i for i in range(n) if fl[i] or i == n - 1]
    cum = {}
    c = base
    for i in range(n):
        if i in incs or fl[i]:
            c += 1
            cum[i] = c
    sem.v = c

    @p.on(eng)
    def _(e, steps=steps, waits=waits, fl=fl, cum=cum, n=n):
        for sm, val in waits:
            e.wait_ge(sm.h, val)
        for i, fn in enumerate(steps):
            ins = fn(e)
            if i in cum:
                ins.then_inc(sem.h, 1)
                if fl[i] and i < n - 1:
                    e.wait_ge(sem.h, cum[i])
    return (sem, c)


class IO:
    def __init__(self, nc):
        self.nc = nc

    def inp(self, name, shape, dtype=F32):
        return self.nc.dram_tensor(name, list(shape), dtype, kind="ExternalInput").ap()

    def out(self, name, shape, dtype=F32):
        return self.nc.dram_tensor(name, list(shape), dtype, kind="ExternalOutput").ap()

    def internal(self, name, shape, dtype=F32):
        return self.nc.dram_tensor(name, list(shape), dtype).ap()


def build_program(emit, debug=False):
    nc = bass.Bass("TRN2", target_bir_lowering=False, debug=debug)
    es = ExitStack()
    p = Prog(nc, es)
    cpool_dram = nc.dram_tensor("cpool", [128, 32], F32, kind="ExternalInput").ap()
    with es:
        block = es.enter_context(nc.Block())
        p.cpool_sb = p.sb([128, 32], F32, "cpool_sb")
        p.s_cpool = p.sem("cpool")
        for eng in ("vector", "scalar", "gpsimd", "tensor"):
            p.flu[eng] = p.sem("flu_" + eng)

        @p.on("sync")
        def _(sy):
            sy.dma_start(out=p.cpool_sb[:], in_=cpool_dram).then_inc(p.s_cpool.h, 16)
        p.s_cpool.add(16)
        emit(nc, p, IO(nc))
        p.finalize(block)
    return nc, list(p.cpool_vals)


def cpool_array(vals):
    a = np.zeros((128, 32), np.float32)
    for i, v in enumerate(vals):
        a[:, i] = v
    return a



def emit_load_tiles(p, dram, tiles, rows=128, queue="sync", waits=()):
    s = p.sem()
    @p.on(queue)
    def _(e):
        for sm, val in waits:
            e.wait_ge(sm.h, val)
        for i in range(len(tiles)):
            if i >= 1:
                e.wait_ge(s.h, 16 * i)
            e.dma_start(out=tiles[i][:], in_=dram[i * rows:(i + 1) * rows, :]
                        ).then_inc(s.h, 16)
    s.add(16 * len(tiles))
    return s


def emit_ln(p, x_tiles, xn_tiles, wait_fn, rms=False, eps=1e-5, n_tiles=None,
            width=D, sc=None, after_tile=None, nb=2):
    NB = nb
    if n_tiles is None:
        n_tiles = NT
    mk = (lambda shape, dt: p.sb_in(sc, shape, dt)) if sc is not None else \
         (lambda shape, dt: p.sb(shape, dt))
    st = [{k: mk([128, 1], F32) for k in ("m", "s2", "mm", "sd", "rstd")}
          | {"x2": mk([128, width], F32)} for _ in range(NB)]
    s_sq = p.sem()
    s_var = p.sem()
    s_sd = p.sem()
    s_fin = p.sem()
    s_flush = p.sem()
    eps_ap = p.constf(eps)
    for i in range(n_tiles):
        b = st[i % NB]
        waits = wait_fn(i)

        @p.on("scalar")
        def _(s, i=i, b=b, waits=waits):
            for sm, val in waits:
                s.wait_ge(sm.h, val)
            if i >= NB:
                s.wait_ge(s_fin.h, i - NB + 1)
            s.activation(b["x2"][:], x_tiles[i][:], AF.Square).then_inc(s_sq.h, 1)
        v_sq = s_sq.add()

        fl = s_flush.v

        @p.on("vector")
        def _(v, i=i, b=b, waits=waits, v_sq=v_sq, fl=fl):
            for sm, val in waits:
                v.wait_ge(sm.h, val)
            if i >= NB:
                v.wait_ge(s_fin.h, i - NB + 1)
            k = fl
            if not rms:
                v.tensor_reduce(b["m"][:], x_tiles[i][:], AX.X, ALU.add
                                ).then_inc(s_flush.h, 1); k += 1
                v.wait_ge(s_flush.h, k)
                v.tensor_scalar(b["m"][:], b["m"][:], 1.0 / width, 0.0,
                                ALU.mult, ALU.add).then_inc(s_flush.h, 1); k += 1
                v.wait_ge(s_flush.h, k)
            v.wait_ge(s_sq.h, v_sq)
            v.tensor_reduce(b["s2"][:], b["x2"][:], AX.X, ALU.add
                            ).then_inc(s_flush.h, 1); k += 1
            v.wait_ge(s_flush.h, k)
            if not rms:
                v.tensor_mul(b["mm"][:], b["m"][:], b["m"][:]
                             ).then_inc(s_flush.h, 1); k += 1
                v.wait_ge(s_flush.h, k)
                v.scalar_tensor_tensor(b["s2"][:], b["s2"][:], 1.0 / width,
                                       b["mm"][:], ALU.mult, ALU.subtract
                                       ).then_inc(s_var.h, 1)
            else:
                v.tensor_scalar(b["s2"][:], b["s2"][:], 1.0 / width, 0.0,
                                ALU.mult, ALU.add).then_inc(s_var.h, 1)
        s_flush.add(4 if not rms else 1)
        v_var = s_var.add()

        @p.on("scalar")
        def _(s, b=b, v_var=v_var, eps_ap=eps_ap):
            s.wait_ge(s_var.h, v_var)
            s.wait_ge(p.s_cpool.h, 16)
            s.activation(b["sd"][:], b["s2"][:], AF.Sqrt, bias=eps_ap
                         ).then_inc(s_sd.h, 1)
        v_sd = s_sd.add()

        fl2 = s_flush.v

        @p.on("vector")
        def _(v, i=i, b=b, v_sd=v_sd, fl2=fl2):
            v.wait_ge(s_sd.h, v_sd)
            v.reciprocal(b["rstd"][:], b["sd"][:]).then_inc(s_flush.h, 1)
            v.wait_ge(s_flush.h, fl2 + 1)
            v.scalar_tensor_tensor(
                xn_tiles[i][:], x_tiles[i][:],
                (b["m"][:] if not rms else 0.0),
                b["rstd"][:].broadcast_to((128, width)),
                ALU.subtract if not rms else ALU.add,
                ALU.mult).then_inc(s_fin.h, 1)
        s_flush.add(1)
        s_fin.add()
        if after_tile is not None:
            after_tile(i, s_fin, s_fin.v)
    return s_fin


def emit_transpose_fm(p, src_tiles, dst_fm, ident_bf, wait_fn, n_tiles=None,
                      ncols=ND, col_off=0, pt=None, sems=None):
    if n_tiles is None:
        n_tiles = NT
    if pt is None:
        pt = [p.ps([128, 128], BF16) for _ in range(2)]
    s_mm, s_ev = sems if sems else (p.sem(), p.sem())
    for i in range(n_tiles):
        for c in range(ncols):
            k = i * ncols + c

            @p.on("tensor")
            def _(t, i=i, c=c, k=k):
                for sm, val in wait_fn(i):
                    t.wait_ge(sm.h, val)
                if k >= 2:
                    t.wait_ge(s_ev.h, k - 1)
                t.transpose(pt[k % 2][:], src_tiles[i][:, c * 128:(c + 1) * 128],
                            ident_bf[:]).then_inc(s_mm.h, 1)
            v_mm = s_mm.add()

            @p.on("vector")
            def _(v, i=i, c=c, v_mm=v_mm, k=k):
                v.wait_ge(s_mm.h, v_mm)
                v.tensor_copy(dst_fm[c][:, (col_off + i) * 128:(col_off + i + 1) * 128],
                              pt[k % 2][:]).then_inc(s_ev.h, 1)
            s_ev.add()
    return s_ev


def emit_weight_load_bf16(p, w_dram, w_tiles, wait=(), queue="gpsimd"):
    s = p.sem()
    @p.on(queue)
    def _(g):
        for sm, val in wait:
            g.wait_ge(sm.h, val)
        for i in range(len(w_tiles)):
            g.dma_start(out=w_tiles[i][:], in_=w_dram[i * 128:(i + 1) * 128, :]
                        ).then_inc(s.h, 16)
    s.add(16 * len(w_tiles))
    return s


def emit_bias_bcast(p, brow_sb, ncols, ones_f32, dst, s_ready, pb=None):
    nchunk = (ncols + 511) // 512
    if pb is None:
        pb = p.ps([128, 512], F32)
    s_mm = p.sem()
    s_ev = p.sem()
    for j in range(nchunk):
        w = min(512, ncols - j * 512)

        @p.on("tensor")
        def _(t, j=j, w=w):
            for sm, val in s_ready:
                t.wait_ge(sm.h, val)
            if j > 0:
                t.wait_ge(s_ev.h, j)
            t.matmul(pb[:, 0:w], ones_f32[:], brow_sb[:, j * 512:j * 512 + w],
                     start=True, stop=True).then_inc(s_mm.h, 1)
        v_mm = s_mm.add()

        @p.on("vector")
        def _(v, j=j, w=w, v_mm=v_mm):
            v.wait_ge(s_mm.h, v_mm)
            v.tensor_copy(dst[:, j * 512:j * 512 + w], pb[:, 0:w]).then_inc(s_ev.h, 1)
        s_ev.add()
    return s_ev



DEBUG_F0 = False

def emit_f0(nc, p, io):
    x_in = io.inp("x", [T, D])
    if DEBUG_F0:
        dbg_xn = io.out("dbg_xn", [128, D], BF16)
        dbg_wct = io.out("dbg_wct", [128, D], BF16)
        dbg_bias = io.out("dbg_bias", [128, D])
        dbg_xnfm = io.out("dbg_xnfm", [128, 128], BF16)
    wt = io.inp("WT", [D, D])
    b = io.inp("b", [1, D])
    ld_ = io.inp("lora_down", [4, 100])
    lu_ = io.inp("lora_up", [50, 4])
    ldT = io.inp("lora_downT", [100, 4])
    luT = io.inp("lora_upT", [4, 50])
    da = io.inp("down_aux", [100, D])
    uaT = io.inp("up_auxT", [50, D])
    ident = io.inp("ident_bf", [128, 128], BF16)
    ones1 = io.inp("ones1", [1, 128])
    x_out = io.out("y", [T, D])

    x_tiles = [p.sb([128, D], F32) for _ in range(NT)]
    xn_tiles = [p.sb([128, D], BF16) for _ in range(NT)]
    xn_fm = [p.sb([128, T], BF16) for _ in range(ND)]
    wct = [p.sb([128, D], BF16) for _ in range(ND)]
    ident_sb = p.sb([128, 128], BF16)
    ones_sb = p.sb([128, 128], F32)
    brow = p.sb([1, D], F32)
    bias_bc = p.sb([128, D], F32)

    s_ld = emit_load_tiles(p, x_in, x_tiles)
    s_misc = p.sem()

    @p.on("sync")
    def _(sy):
        sy.dma_start(out=ident_sb[:], in_=ident).then_inc(s_misc.h, 16)
        sy.dma_start(out=ones_sb[0:1, :], in_=ones1).then_inc(s_misc.h, 16)
        sy.dma_start(out=brow[:], in_=b).then_inc(s_misc.h, 16)
    s_misc.add(48)

    lsb = {
        "ldT": p.sb([128, 4], F32), "luT": p.sb([128, 50], F32),
        "da": p.sb([128, D], F32), "uaT": p.sb([128, D], F32),
        "t1": p.sb([128, D], F32), "t2": p.sb([128, D], F32),
    }
    s_lw = p.sem()

    @p.on("sync")
    def _(sy):
        sy.dma_start(out=lsb["ldT"][0:100, :], in_=ldT).then_inc(s_lw.h, 16)
        sy.dma_start(out=lsb["luT"][0:4, :], in_=luT).then_inc(s_lw.h, 16)
        sy.dma_start(out=lsb["da"][0:100, :], in_=da).then_inc(s_lw.h, 16)
        sy.dma_start(out=lsb["uaT"][0:50, :], in_=uaT).then_inc(s_lw.h, 16)
    s_lw.add(64)
    pl = p.ps([128, 512], F32)
    s_lmm = p.sem()
    s_lev = p.sem()
    for j in range(2):
        @p.on("tensor")
        def _(t, j=j):
            t.wait_ge(s_lw.h, 64)
            if j > 0:
                t.wait_ge(s_lev.h, j)
            t.matmul(pl[0:4, :], lsb["ldT"][0:100, :],
                     lsb["da"][0:100, j * 512:(j + 1) * 512],
                     start=True, stop=True).then_inc(s_lmm.h, 1)
        vm = s_lmm.add()

        @p.on("vector")
        def _(v, j=j, vm=vm):
            v.wait_ge(s_lmm.h, vm)
            v.tensor_copy(lsb["t1"][0:4, j * 512:(j + 1) * 512],
                          pl[0:4, :]).then_inc(s_lev.h, 1)
        s_lev.add()
    for j in range(2):
        @p.on("tensor")
        def _(t, j=j):
            t.wait_ge(s_lev.h, 2 + j)
            t.matmul(pl[0:50, :], lsb["luT"][0:4, :],
                     lsb["t1"][0:4, j * 512:(j + 1) * 512],
                     start=True, stop=True).then_inc(s_lmm.h, 1)
        vm = s_lmm.add()

        @p.on("vector")
        def _(v, j=j, vm=vm):
            v.wait_ge(s_lmm.h, vm)
            v.tensor_copy(lsb["t2"][0:50, j * 512:(j + 1) * 512],
                          pl[0:50, :]).then_inc(s_lev.h, 1)
        s_lev.add()
    wtf = p.sb([128, D], F32)
    s_wt = p.sem()
    plw = [p.ps([128, 512], F32) for _ in range(2)]
    s_wmm = p.sem()
    s_wev = p.sem()
    for c in range(ND):
        @p.on("sync")
        def _(sy, c=c):
            if c > 0:
                sy.wait_ge(s_wev.h, 2 * c)
            sy.dma_start(out=wtf[:], in_=wt[c * 128:(c + 1) * 128, :]
                         ).then_inc(s_wt.h, 16)
        s_wt.add(16)
        for j in range(2):
            @p.on("tensor")
            def _(t, c=c, j=j):
                t.wait_ge(s_lev.h, 4)
                if c * 2 + j >= 2:
                    t.wait_ge(s_wev.h, c * 2 + j - 1)
                t.matmul(plw[j % 2][:], lsb["t2"][0:50, c * 128:(c + 1) * 128],
                         lsb["uaT"][0:50, j * 512:(j + 1) * 512],
                         start=True, stop=True).then_inc(s_wmm.h, 1)
            vm = s_wmm.add()

            @p.on("vector")
            def _(v, c=c, j=j, vm=vm):
                v.wait_ge(s_wmm.h, vm)
                v.wait_ge(s_wt.h, 16 * (c + 1))
                v.tensor_copy(wct[c][:, j * 512:(j + 1) * 512], plw[j % 2][:])
                v.tensor_tensor(wct[c][:, j * 512:(j + 1) * 512],
                                wct[c][:, j * 512:(j + 1) * 512],
                                wtf[:, j * 512:(j + 1) * 512],
                                ALU.add).then_inc(s_wev.h, 1)
            s_wev.add()
    s_wdone_v = s_wev.v

    s_bias = emit_bias_bcast(p, brow, D, ones_sb[0:1, 0:128], bias_bc,
                             [(s_misc, 48)])

    s_ln = emit_ln(p, x_tiles, xn_tiles,
                   lambda i: [(s_ld, 16 * (i + 1))])
    s_tp = emit_transpose_fm(p, xn_tiles, xn_fm, ident_sb,
                             lambda i: [(s_ln, i + 1), (s_misc, 16)])

    pmm = [p.ps([128, 512], F32) for _ in range(2)]
    s_mm = p.sem()
    s_ev = p.sem()
    s_st = p.sem()
    yt = [p.sb([128, D], F32) for _ in range(2)]
    for i in range(NT):
        for j in range(2):
            @p.on("tensor")
            def _(t, i=i, j=j):
                t.wait_ge(s_tp.h, (i + 1) * ND)
                t.wait_ge(s_wev.h, s_wdone_v)
                k = i * 2 + j
                if k >= 2:
                    t.wait_ge(s_ev.h, k - 1)
                ins = None
                for c in range(ND):
                    ins = t.matmul(pmm[j][:], xn_fm[c][:, i * 128:(i + 1) * 128],
                                   wct[c][:, j * 512:(j + 1) * 512],
                                   start=(c == 0), stop=(c == ND - 1))
                ins.then_inc(s_mm.h, 1)
            vm = s_mm.add()

            @p.on("vector")
            def _(v, i=i, j=j, vm=vm):
                v.wait_ge(s_mm.h, vm)
                v.wait_ge(s_bias.h, 2)
                if i >= 1:
                    v.wait_ge(s_st.h, 32 * i)
                v.tensor_tensor(yt[j][:, j * 512:(j + 1) * 512], pmm[j][:],
                                bias_bc[:, j * 512:(j + 1) * 512], ALU.add
                                ).then_inc(s_ev.h, 1)
            s_ev.add()

        gfl = p.flu["gpsimd"].add(2)

        @p.on("gpsimd")
        def _(g, i=i, gfl=gfl):
            g.wait_ge(s_ev.h, (i + 1) * 2)
            for j in range(2):
                g.tensor_tensor(yt[j][:, j * 512:(j + 1) * 512],
                                yt[j][:, j * 512:(j + 1) * 512],
                                x_tiles[i][:, j * 512:(j + 1) * 512], ALU.add
                                ).then_inc(p.flu["gpsimd"].h, 1)
            g.wait_ge(p.flu["gpsimd"].h, gfl)
            g.dma_start(out=x_out[i * 128:(i + 1) * 128, 0:512],
                        in_=yt[0][:, 0:512]).then_inc(s_st.h, 16)
            g.dma_start(out=x_out[i * 128:(i + 1) * 128, 512:1024],
                        in_=yt[1][:, 512:1024]).then_inc(s_st.h, 16)
            g.wait_ge(s_st.h, 32 * (i + 1))
        s_st.add(32)

    if DEBUG_F0:
        @p.on("gpsimd")
        def _(g):
            g.wait_ge(s_ev.h, 2)
            g.dma_start(out=dbg_xn, in_=xn_tiles[0][:]).then_inc(s_st.h, 16)
            g.dma_start(out=dbg_wct, in_=wct[0][:]).then_inc(s_st.h, 16)
            g.dma_start(out=dbg_bias, in_=bias_bc[:]).then_inc(s_st.h, 16)
            g.dma_start(out=dbg_xnfm, in_=xn_fm[0][:, 0:128]).then_inc(s_st.h, 16)
        s_st.add(64)

    @p.on("gpsimd")
    def _(g):
        g.wait_ge(s_st.h, s_st.v)




def emit_attn_a(nc, p, io, variant):
    x_in = io.inp("x", [T, D])
    xp_in = io.inp("xp", [T, D])
    w_d = {k: io.inp("W" + k, [D, D]) for k in ("q", "k", "v", "o")}
    b_d = {k: io.inp("b" + k, [1, D]) for k in ("q", "k", "v", "o")}
    ident_bf_d = io.inp("ident_bf", [128, 128], BF16)
    ident_f_d = io.inp("ident_f", [128, 128])
    ones1_d = io.inp("ones1", [1, 128])
    y_out = io.out("y", [T, D])

    NH, DH = 16, 64
    NKT = 2 * NT
    L0 = (variant == 0)

    K_FM = [p.sb([128, 2 * T], BF16) for _ in range(ND)]
    V_AUG = [p.sb([128, NH * 65], BF16) for _ in range(NKT)]
    q_scr = io.internal("q_scr", [T, D], BF16)
    ident_bf = p.sb([128, 128], BF16)
    ident_f = p.sb([128, 128], F32) if L0 else None
    ones_f = p.sb([1, 128], F32)
    negB = p.sb([128, NH], F32)
    s_misc = p.sem()

    @p.on("sync")
    def _(sy):
        sy.dma_start(out=ident_bf[:], in_=ident_bf_d).then_inc(s_misc.h, 16)
        sy.wait_ge(s_misc.h, 16)
        if L0:
            sy.dma_start(out=ident_f[:], in_=ident_f_d).then_inc(s_misc.h, 16)
        else:
            sy.dma_start(out=ones_f[:], in_=ones1_d).then_inc(s_misc.h, 16)
        sy.wait_ge(s_misc.h, 32)
        sy.dma_start(out=ones_f[:], in_=ones1_d).then_inc(s_misc.h, 16)
    s_misc.add(48)

    c_vone = chain(p, "vector", [], [
        (lambda v, ti=ti: v.memset(V_AUG[ti][:, 64:NH * 65:65], 1.0))
        for ti in range(NKT)], flush=[False] * NKT)

    if L0:
        Q2M = p.sb([128, NH], F32)
        K2M = p.sb([128, NH], F32)
        c_m0 = chain(p, "vector", [], [
            lambda v: v.memset(Q2M[:], 0.0),
            lambda v: v.memset(K2M[:], 0.0)], flush=[False, False])
    else:
        c_m0 = None

    scAB = p.scope()
    w_sb = [p.sb_in(scAB, [128, D], BF16) for _ in range(ND)]
    bias_bc = {k: p.sb_in(scAB, [128, D], BF16) for k in ("k", "v", "q")}
    ktmp = p.sb_in(scAB, [128, D], F32)
    x2t = p.sb_in(scAB, [128, D], F32)
    kn2 = p.sb_in(scAB, [128, NH], F32)
    ksd = p.sb_in(scAB, [128, NH], F32)
    kbf = p.sb_in(scAB, [128, D], BF16)
    qst = [p.sb_in(scAB, [128, D], BF16) for _ in range(2)]
    s_qdma = p.sem()
    pps = [p.ps_in(scAB, [128, 512], F32) for _ in range(2)]
    ptp = [p.ps_in(scAB, [128, 128], BF16) for _ in range(2)]
    ptp2 = [p.ps_in(scAB, [128, 128], BF16) for _ in range(2)]
    s_bias2 = p.sem()

    @p.on("gpsimd")
    def _(g):
        for ki, key in enumerate(("k", "v", "q")):
            if ki >= 1:
                g.wait_ge(s_bias2.h, 16 * ki)
            g.dma_start(out=bias_bc[key][:],
                        in_=b_d[key].broadcast_to((128, D))
                        ).then_inc(s_bias2.h, 16)
    s_bias2.add(48)
    s_bias = {"k": (s_bias2, 16), "v": (s_bias2, 32), "q": (s_bias2, 48)}

    s_ktp = p.sem()
    n_ktp = [0]
    prev_w = [None]
    prev_fin = [[]]
    s_pmm = p.sem()
    s_pev = p.sem()

    def proj_pass(xnfm, wdram, bias_key, mode, dest, dst_off, s_tp, i0=0,
                  ntile=None, tp_base=0):
        s_w = p.sem()
        s_mm = s_pmm
        s_ev = s_pev
        ev_base = s_pev.v
        pw_cap = prev_w[0]

        @p.on("gpsimd")
        def _(g, pw_cap=pw_cap):
            if pw_cap is not None:
                g.wait_ge(pw_cap[0].h, pw_cap[1])
            for c in range(ND):
                if c >= 1:
                    g.wait_ge(s_w.h, 16 * c)
                g.dma_start(out=w_sb[c][:], in_=wdram[c * 128:(c + 1) * 128, :]
                            ).then_inc(s_w.h, 16)
        s_w.add(16 * ND)
        if ntile is None:
            ntile = NT
        comp = []
        for i in range(i0, i0 + ntile):
            for jc in range(2):
                k = ev_base + 2 * (i - i0) + jc
                pf = list(prev_fin[0])

                @p.on("tensor")
                def _(t, i=i, jc=jc, k=k):
                    t.wait_ge(s_w.h, 16 * ND)
                    t.wait_ge(s_tp.h, ND * (i + 1))
                    if k >= 2:
                        t.wait_ge(s_ev.h, k - 1)
                    ins = None
                    for c in range(ND):
                        ins = t.matmul(pps[jc][:],
                                       xnfm[c][:, (i - i0) * 128:(i - i0 + 1) * 128],
                                       w_sb[c][:, jc * 512:(jc + 1) * 512],
                                       start=(c == 0), stop=(c == ND - 1))
                    ins.then_inc(s_mm.h, 1)
                v_mm = s_mm.add()

                @p.on("vector")
                def _(v, jc=jc, v_mm=v_mm, pf=pf):
                    v.wait_ge(s_mm.h, v_mm)
                    v.wait_ge(s_bias[bias_key][0].h, s_bias[bias_key][1])
                    for sm, val in pf:
                        v.wait_ge(sm.h, val)
                    v.tensor_tensor(ktmp[:, jc * 512:(jc + 1) * 512], pps[jc][:],
                                    bias_bc[bias_key][:, jc * 512:(jc + 1) * 512],
                                    ALU.add).then_inc(s_ev.h, 1)
                s_ev.add()
            v_ev = s_ev.v
            readers = []

            if mode == "maxtrack":
                c_sq = chain(p, "scalar", [(s_ev, v_ev)], [
                    lambda s: s.activation(x2t[:], ktmp[:], AF.Square)])
                readers.append(c_sq)
                acc = Q2M if dest == "Q" else K2M
                dst_t = kbf if dest == "K" else qst[i % 2]
                c_fin = chain(p, "vector", [c_sq, c_m0] +
                              ([(s_qdma, 16 * (i - 1))] if (dest == "Q" and i >= 2) else []), [
                    lambda v: v.tensor_reduce(
                        kn2[:].rearrange("p (g o) -> p g o", o=1),
                        x2t[:].rearrange("p (g x) -> p g x", g=NH),
                        AX.X, ALU.add),
                    lambda v, acc=acc: v.tensor_tensor(acc[:], acc[:], kn2[:],
                                                       ALU.max),
                    lambda v, dst_t=dst_t: v.tensor_copy(dst_t[:], ktmp[:]),
                ], flush=[True, False, True])
                readers.append(c_fin)
            elif mode == "normalize":
                c_sq = chain(p, "scalar", [(s_ev, v_ev)], [
                    lambda s: s.activation(x2t[:], ktmp[:], AF.Square)])
                c_red = chain(p, "vector", [c_sq], [
                    lambda v: v.tensor_reduce(
                        kn2[:].rearrange("p (g o) -> p g o", o=1),
                        x2t[:].rearrange("p (g x) -> p g x", g=NH),
                        AX.X, ALU.add)])
                c_sd = chain(p, "scalar", [c_red, (p.s_cpool, 16)], [
                    lambda s: s.activation(ksd[:], kn2[:], AF.Sqrt,
                                           bias=p.constf(0.0))])
                readers.append(c_sd)
                dst_t = kbf if dest == "K" else qst[i % 2]
                steps = [lambda v: v.tensor_scalar(ksd[:], ksd[:], 1.0, 1e-8,
                                                   ALU.mult, ALU.add),
                         lambda v: v.reciprocal(kn2[:], ksd[:])]
                steps += [
                    (lambda v, h=h, dst_t=dst_t: v.tensor_tensor(
                        dst_t[:, h * DH:(h + 1) * DH],
                        ktmp[:, h * DH:(h + 1) * DH],
                        kn2[:, h:h + 1].broadcast_to((128, DH)), ALU.mult))
                    for h in range(NH)]
                c_fin = chain(p, "vector", [c_sd] +
                              ([(s_qdma, 16 * (i - 1))] if (dest == "Q" and i >= 2) else []),
                              steps, flush=[True, True] + [False] * NH)
                readers.append(c_fin)
            else:
                if dest == "V":
                    steps = [
                        (lambda v, h=h, i=i: v.tensor_copy(
                            V_AUG[dst_off + i][:, h * 65:h * 65 + 64],
                            ktmp[:, h * DH:(h + 1) * DH]))
                        for h in range(NH)]
                else:
                    dst_t = kbf if dest == "K" else qst[i % 2]
                    steps = [lambda v, dst_t=dst_t: v.tensor_copy(dst_t[:],
                                                                  ktmp[:])]
                c_fin = chain(p, "vector", [(s_ev, v_ev)], steps,
                              flush=[False] * len(steps))
                readers.append(c_fin)

            if dest == "Q":
                qv = s_qdma.v

                @p.on("gpsimd")
                def _(g, i=i, c_fin=c_fin, qv=qv):
                    g.wait_ge(c_fin[0].h, c_fin[1])
                    if qv:
                        g.wait_ge(s_qdma.h, qv)
                    g.dma_start(out=q_scr[i * 128:(i + 1) * 128, :],
                                in_=qst[i % 2][:]).then_inc(s_qdma.h, 16)
                s_qdma.add(16)
                readers.append((s_qdma, s_qdma.v))
            if dest == "K":
                tb = n_ktp[0]
                for c in range(ND):
                    kk = tb + c

                    @p.on("tensor")
                    def _(t, c=c, kk=kk, c_fin=c_fin):
                        t.wait_ge(c_fin[0].h, c_fin[1])
                        t.wait_ge(s_misc.h, 16)
                        if kk >= 2:
                            t.wait_ge(s_ktp.h, kk - 1)
                        t.transpose(ptp2[kk % 2][:],
                                    kbf[:, c * 128:(c + 1) * 128],
                                    ident_bf[:]).then_inc(s_mm.h, 1)
                    v_t = s_mm.add()

                    @p.on("vector")
                    def _(v, c=c, kk=kk, v_t=v_t, i=i):
                        v.wait_ge(s_mm.h, v_t)
                        v.tensor_copy(
                            K_FM[c][:, (dst_off + i) * 128:(dst_off + i + 1) * 128],
                            ptp2[kk % 2][:]).then_inc(s_ktp.h, 1)
                    s_ktp.add()
                n_ktp[0] += ND
                readers.append((s_ktp, n_ktp[0]))
            prev_fin[0] = readers
            comp = readers
        prev_w[0] = (s_mm, s_mm.v)
        return comp

    deps_AB = []
    prev_half_ld = [None]

    def emit_half(xd, qpass, dst_off):
        sc = p.scope()
        NROT = 2
        NSUB = NT // 2
        xr = [p.sb_in(sc, [128, D], F32) for _ in range(NROT)]
        xnr = [p.sb_in(sc, [128, D], BF16) for _ in range(NROT)]
        xnfm = [p.sb_in(sc, [128, NSUB * 128], BF16) for _ in range(ND)]
        s_ld = p.sem()
        s_tmm = p.sem()
        s_tp = p.sem()
        s_ln_subs = []

        for sub in range(2):
            i0 = sub * NSUB
            pw = prev_w[0] if sub > 0 else None

            def wait_fn(j, i0=i0, sub=sub):
                i = i0 + j
                w = [(s_ld, 16 * (i + 1))]
                if i >= NROT:
                    w.append((s_tp, ND * (i - NROT + 1)))
                return w

            def after_tile(j, s_fin, fin_v, i0=i0, sub=sub, pw=pw):
                i = i0 + j
                for c in range(ND):
                    kk = i * ND + c

                    @p.on("tensor")
                    def _(t, i=i, c=c, kk=kk, j=j, fin_v=fin_v, pw=pw,
                          s_fin=s_fin):
                        t.wait_ge(s_fin.h, fin_v)
                        t.wait_ge(s_misc.h, 16)
                        if pw is not None and j == 0 and c == 0:
                            t.wait_ge(pw[0].h, pw[1])
                        if kk >= 2:
                            t.wait_ge(s_tp.h, kk - 1)
                        t.transpose(ptp[kk % 2][:],
                                    xnr[i % NROT][:, c * 128:(c + 1) * 128],
                                    ident_bf[:]).then_inc(s_tmm.h, 1)
                    v_t = s_tmm.add()

                    @p.on("vector")
                    def _(v, j=j, c=c, v_t=v_t, kk=kk):
                        v.wait_ge(s_tmm.h, v_t)
                        v.tensor_copy(xnfm[c][:, j * 128:(j + 1) * 128],
                                      ptp[kk % 2][:]).then_inc(s_tp.h, 1)
                    s_tp.add()

            s_ln = emit_ln(p, [xr[(i0 + j) % NROT] for j in range(NSUB)],
                           [xnr[(i0 + j) % NROT] for j in range(NSUB)],
                           wait_fn, sc=sc, n_tiles=NSUB, after_tile=after_tile,
                           nb=1)
            s_ln_subs.append(s_ln)
            tp_base = i0 * ND
            deps_AB.extend(proj_pass(xnfm, w_d["k"], "k",
                                     "maxtrack" if L0 else "normalize", "K",
                                     dst_off, s_tp, i0=i0, ntile=NSUB,
                                     tp_base=tp_base))
            deps_AB.extend(proj_pass(xnfm, w_d["v"], "v", "raw", "V", dst_off,
                                     s_tp, i0=i0, ntile=NSUB, tp_base=tp_base))
            if qpass:
                deps_AB.extend(proj_pass(xnfm, w_d["q"], "q",
                                         "maxtrack" if L0 else "normalize",
                                         "Q", 0, s_tp, i0=i0, ntile=NSUB,
                                         tp_base=tp_base))

        ph = prev_half_ld[0]

        @p.on("sync")
        def _(sy, ph=ph):
            for i in range(NT):
                if i == 0 and ph is not None:
                    for sm, val in ph:
                        sy.wait_ge(sm.h, val)
                if i >= 1:
                    sy.wait_ge(s_ld.h, 16 * i)
                if i >= NROT:
                    k = i - NROT
                    sy.wait_ge(s_ln_subs[k // NSUB].h, k % NSUB + 1)
                sy.dma_start(out=xr[i % NROT][:],
                             in_=xd[i * 128:(i + 1) * 128, :]).then_inc(s_ld.h, 16)
        s_ld.add(16 * NT)
        prev_half_ld[0] = ([(s_ld, 16 * NT), (s_tp, s_tp.v)] +
                           list(prev_fin[0]) + [prev_w[0]])
        sc.close()

    emit_half(xp_in, False, NT)
    emit_half(x_in, True, 0)
    deps_AB.append(c_vone)
    scAB.close()

    if L0:
        nb_scr = io.internal("nb_scr", [1, 16])
        scC = p.scope()
        mrow = p.sb_in(scC, [16, 256], F32)
        mcol = p.sb_in(scC, [16, 2], F32)
        mcsd = p.sb_in(scC, [16, 2], F32)
        nbcol = p.sb_in(scC, [16, 1], F32)
        pC = p.ps_in(scC, [128, 128], F32)
        s_cmm = p.sem()
        cw = [None, None]
        for j in range(2):
            acc = (Q2M, K2M)[j]

            @p.on("tensor")
            def _(t, j=j, acc=acc, cwp=cw[0]):
                if j == 0:
                    for sm, val in deps_AB:
                        t.wait_ge(sm.h, val)
                    t.wait_ge(s_misc.h, 32)
                else:
                    t.wait_ge(p.flu["vector"].h, cwp[1])
                t.transpose(pC[0:16, :], acc[:], ident_f[:]).then_inc(s_cmm.h, 1)
            v_c = s_cmm.add()
            cw[0] = chain(p, "vector", [(s_cmm, v_c)], [
                lambda v, j=j: v.tensor_copy(mrow[:, j * 128:(j + 1) * 128],
                                             pC[0:16, :]),
                lambda v, j=j: v.reduce_max(mcol[:, j:j + 1],
                                            mrow[:, j * 128:(j + 1) * 128],
                                            AX.X),
            ], flush=[True, True])
        c_msd = chain(p, "scalar", [cw[0], (p.s_cpool, 16)], [
            lambda s: s.activation(mcsd[:], mcol[:], AF.Sqrt,
                                   bias=p.constf(0.0)[0:16, :])])
        c_nb = chain(p, "vector", [c_msd], [
            lambda v: v.tensor_mul(nbcol[:], mcsd[:, 0:1], mcsd[:, 1:2]),
            lambda v: v.tensor_scalar(nbcol[:], nbcol[:], -0.125, 0.0,
                                      ALU.mult, ALU.add),
        ], flush=[True, True])

        s_nbd = p.sem()

        @p.on("gpsimd")
        def _(g, c_nb=c_nb):
            g.wait_ge(c_nb[0].h, c_nb[1])
            g.dma_start(out=nb_scr, in_=nbcol[:]).then_inc(s_nbd.h, 16)
            g.wait_ge(s_nbd.h, 16)
            g.dma_start(out=negB[:], in_=nb_scr.broadcast_to((128, 16))
                        ).then_inc(s_nbd.h, 16)
        s_nbd.add(32)
        c_negB = (s_nbd, 32)
        scC.close()
    else:
        c_negB = deps_AB[0]

    QCW = T // NQC
    scD = p.scope()
    Qfm = [p.sb_in(scD, [128, QCW], BF16) for _ in range(ND)]
    qre = [p.sb_in(scD, [128, D], BF16) for _ in range(2)]
    s_qre = p.sem()
    attno = [p.sb_in(scD, [64, QCW], BF16) for _ in range(NH)]
    den64 = [p.sb_in(scD, [65, QCW], F32)]
    den_all = p.sb_in(scD, [16, QCW], F32)
    rden_one = [p.sb_in(scD, [1, QCW], F32)]
    wo_h = [p.sb_in(scD, [64, 512], BF16) for _ in range(NH)]
    bias_o = p.sb_in(scD, [128, D], BF16)
    xre = p.sb_in(scD, [128, D], F32)
    ytp = [p.sb_in(scD, [128, 512], F32) for _ in range(2)]
    phi = [p.sb_in(scD, [128, 2 * QCW], BF16) for _ in range(2)]
    if not L0:
        lnd = p.sb_in(scD, [128, 2 * QCW], F32)
        u2b = [p.sb_in(scD, [128, 2 * QCW], BF16) for _ in range(2)]
    ps_big = [p.ps_in(scD, [128, 2 * QCW], F32) for _ in range(2)]
    ps_o = p.ps_in(scD, [65, QCW], F32)
    ps_y = [p.ps_in(scD, [128, 512], F32) for _ in range(2)]
    ptq = p.ps_in(scD, [128, 128], BF16)

    s_wo = p.sem()

    @p.on("gpsimd")
    def _(g):
        for sm, val in deps_AB:
            g.wait_ge(sm.h, val)
        g.dma_start(out=bias_o[:], in_=b_d["o"].broadcast_to((128, D))
                    ).then_inc(s_wo.h, 16)
    s_wo.add(16)
    s_wol = p.sem()

    s_qt = p.sem()
    s_qev = p.sem()
    s_smm = p.sem()
    s_pv = p.sem()
    s_oev = p.sem()
    s_den = p.sem()
    s_rfl = p.sem()
    s_xre = p.sem()
    s_ymm = p.sem()
    s_yev = p.sem()
    s_yst = p.sem()
    s_nb = p.sem()
    act_done = {}
    phi_done = {}

    NTT = (T // NQC) // 128
    for qc in range(NQC):
        qb = qc * NTT * ND
        for tt in range(NTT):
            ti = qc * 4 + tt

            @p.on("sync")
            def _(sy, ti=ti, tt=tt, qc=qc):
                idx = qc * NTT + tt
                if idx == 0:
                    for sm, val in (prev_half_ld[0] or []):
                        sy.wait_ge(sm.h, val)
                    for sm, val in deps_AB:
                        sy.wait_ge(sm.h, val)
                if idx >= 1:
                    sy.wait_ge(s_qre.h, 16 * idx)
                if idx >= 2:
                    sy.wait_ge(s_qt.h, (idx - 1) * ND)
                sy.dma_start(out=qre[ti % 2][:],
                             in_=q_scr[ti * 128:(ti + 1) * 128, :]
                             ).then_inc(s_qre.h, 16)
            s_qre.add(16)
            v_qre = s_qre.v
            for c in range(ND):
                kq = qb + tt * ND + c

                @p.on("tensor")
                def _(t, tt=tt, c=c, kq=kq, qc=qc, ti=ti, v_qre=v_qre):
                    if kq == 0:
                        for sm, val in deps_AB:
                            t.wait_ge(sm.h, val)
                        t.wait_ge(c_negB[0].h, c_negB[1])
                    t.wait_ge(s_qre.h, v_qre)
                    if kq >= 1:
                        t.wait_ge(s_qev.h, kq)
                    if qc > 0:
                        t.wait_ge(s_smm.h, qc * NH * NT)
                    t.transpose(ptq[:], qre[ti % 2][:, c * 128:(c + 1) * 128],
                                ident_bf[:]).then_inc(s_qt.h, 1)
                v_q = s_qt.add()

                @p.on("vector")
                def _(v, tt=tt, c=c, v_q=v_q):
                    v.wait_ge(s_qt.h, v_q)
                    v.tensor_copy(Qfm[c][:, tt * 128:(tt + 1) * 128], ptq[:]
                                  ).then_inc(s_qev.h, 1)
                s_qev.add()

        for h in range(NH):
            hb = (h % 2) * 64
            ch = h // 2
            NPR = NT
            for pr in range(NPR):
                gidx = ((qc * NH) + h) * NPR + pr

                @p.on("tensor")
                def _(t, h=h, pr=pr, qc=qc, hb=hb, ch=ch, gidx=gidx):
                    t.wait_ge(s_qev.h, (qc + 1) * NTT * ND)
                    if gidx >= 2:
                        w = act_done[gidx - 2]
                        t.wait_ge(w[0].h, w[1])
                    ins = None
                    for sub in range(2):
                        kt = 2 * pr + sub
                        ins = t.matmul(
                            ps_big[pr % 2][:, sub * QCW:(sub + 1) * QCW],
                            K_FM[ch][hb:hb + 64, kt * 128:(kt + 1) * 128],
                            Qfm[ch][hb:hb + 64, :],
                            start=True, stop=True,
                            tile_position=(hb, 0))
                    ins.then_inc(s_smm.h, 1)
                v_s = s_smm.add()

                if L0:
                    c_act = chain(p, "scalar", [(s_smm, v_s), c_negB] +
                                  ([(s_pv, gidx - 1)] if gidx >= 2 else []), [
                        lambda s, h=h, pr=pr: s.activation(
                            phi[pr % 2][:], ps_big[pr % 2][:], AF.Exp,
                            bias=negB[:, h:h + 1], scale=0.125)])
                    c_phi = c_act
                else:
                    c_act = chain(p, "scalar", [(s_smm, v_s), (p.s_cpool, 16)] +
                                  ([phi_done[gidx - 2], (s_pv, gidx - 1)]
                                   if gidx >= 2 else []), [
                        lambda s, pr=pr: s.activation(
                            lnd[:], ps_big[pr % 2][:], AF.Ln,
                            bias=p.constf(1.1), scale=-1.0),
                        lambda s, pr=pr: s.activation(
                            phi[pr % 2][:], lnd[:], AF.Exp,
                            bias=p.constf(0.0), scale=-1.0),
                        lambda s, pr=pr: s.activation(
                            u2b[pr % 2][:], ps_big[pr % 2][:], AF.Square,
                            bias=p.constf(1.0), scale=1.0),
                    ], flush=[True, False, True])
                    c_phi = chain(p, "vector", [c_act] +
                                  ([(s_pv, gidx - 1)] if gidx >= 2 else []), [
                        lambda v, pr=pr: v.tensor_tensor(
                            phi[pr % 2][:], phi[pr % 2][:], u2b[pr % 2][:],
                            ALU.mult)])
                act_done[gidx] = c_act
                phi_done[gidx] = c_phi

                @p.on("tensor")
                def _(t, h=h, pr=pr, c_phi=c_phi, qc=qc):
                    t.wait_ge(c_phi[0].h, c_phi[1])
                    if pr == 0:
                        t.wait_ge(s_oev.h, qc * NH + h)
                    ins = None
                    for sub in range(2):
                        kt = 2 * pr + sub
                        ins = t.matmul(ps_o[:],
                                       V_AUG[kt][:, h * 65:(h + 1) * 65],
                                       phi[pr % 2][:, sub * QCW:(sub + 1) * QCW],
                                       start=(kt == 0), stop=(kt == 2 * NT - 1))
                    ins.then_inc(s_pv.h, 1)
                s_pv.add()

            v_pv = s_pv.v

            @p.on("vector")
            def _(v, h=h, qc=qc, v_pv=v_pv, dguard=qc * NH + h - 1):
                v.wait_ge(s_pv.h, v_pv)
                if qc > 0:
                    v.wait_ge(s_ymm.h, qc * 2 * NTT)
                if qc * NH + h >= 1:
                    v.wait_ge(s_den.h, 16 * (dguard + 1))
                v.tensor_copy(attno[h][:], ps_o[0:64, :])
                v.tensor_copy(den64[0][64:65, :], ps_o[64:65, :]
                              ).then_inc(s_oev.h, 1)
            s_oev.add()
            v_oe = s_oev.v

            dv = s_den.v

            @p.on("gpsimd")
            def _(g, h=h, v_oe=v_oe, qc=qc, dv=dv):
                g.wait_ge(s_oev.h, v_oe)
                if dv:
                    g.wait_ge(s_den.h, dv)
                if h == 0 and qc > 0:
                    g.wait_ge(s_rfl.h, qc * 16)
                g.dma_start(out=den_all[h:h + 1, :], in_=den64[0][64:65, :]
                            ).then_inc(s_den.h, 16)
            s_den.add(16)

        rd_steps = []
        if not L0:
            rd_steps.append(lambda v: v.tensor_scalar(
                den_all[:], den_all[:], 1.0, 1e-6, ALU.mult, ALU.add))
        rd_steps.append(lambda v: v.reciprocal(den_all[:], den_all[:]))
        c_rd = chain(p, "vector", [(s_den, s_den.v)], rd_steps,
                     flush=[True] * len(rd_steps))

        nb_base = s_nb.v
        for h in range(NH):
            rv = s_rfl.v

            @p.on("gpsimd")
            def _(g, h=h, c_rd=c_rd, qc=qc, rv=rv, nb_base=nb_base):
                if h == 0:
                    g.wait_ge(c_rd[0].h, c_rd[1])
                if rv:
                    g.wait_ge(s_rfl.h, rv)
                if h >= 1:
                    g.wait_ge(s_nb.h, nb_base + 2 * h - 1)
                elif nb_base:
                    g.wait_ge(s_nb.h, nb_base - 1)
                g.dma_start(out=rden_one[0][:],
                            in_=den_all[h:h + 1, :]).then_inc(s_rfl.h, 16)
            s_rfl.add(16)
            v_rdfl = s_rfl.v

            @p.on("tensor")
            def _(t, h=h, v_rdfl=v_rdfl, qc=qc, nb_base=nb_base):
                t.wait_ge(s_rfl.h, v_rdfl)
                if h == 0 and qc == 0:
                    t.wait_ge(s_misc.h, 48)
                if h == 0 and qc > 0:
                    t.wait_ge(s_yev.h, qc * 2 * NTT)
                if h > 0:
                    t.wait_ge(s_nb.h, nb_base + 2 * h)
                t.matmul(ps_y[0][0:64, :], ones_f[:, 0:64],
                         rden_one[0][:],
                         start=True, stop=True).then_inc(s_nb.h, 1)
            v_n = s_nb.add()

            @p.on("vector")
            def _(v, h=h, v_n=v_n):
                v.wait_ge(s_nb.h, v_n)
                v.tensor_tensor(attno[h][:], attno[h][:], ps_y[0][0:64, :],
                                ALU.mult).then_inc(s_nb.h, 1)
            s_nb.add()
        v_nrm = s_nb.v

        NOG = 2 * NTT
        for oc in range(2):
            v_prev_mm = s_ymm.v

            v_prev_wol = s_wol.v

            @p.on("gpsimd")
            def _(g, qc=qc, oc=oc, v_prev_mm=v_prev_mm, v_prev_wol=v_prev_wol):
                g.wait_ge(s_wo.h, 16)
                g.wait_ge(s_ymm.h, v_prev_mm)
                for h in range(NH):
                    if v_prev_wol + 16 * h > 0:
                        g.wait_ge(s_wol.h, v_prev_wol + 16 * h)
                    g.dma_start(out=wo_h[h][:],
                                in_=w_d["o"][h * DH:(h + 1) * DH,
                                             oc * 512:(oc + 1) * 512]
                                ).then_inc(s_wol.h, 16)
            s_wol.add(16 * NH)
            v_wol = s_wol.v
            for tt in range(NTT):
                ti = qc * NTT + tt
                g_idx = qc * NOG + oc * NTT + tt

                @p.on("sync")
                def _(sy, ti=ti, g_idx=g_idx):
                    if g_idx == 0:
                        for sm, val in deps_AB:
                            sy.wait_ge(sm.h, val)
                    if g_idx >= 1:
                        sy.wait_ge(s_xre.h, 16 * g_idx)
                        sy.wait_ge(s_yst.h, 16 * g_idx)
                    sy.dma_start(out=xre[:],
                                 in_=x_in[ti * 128:(ti + 1) * 128, :]
                                 ).then_inc(s_xre.h, 16)
                s_xre.add(16)
                v_xre = s_xre.v

                @p.on("tensor")
                def _(t, tt=tt, oc=oc, v_nrm=v_nrm, g_idx=g_idx, v_wol=v_wol):
                    t.wait_ge(s_nb.h, v_nrm)
                    t.wait_ge(s_wol.h, v_wol)
                    if g_idx >= 2:
                        t.wait_ge(s_yev.h, g_idx - 1)
                    ins = None
                    for h in range(NH):
                        ins = t.matmul(ps_y[g_idx % 2][:],
                                       attno[h][:, tt * 128:(tt + 1) * 128],
                                       wo_h[h][:],
                                       start=(h == 0), stop=(h == NH - 1))
                    ins.then_inc(s_ymm.h, 1)
                v_y = s_ymm.add()

                @p.on("vector")
                def _(v, oc=oc, v_y=v_y, g_idx=g_idx):
                    v.wait_ge(s_ymm.h, v_y)
                    v.wait_ge(s_wo.h, 16)
                    if g_idx >= 2:
                        v.wait_ge(s_yst.h, 16 * (g_idx - 1))
                    v.tensor_tensor(ytp[g_idx % 2][:], ps_y[g_idx % 2][:],
                                    bias_o[:, oc * 512:(oc + 1) * 512],
                                    ALU.add).then_inc(s_yev.h, 1)
                s_yev.add()
                v_ev2 = s_yev.v

                @p.on("gpsimd")
                def _(g, ti=ti, oc=oc, v_ev2=v_ev2, v_xre=v_xre, g_idx=g_idx):
                    g.wait_ge(s_yev.h, v_ev2)
                    g.wait_ge(s_xre.h, v_xre)
                    if g_idx >= 1:
                        g.wait_ge(s_yst.h, 16 * g_idx)
                    g.tensor_tensor(ytp[g_idx % 2][:], ytp[g_idx % 2][:],
                                    xre[:, oc * 512:(oc + 1) * 512],
                                    ALU.add).then_inc(p.flu["gpsimd"].h, 1)
                vf = p.flu["gpsimd"].add()

                @p.on("gpsimd")
                def _(g, ti=ti, oc=oc, g_idx=g_idx, vf=vf):
                    g.wait_ge(p.flu["gpsimd"].h, vf)
                    g.dma_start(out=y_out[ti * 128:(ti + 1) * 128,
                                          oc * 512:(oc + 1) * 512],
                                in_=ytp[g_idx % 2][:]).then_inc(s_yst.h, 16)
                s_yst.add(16)
    scD.close()

    @p.on("gpsimd")
    def _(g):
        g.wait_ge(s_yst.h, 16 * 2 * NT)



F6 = 3072
QMAX6 = 32767.0
DEBUG_F1 = False


def emit_wdq(nc, p, io, w_dram, wT_dram, O, I, ident_sb, s_misc_v, s_misc,
             inv7_pp, c_inv, prev=None):
    nb = I // 32
    sc = p.scope()
    wt_ = [p.sb_in(sc, [128, I], F32) for _ in range(2)]
    wab = p.sb_in(sc, [128, I], F32)
    qi = p.sb_in(sc, [128, I], I32)
    qf = p.sb_in(sc, [128, I], F32)
    wdq = p.sb_in(sc, [128, I], BF16)
    bm = p.sb_in(sc, [128, nb], F32)
    binv = p.sb_in(sc, [128, nb], F32)
    ptp = [p.ps_in(sc, [128, 128], BF16) for _ in range(2)]
    etp = [p.sb_in(sc, [128, 128], BF16) for _ in range(2)]
    s_ld = p.sem()
    s_ab = p.sem()
    s_dq = p.sem()
    s_tm = p.sem()
    s_te = p.sem()
    s_st = p.sem()
    not_ = O // 128
    nc_ = I // 128
    for ot in range(not_):
        @p.on("sync")
        def _(sy, ot=ot):
            if ot == 0 and prev is not None:
                sy.wait_ge(prev[0].h, prev[1])
            if ot >= 1:
                sy.wait_ge(s_ld.h, 16 * ot)
            if ot >= 2:
                sy.wait_ge(s_dq.h, ot - 1)
            sy.dma_start(out=wt_[ot % 2][:], in_=w_dram[ot * 128:(ot + 1) * 128, :]
                         ).then_inc(s_ld.h, 16)
        s_ld.add(16)
        v_ld = s_ld.v

        @p.on("scalar")
        def _(s, ot=ot, v_ld=v_ld):
            if ot == 0 and prev is not None:
                s.wait_ge(prev[0].h, prev[1])
            s.wait_ge(s_ld.h, v_ld)
            if ot >= 1:
                s.wait_ge(s_dq.h, ot)
            s.activation(wab[:], wt_[ot % 2][:], AF.Abs).then_inc(s_ab.h, 1)
        v_ab = s_ab.add()

        fl = p.flu["vector"]
        b0 = fl.v

        @p.on("vector")
        def _(v, ot=ot, v_ld=v_ld, b0=b0, v_ab=v_ab):
            v.wait_ge(s_ld.h, v_ld)
            v.wait_ge(s_ab.h, v_ab)
            v.wait_ge(c_inv[0].h, c_inv[1])
            if ot >= 1:
                v.wait_ge(s_te.h, ot * nc_)
            w = wt_[ot % 2]
            v.tensor_reduce(bm[:].rearrange("p (g o) -> p g o", o=1),
                            wab[:].rearrange("p (g k) -> p g k", g=nb),
                            AX.X, ALU.max).then_inc(fl.h, 1)
            v.wait_ge(fl.h, b0 + 1)
            v.tensor_tensor(bm[:], bm[:], inv7_pp[:].broadcast_to((128, nb)),
                            ALU.mult).then_inc(fl.h, 1)
            v.wait_ge(fl.h, b0 + 2)
            v.tensor_scalar(bm[:], bm[:], 1e-12, 0.0, ALU.max, ALU.add
                            ).then_inc(fl.h, 1)
            v.wait_ge(fl.h, b0 + 3)
            v.reciprocal(binv[:], bm[:]).then_inc(fl.h, 1)
            v.wait_ge(fl.h, b0 + 4)
            for b in range(nb):
                ins = v.tensor_tensor(qi[:, b * 32:(b + 1) * 32],
                                      w[:, b * 32:(b + 1) * 32],
                                      binv[:, b:b + 1].broadcast_to((128, 32)),
                                      ALU.mult)
            ins.then_inc(fl.h, 1)
            v.wait_ge(fl.h, b0 + 5)
            v.tensor_copy(qf[:], qi[:]).then_inc(fl.h, 1)
            v.wait_ge(fl.h, b0 + 6)
            for b in range(nb):
                ins = v.tensor_tensor(wdq[:, b * 32:(b + 1) * 32],
                                      qf[:, b * 32:(b + 1) * 32],
                                      bm[:, b:b + 1].broadcast_to((128, 32)),
                                      ALU.mult)
            ins.then_inc(s_dq.h, 1)
        fl.add(6)
        v_dq = s_dq.add()

        for c in range(nc_):
            k = ot * nc_ + c

            @p.on("tensor")
            def _(t, ot=ot, c=c, k=k, v_dq=v_dq):
                if k == 0 and prev is not None:
                    t.wait_ge(prev[0].h, prev[1])
                t.wait_ge(s_dq.h, v_dq)
                t.wait_ge(s_misc.h, s_misc_v)
                if k >= 2:
                    t.wait_ge(s_te.h, k - 1)
                t.transpose(ptp[k % 2][:], wdq[:, c * 128:(c + 1) * 128],
                            ident_sb[:]).then_inc(s_tm.h, 1)
            v_tm = s_tm.add()

            @p.on("vector")
            def _(v, k=k, v_tm=v_tm):
                v.wait_ge(s_tm.h, v_tm)
                if k >= 2:
                    v.wait_ge(s_st.h, 16 * (k - 1))
                v.tensor_copy(etp[k % 2][:], ptp[k % 2][:]).then_inc(s_te.h, 1)
            v_te = s_te.add()

            @p.on("gpsimd")
            def _(g, ot=ot, c=c, k=k, v_te=v_te):
                g.wait_ge(s_te.h, v_te)
                if k >= 1:
                    g.wait_ge(s_st.h, 16 * k)
                g.dma_start(out=wT_dram[c * 128:(c + 1) * 128,
                                        ot * 128:(ot + 1) * 128],
                            in_=etp[k % 2][:]).then_inc(s_st.h, 16)
            s_st.add(16)
    sc.close()
    return (s_st, s_st.v)


def emit_armax(nc, p, io, name, groups, acc_list, sq_pp_list, inv_pp_list,
               waits, invqm_pp, c_inv, arb):
    nr = len(acc_list)
    ar_in = io.internal(name + "_in", [nr, 128])
    ar_out = io.internal(name + "_out", [nr, 128])
    s_d = p.sem()
    s_c = p.sem()

    @p.on("gpsimd")
    def _(g):
        for sm, val in waits:
            g.wait_ge(sm.h, val)
        for r in range(nr):
            if r >= 1:
                g.wait_ge(s_d.h, 16 * r)
            g.dma_start(out=ar_in[r:r + 1, :], in_=acc_list[r][:, 0:1]
                        ).then_inc(s_d.h, 16)
        g.wait_ge(s_d.h, 16 * nr)
        g.collective_compute(
            "AllReduce", ALU.max, replica_groups=groups,
            ins=[ar_in.opt()], outs=[ar_out.opt()],
        ).then_inc(s_c.h, 1)
    s_d.add(16 * nr)
    s_c.add(1)
    last = None
    for r in range(nr):
        dv = s_d.v
        cv = s_c.v

        @p.on("gpsimd")
        def _(g, r=r, dv=dv, cv=cv, last=last):
            g.wait_ge(s_c.h, cv)
            if dv:
                g.wait_ge(s_d.h, dv)
            if last is not None:
                g.wait_ge(last[0].h, last[1])
            g.dma_start(out=arb[:], in_=ar_out[r:r + 1, :].broadcast_to((128, 128))
                        ).then_inc(s_d.h, 16)
        v_b = s_d.add(16)
        sq = sq_pp_list[r]
        iq = inv_pp_list[r]
        last = chain(p, "vector", [(s_d, v_b), c_inv], [
            lambda v, sq=sq: v.tensor_reduce(sq[:], arb[:], AX.X, ALU.max),
            lambda v, sq=sq: v.tensor_tensor(sq[:], sq[:], invqm_pp[:],
                                             ALU.mult),
            lambda v, sq=sq: v.tensor_scalar(sq[:], sq[:], 1e-12, 0.0,
                                             ALU.max, ALU.add),
            lambda v, sq=sq, iq=iq: v.reciprocal(iq[:], sq[:]),
        ], flush=[True, True, True, True])
    return last


def emit_f1(nc, p, io, ncores=None):
    if ncores is None:
        ncores = NCORE
    groups = [list(range(ncores))]
    x_in = io.inp("x", [T, D])
    wg_d = io.inp("w_gate", [F6, D])
    wu_d = io.inp("w_up", [F6, D])
    wd_d = io.inp("w_down", [D, F6])
    ident_bf_d = io.inp("ident_bf", [128, 128], BF16)
    y_out = io.out("y", [T, D])

    if DEBUG_F1:
        dbg_sc = io.out("dbg_sc", [128, 16])
        dbg_xq = io.out("dbg_xq", [128, 128], BF16)
        dbg_up0 = io.out("dbg_up0", [128, 512])
        dbg_upq0 = io.out("dbg_upq0", [128, 512])
        dbg_g20 = io.out("dbg_g20", [128, 512])
        dbg_o0 = io.out("dbg_o0", [128, 512])
        dbg_wuT = io.out("dbg_wuT", [128, 512], BF16)
    wgT = io.internal("wgT", [D, F6], BF16)
    wuT = io.internal("wuT", [D, F6], BF16)
    wdT = io.internal("wdT", [F6, D], BF16)
    m_up = io.internal("m_up", [T, F6])
    m_gate = io.internal("m_gate", [T, F6])

    ident_sb = p.sb([128, 128], BF16)
    s_misc = p.sem()

    @p.on("sync")
    def _(sy):
        sy.dma_start(out=ident_sb[:], in_=ident_bf_d).then_inc(s_misc.h, 16)
    s_misc.add(16)

    invqm_pp = p.sb([128, 1], F32)
    inv7_pp = p.sb([128, 1], F32)
    c_inv = chain(p, "vector", [(p.s_cpool, 16)], [
        lambda v: v.reciprocal(invqm_pp[:], p.constf(QMAX6)),
        lambda v: v.reciprocal(inv7_pp[:], p.constf(7.0)),
    ], flush=[True, True])

    sq_x, inv_x = p.sb([128, 1], F32), p.sb([128, 1], F32)
    sq_u, inv_u = p.sb([128, 1], F32), p.sb([128, 1], F32)
    sq_g, inv_g = p.sb([128, 1], F32), p.sb([128, 1], F32)
    sq_g2, inv_g2 = p.sb([128, 1], F32), p.sb([128, 1], F32)
    sq_o, inv_o = p.sb([128, 1], F32), p.sb([128, 1], F32)
    acc_x = p.sb([128, 1], F32)
    rx = p.sb([128, 1], F32)
    acc_u = p.sb([128, 1], F32)
    acc_g = p.sb([128, 1], F32)
    acc_g2 = p.sb([128, 1], F32)
    acc_o = p.sb([128, 1], F32)
    arb6 = p.sb([128, 128], F32)
    XQ = [p.sb([128, T], BF16) for _ in range(ND)]
    dbsc = p.sb([128, 16], F32) if DEBUG_F1 else None

    c_wu = emit_wdq(nc, p, io, wu_d, wuT, F6, D, ident_sb, 16, s_misc,
                    inv7_pp, c_inv)
    c_wg = emit_wdq(nc, p, io, wg_d, wgT, F6, D, ident_sb, 16, s_misc,
                    inv7_pp, c_inv, prev=c_wu)
    c_wd = emit_wdq(nc, p, io, wd_d, wdT, D, F6, ident_sb, 16, s_misc,
                    inv7_pp, c_inv, prev=c_wg)

    scX = p.scope()
    x_t = [p.sb_in(scX, [128, D], F32) for _ in range(NT)]
    xab = p.sb_in(scX, [128, D], F32)
    s_xab = p.sem()
    s_ld = emit_load_tiles(p, x_in, x_t, waits=[c_wd])
    s_ln = emit_ln(p, x_t, x_t, lambda i: [(s_ld, 16 * (i + 1))], nb=2,
                   sc=scX)
    fl = p.flu["vector"]

    @p.on("vector")
    def _(v):
        v.memset(acc_x[:], 0.0)
    mx_last = None
    for i in range(NT):
        ml = mx_last

        @p.on("scalar")
        def _(s, i=i, ml=ml):
            s.wait_ge(s_ln.h, i + 1)
            if ml is not None:
                s.wait_ge(ml[0].h, ml[1])
            s.activation(xab[:], x_t[i][:], AF.Abs).then_inc(s_xab.h, 1)
        v_xab = s_xab.add()
        mx_last = chain(p, "vector",
                        [(s_xab, v_xab)] + ([mx_last] if mx_last else []), [
            lambda v, i=i: v.tensor_reduce(rx[:], xab[:], AX.X, ALU.max),
            lambda v: v.tensor_tensor(acc_x[:], acc_x[:], rx[:], ALU.max),
        ], flush=[True, True])
    c_sx = emit_armax(nc, p, io, "ar_x", groups, [acc_x], [sq_x], [inv_x],
                      [mx_last], invqm_pp, c_inv, arb6)

    qi_x = p.sb_in(scX, [128, D], I32)
    qf_x = p.sb_in(scX, [128, D], F32)
    xqb = p.sb_in(scX, [128, D], BF16)
    ptq = [p.ps_in(scX, [128, 128], BF16) for _ in range(2)]
    s_xq = p.sem()
    s_xtm = p.sem()
    s_xte = p.sem()
    for i in range(NT):
        fl0 = fl.v

        @p.on("vector")
        def _(v, i=i, c_sx=c_sx, fl0=fl0):
            v.wait_ge(c_sx[0].h, c_sx[1])
            if i >= 1:
                v.wait_ge(s_xte.h, i * ND)
            v.scalar_tensor_tensor(qi_x[:], x_t[i][:], inv_x[:], x_t[i][:],
                                   ALU.mult, ALU.bypass).then_inc(fl.h, 1)
            v.wait_ge(fl.h, fl0 + 1)
            v.tensor_copy(qf_x[:], qi_x[:]).then_inc(fl.h, 1)
            v.wait_ge(fl.h, fl0 + 2)
            v.scalar_tensor_tensor(xqb[:], qf_x[:], sq_x[:], qf_x[:],
                                   ALU.mult, ALU.bypass).then_inc(s_xq.h, 1)
        fl.add(2)
        v_xq = s_xq.add()
        for c in range(ND):
            k = i * ND + c

            @p.on("tensor")
            def _(t, i=i, c=c, k=k, v_xq=v_xq):
                t.wait_ge(s_xq.h, v_xq)
                t.wait_ge(s_misc.h, 16)
                if k >= 2:
                    t.wait_ge(s_xte.h, k - 1)
                t.transpose(ptq[k % 2][:], xqb[:, c * 128:(c + 1) * 128],
                            ident_sb[:]).then_inc(s_xtm.h, 1)
            v_tm = s_xtm.add()

            @p.on("vector")
            def _(v, i=i, c=c, k=k, v_tm=v_tm):
                v.wait_ge(s_xtm.h, v_tm)
                v.tensor_copy(XQ[c][:, i * 128:(i + 1) * 128],
                              ptq[k % 2][:]).then_inc(s_xte.h, 1)
            s_xte.add()
    v_xdone = s_xte.v
    scX.close()

    scM = p.scope()
    wus = [p.sb_in(scM, [128, 512], BF16) for _ in range(ND)]
    wgs = [p.sb_in(scM, [128, 512], BF16) for _ in range(ND)]
    ut = [p.sb_in(scM, [128, 512], F32) for _ in range(2)]
    gt = [p.sb_in(scM, [128, 512], F32) for _ in range(2)]
    pu = [p.ps_in(scM, [128, 512], F32) for _ in range(2)]
    pg = [p.ps_in(scM, [128, 512], F32) for _ in range(2)]
    r1 = p.sb_in(scM, [128, 1], F32)
    uab = p.sb_in(scM, [128, 512], F32)
    gab = p.sb_in(scM, [128, 512], F32)
    s_wl = p.sem()
    s_mm = p.sem()
    s_mab = p.sem()
    s_ev = p.sem()
    s_st = p.sem()

    @p.on("vector")
    def _(v):
        v.memset(acc_u[:], 0.0)
        v.memset(acc_g[:], 0.0)

    NOC = F6 // 512
    for oc in range(NOC):
        @p.on("sync")
        def _(sy, oc=oc, c_wu=c_wu, c_wg=c_wg):
            if oc == 0:
                sy.wait_ge(c_wu[0].h, c_wu[1])
                sy.wait_ge(c_wg[0].h, c_wg[1])
                sy.wait_ge(s_xte.h, v_xdone)
            else:
                sy.wait_ge(s_wl.h, 16 * 2 * ND * oc)
                sy.wait_ge(s_mm.h, 2 * NT * oc)
            for c in range(ND):
                sy.dma_start(out=wus[c][:],
                             in_=wuT[c * 128:(c + 1) * 128,
                                     oc * 512:(oc + 1) * 512]
                             ).then_inc(s_wl.h, 16)
            for c in range(ND):
                sy.dma_start(out=wgs[c][:],
                             in_=wgT[c * 128:(c + 1) * 128,
                                     oc * 512:(oc + 1) * 512]
                             ).then_inc(s_wl.h, 16)
        s_wl.add(16 * 2 * ND)
        v_wl = s_wl.v

        for tt in range(NT):
            g_idx = oc * NT + tt

            @p.on("tensor")
            def _(t, oc=oc, tt=tt, g_idx=g_idx, v_wl=v_wl):
                if g_idx == 0:
                    t.wait_ge(s_xte.h, v_xdone)
                t.wait_ge(s_wl.h, v_wl)
                if g_idx >= 2:
                    t.wait_ge(s_ev.h, g_idx - 1)
                ins = None
                for c in range(ND):
                    ins = t.matmul(pu[tt % 2][:],
                                   XQ[c][:, tt * 128:(tt + 1) * 128],
                                   wus[c][:], start=(c == 0), stop=(c == ND - 1))
                ins.then_inc(s_mm.h, 1)
                ins = None
                for c in range(ND):
                    ins = t.matmul(pg[tt % 2][:],
                                   XQ[c][:, tt * 128:(tt + 1) * 128],
                                   wgs[c][:], start=(c == 0), stop=(c == ND - 1))
                ins.then_inc(s_mm.h, 1)
            v_mm = s_mm.add(2)

            @p.on("scalar")
            def _(s, tt=tt, g_idx=g_idx, v_mm=v_mm):
                s.wait_ge(s_mm.h, v_mm)
                if g_idx >= 1:
                    s.wait_ge(s_ev.h, g_idx)
                s.activation(uab[:], pu[tt % 2][:], AF.Abs)
                s.activation(gab[:], pg[tt % 2][:], AF.Abs
                             ).then_inc(s_mab.h, 1)
            v_mab = s_mab.add()

            fl0 = fl.v

            @p.on("vector")
            def _(v, oc=oc, tt=tt, g_idx=g_idx, v_mm=v_mm, fl0=fl0,
                  v_mab=v_mab):
                v.wait_ge(s_mm.h, v_mm)
                v.wait_ge(s_mab.h, v_mab)
                if g_idx >= 2:
                    v.wait_ge(s_st.h, 32 * (g_idx - 1))
                v.tensor_reduce(r1[:], uab[:], AX.X, ALU.max
                                ).then_inc(fl.h, 1)
                v.wait_ge(fl.h, fl0 + 1)
                v.tensor_tensor(acc_u[:], acc_u[:], r1[:], ALU.max
                                ).then_inc(fl.h, 1)
                v.wait_ge(fl.h, fl0 + 2)
                v.tensor_reduce(r1[:], gab[:], AX.X, ALU.max
                                ).then_inc(fl.h, 1)
                v.wait_ge(fl.h, fl0 + 3)
                v.tensor_tensor(acc_g[:], acc_g[:], r1[:], ALU.max
                                ).then_inc(fl.h, 1)
                v.wait_ge(fl.h, fl0 + 4)
                v.tensor_copy(ut[tt % 2][:], pu[tt % 2][:])
                v.tensor_copy(gt[tt % 2][:], pg[tt % 2][:]
                              ).then_inc(s_ev.h, 1)
            fl.add(4)
            v_ev = s_ev.add(1)

            @p.on("gpsimd")
            def _(g, oc=oc, tt=tt, g_idx=g_idx, v_ev=v_ev):
                g.wait_ge(s_ev.h, v_ev)
                if g_idx >= 1:
                    g.wait_ge(s_st.h, 32 * g_idx)
                g.dma_start(out=m_up[tt * 128:(tt + 1) * 128,
                                     oc * 512:(oc + 1) * 512],
                            in_=ut[tt % 2][:]).then_inc(s_st.h, 16)
                g.dma_start(out=m_gate[tt * 128:(tt + 1) * 128,
                                       oc * 512:(oc + 1) * 512],
                            in_=gt[tt % 2][:]).then_inc(s_st.h, 16)
            s_st.add(32)
    v_m1st = s_st.v
    if DEBUG_F1:
        @p.on("gpsimd")
        def _(g, vm=v_m1st):
            g.wait_ge(s_st.h, vm)
            g.dma_start(out=dbg_up0, in_=m_up[0:128, 0:512]).then_inc(s_st.h, 16)
        s_st.add(16)
        v_m1st = s_st.v
    c_sug = emit_armax(nc, p, io, "ar_ug", groups, [acc_u, acc_g],
                       [sq_u, sq_g], [inv_u, inv_g],
                       [(s_ev, s_ev.v), (s_st, v_m1st)], invqm_pp, c_inv, arb6)
    scM.close()

    ECH = 1024
    NEC = F6 // ECH
    scE = p.scope()
    eu = [p.sb_in(scE, [128, ECH], F32) for _ in range(2)]
    eg = [p.sb_in(scE, [128, ECH], F32) for _ in range(2)]
    eqi = p.sb_in(scE, [128, ECH], I32)
    eqf = p.sb_in(scE, [128, ECH], F32)
    esig = p.sb_in(scE, [128, ECH], F32)
    eab = p.sb_in(scE, [128, ECH], F32)
    r2 = p.sb_in(scE, [128, 1], F32)
    s_el = p.sem()
    s_ec = p.sem()
    s_es = p.sem()
    s_esg = p.sem()
    s_eab = p.sem()

    @p.on("vector")
    def _(v):
        v.memset(acc_g2[:], 0.0)

    NCHU = NT * NEC
    for i in range(NCHU):
        tt, ec = divmod(i, NEC)
        ro, co = tt * 128, ec * ECH

        @p.on("sync")
        def _(sy, i=i, ro=ro, co=co, c_sug=c_sug):
            if i == 0:
                sy.wait_ge(c_sug[0].h, c_sug[1])
            else:
                sy.wait_ge(s_el.h, 32 * i)
            if i >= 2:
                sy.wait_ge(s_es.h, 32 * (i - 1))
            sy.dma_start(out=eu[i % 2][:], in_=m_up[ro:ro + 128, co:co + ECH]
                         ).then_inc(s_el.h, 16)
            sy.dma_start(out=eg[i % 2][:], in_=m_gate[ro:ro + 128, co:co + ECH]
                         ).then_inc(s_el.h, 16)
        s_el.add(32)
        v_el = s_el.v

        fl0 = fl.v

        @p.on("vector")
        def _(v, i=i, v_el=v_el, fl0=fl0):
            v.wait_ge(s_el.h, v_el)
            u, g = eu[i % 2], eg[i % 2]
            v.scalar_tensor_tensor(eqi[:], u[:], inv_u[:], u[:],
                                   ALU.mult, ALU.bypass).then_inc(fl.h, 1)
            v.wait_ge(fl.h, fl0 + 1)
            v.tensor_copy(eqf[:], eqi[:]).then_inc(fl.h, 1)
            v.wait_ge(fl.h, fl0 + 2)
            v.scalar_tensor_tensor(u[:], eqf[:], sq_u[:], eqf[:],
                                   ALU.mult, ALU.bypass).then_inc(fl.h, 1)
            v.wait_ge(fl.h, fl0 + 3)
            v.scalar_tensor_tensor(eqi[:], g[:], inv_g[:], g[:],
                                   ALU.mult, ALU.bypass).then_inc(fl.h, 1)
            v.wait_ge(fl.h, fl0 + 4)
            v.tensor_copy(eqf[:], eqi[:]).then_inc(fl.h, 1)
            v.wait_ge(fl.h, fl0 + 5)
            v.scalar_tensor_tensor(g[:], eqf[:], sq_g[:], eqf[:],
                                   ALU.mult, ALU.bypass).then_inc(s_ec.h, 1)
        fl.add(5)
        v_ec = s_ec.add()

        @p.on("scalar")
        def _(s, i=i, v_ec=v_ec):
            s.wait_ge(s_ec.h, v_ec)
            s.activation(esig[:], eg[i % 2][:], AF.Sigmoid).then_inc(s_esg.h, 1)
        v_sg = s_esg.add()

        fl0 = fl.v

        @p.on("vector")
        def _(v, i=i, v_sg=v_sg, fl0=fl0):
            v.wait_ge(s_esg.h, v_sg)
            g = eg[i % 2]
            v.tensor_scalar(eqi[:], esig[:], 65536.0, 0.0, ALU.mult, ALU.add
                            ).then_inc(fl.h, 1)
            v.wait_ge(fl.h, fl0 + 1)
            v.tensor_copy(eqf[:], eqi[:]).then_inc(fl.h, 1)
            v.wait_ge(fl.h, fl0 + 2)
            v.tensor_scalar(esig[:], eqf[:], 65535.0, 1.0 / 65536.0,
                            ALU.min, ALU.mult).then_inc(fl.h, 1)
            v.wait_ge(fl.h, fl0 + 3)
            v.tensor_tensor(g[:], g[:], esig[:], ALU.mult).then_inc(fl.h, 1)
        fl.add(4)
        v_fl4 = fl.v

        @p.on("scalar")
        def _(s, i=i, v_fl4=v_fl4):
            s.wait_ge(fl.h, v_fl4)
            s.activation(eab[:], eg[i % 2][:], AF.Abs).then_inc(s_eab.h, 1)
        v_eab = s_eab.add()

        fl0b = fl.v

        @p.on("vector")
        def _(v, i=i, v_eab=v_eab, fl0b=fl0b):
            v.wait_ge(s_eab.h, v_eab)
            v.tensor_reduce(r2[:], eab[:], AX.X, ALU.max).then_inc(fl.h, 1)
            v.wait_ge(fl.h, fl0b + 1)
            v.tensor_tensor(acc_g2[:], acc_g2[:], r2[:], ALU.max
                            ).then_inc(s_ec.h, 1)
        fl.add(1)
        v_ec2 = s_ec.add()

        @p.on("gpsimd")
        def _(g_, i=i, ro=ro, co=co, v_ec2=v_ec2):
            g_.wait_ge(s_ec.h, v_ec2)
            if i >= 1:
                g_.wait_ge(s_es.h, 32 * i)
            g_.dma_start(out=m_up[ro:ro + 128, co:co + ECH], in_=eu[i % 2][:]
                         ).then_inc(s_es.h, 16)
            g_.dma_start(out=m_gate[ro:ro + 128, co:co + ECH], in_=eg[i % 2][:]
                         ).then_inc(s_es.h, 16)
        s_es.add(32)
    if DEBUG_F1:
        @p.on("gpsimd")
        def _(g, vv=s_es.v):
            g.wait_ge(s_es.h, vv)
            g.dma_start(out=dbg_g20, in_=m_gate[0:128, 0:512]).then_inc(s_es.h, 16)
            g.wait_ge(s_es.h, vv + 16)
            g.dma_start(out=dbg_upq0, in_=m_up[0:128, 0:512]).then_inc(s_es.h, 16)
        s_es.add(32)
    c_sg2 = emit_armax(nc, p, io, "ar_g2", groups, [acc_g2], [sq_g2], [inv_g2],
                       [(s_ec, s_ec.v), (s_es, s_es.v)], invqm_pp, c_inv, arb6)


    @p.on("vector")
    def _(v):
        v.memset(acc_o[:], 0.0)
    s_el2 = p.sem()
    s_ec2 = p.sem()
    s_es2 = p.sem()
    for i in range(NCHU):
        tt, ec = divmod(i, NEC)
        ro, co = tt * 128, ec * ECH

        @p.on("sync")
        def _(sy, i=i, ro=ro, co=co, c_sg2=c_sg2):
            if i == 0:
                sy.wait_ge(c_sg2[0].h, c_sg2[1])
            else:
                sy.wait_ge(s_el2.h, 32 * i)
            if i >= 2:
                sy.wait_ge(s_es2.h, 16 * (i - 1))
                sy.wait_ge(s_ec2.h, i - 1)
            sy.dma_start(out=eu[i % 2][:], in_=m_up[ro:ro + 128, co:co + ECH]
                         ).then_inc(s_el2.h, 16)
            sy.dma_start(out=eg[i % 2][:], in_=m_gate[ro:ro + 128, co:co + ECH]
                         ).then_inc(s_el2.h, 16)
        s_el2.add(32)
        v_el = s_el2.v

        fl0 = fl.v

        @p.on("vector")
        def _(v, i=i, v_el=v_el, fl0=fl0):
            v.wait_ge(s_el2.h, v_el)
            u, g = eu[i % 2], eg[i % 2]
            v.scalar_tensor_tensor(eqi[:], g[:], inv_g2[:], g[:],
                                   ALU.mult, ALU.bypass).then_inc(fl.h, 1)
            v.wait_ge(fl.h, fl0 + 1)
            v.tensor_copy(eqf[:], eqi[:]).then_inc(fl.h, 1)
            v.wait_ge(fl.h, fl0 + 2)
            v.scalar_tensor_tensor(g[:], eqf[:], sq_g2[:], eqf[:],
                                   ALU.mult, ALU.bypass).then_inc(fl.h, 1)
            v.wait_ge(fl.h, fl0 + 3)
            v.tensor_tensor(g[:], g[:], u[:], ALU.mult).then_inc(fl.h, 1)
        fl.add(4)
        v_fl4 = fl.v

        @p.on("scalar")
        def _(s, i=i, v_fl4=v_fl4):
            s.wait_ge(fl.h, v_fl4)
            s.activation(eab[:], eg[i % 2][:], AF.Abs).then_inc(s_eab.h, 1)
        v_eab = s_eab.add()

        fl0b = fl.v

        @p.on("vector")
        def _(v, i=i, v_eab=v_eab, fl0b=fl0b):
            v.wait_ge(s_eab.h, v_eab)
            v.tensor_reduce(r2[:], eab[:], AX.X, ALU.max).then_inc(fl.h, 1)
            v.wait_ge(fl.h, fl0b + 1)
            v.tensor_tensor(acc_o[:], acc_o[:], r2[:], ALU.max
                            ).then_inc(s_ec2.h, 1)
        fl.add(1)
        v_ec = s_ec2.add()

        @p.on("gpsimd")
        def _(g_, i=i, ro=ro, co=co, v_ec=v_ec):
            g_.wait_ge(s_ec2.h, v_ec)
            if i >= 1:
                g_.wait_ge(s_es2.h, 16 * i)
            g_.dma_start(out=m_gate[ro:ro + 128, co:co + ECH], in_=eg[i % 2][:]
                         ).then_inc(s_es2.h, 16)
        s_es2.add(16)
    if DEBUG_F1:
        @p.on("gpsimd")
        def _(g, vv=s_es2.v):
            g.wait_ge(s_es2.h, vv)
            g.dma_start(out=dbg_o0, in_=m_gate[0:128, 0:512]).then_inc(s_es2.h, 16)
        s_es2.add(16)
    c_so = emit_armax(nc, p, io, "ar_o", groups, [acc_o], [sq_o], [inv_o],
                      [(s_ec2, s_ec2.v), (s_es2, s_es2.v)], invqm_pp, c_inv,
                      arb6)
    scE.close()

    scD = p.scope()
    wds = [p.sb_in(scD, [128, D], BF16) for _ in range(F6 // 128)]
    ot_ = [p.sb_in(scD, [128, F6], F32) for _ in range(2)]
    oqi = p.sb_in(scD, [128, F6], I32)
    oqf = p.sb_in(scD, [128, F6], F32)
    oqb = p.sb_in(scD, [128, F6], BF16)
    OQF = [p.sb_in(scD, [128, 128], BF16) for _ in range(F6 // 128)]
    xr = [p.sb_in(scD, [128, D], F32) for _ in range(2)]
    yt = [p.sb_in(scD, [128, 512], F32) for _ in range(2)]
    pd = [p.ps_in(scD, [128, 512], F32) for _ in range(2)]
    ptd = [p.ps_in(scD, [128, 128], BF16) for _ in range(2)]
    s_wdl = p.sem()
    s_ol = p.sem()
    s_oq = p.sem()
    s_otm = p.sem()
    s_ote = p.sem()
    s_dmm = p.sem()
    s_dev = p.sem()
    s_xr = p.sem()
    s_yst = p.sem()
    NF = F6 // 128

    @p.on("sync")
    def _(sy, c_wd=c_wd, c_so=c_so):
        sy.wait_ge(c_wd[0].h, c_wd[1])
        sy.wait_ge(c_so[0].h, c_so[1])
        for c in range(NF):
            if c >= 1:
                sy.wait_ge(s_wdl.h, 16 * c)
            sy.dma_start(out=wds[c][:], in_=wdT[c * 128:(c + 1) * 128, :]
                         ).then_inc(s_wdl.h, 16)
    s_wdl.add(16 * NF)

    for tt in range(NT):
        @p.on("sync")
        def _(sy, tt=tt, c_so=c_so):
            if tt == 0:
                sy.wait_ge(c_so[0].h, c_so[1])
            else:
                sy.wait_ge(s_ol.h, 16 * tt)
            if tt >= 2:
                sy.wait_ge(s_oq.h, tt - 1)
            sy.dma_start(out=ot_[tt % 2][:],
                         in_=m_gate[tt * 128:(tt + 1) * 128, :]
                         ).then_inc(s_ol.h, 16)
        s_ol.add(16)
        v_ol = s_ol.v

        fl0 = fl.v

        @p.on("vector")
        def _(v, tt=tt, v_ol=v_ol, fl0=fl0):
            v.wait_ge(s_ol.h, v_ol)
            if tt >= 1:
                v.wait_ge(s_ote.h, tt * NF)
            o = ot_[tt % 2]
            v.scalar_tensor_tensor(oqi[:], o[:], inv_o[:], o[:],
                                   ALU.mult, ALU.bypass).then_inc(fl.h, 1)
            v.wait_ge(fl.h, fl0 + 1)
            v.tensor_copy(oqf[:], oqi[:]).then_inc(fl.h, 1)
            v.wait_ge(fl.h, fl0 + 2)
            v.scalar_tensor_tensor(oqb[:], oqf[:], sq_o[:], oqf[:],
                                   ALU.mult, ALU.bypass).then_inc(s_oq.h, 1)
        fl.add(2)
        v_oq = s_oq.add()

        for c in range(NF):
            k = tt * NF + c

            @p.on("tensor")
            def _(t, c=c, k=k, v_oq=v_oq):
                t.wait_ge(s_oq.h, v_oq)
                if k >= 2:
                    t.wait_ge(s_ote.h, k - 1)
                t.transpose(ptd[k % 2][:], oqb[:, c * 128:(c + 1) * 128],
                            ident_sb[:]).then_inc(s_otm.h, 1)
            v_tm = s_otm.add()

            @p.on("vector")
            def _(v, c=c, k=k, v_tm=v_tm, tt=tt):
                v.wait_ge(s_otm.h, v_tm)
                if tt >= 1 and c == 0:
                    v.wait_ge(s_dmm.h, 2 * tt)
                v.tensor_copy(OQF[c][:], ptd[k % 2][:]).then_inc(s_ote.h, 1)
            s_ote.add()
        v_ote = s_ote.v

        @p.on("sync")
        def _(sy, tt=tt):
            if tt >= 1:
                sy.wait_ge(s_xr.h, 16 * tt)
            if tt >= 2:
                sy.wait_ge(s_yst.h, 32 * (tt - 1))
            sy.dma_start(out=xr[tt % 2][:], in_=x_in[tt * 128:(tt + 1) * 128, :]
                         ).then_inc(s_xr.h, 16)
        s_xr.add(16)
        v_xr = s_xr.v

        for dc in range(2):
            g_idx = tt * 2 + dc

            @p.on("tensor")
            def _(t, tt=tt, dc=dc, g_idx=g_idx, v_ote=v_ote):
                t.wait_ge(s_ote.h, v_ote)
                if g_idx == 0:
                    t.wait_ge(s_wdl.h, 16 * NF)
                if g_idx >= 2:
                    t.wait_ge(s_dev.h, g_idx - 1)
                ins = None
                for c in range(NF):
                    ins = t.matmul(pd[dc][:], OQF[c][:],
                                   wds[c][:, dc * 512:(dc + 1) * 512],
                                   start=(c == 0), stop=(c == NF - 1))
                ins.then_inc(s_dmm.h, 1)
            v_dm = s_dmm.add()

            @p.on("vector")
            def _(v, tt=tt, dc=dc, g_idx=g_idx, v_dm=v_dm):
                v.wait_ge(s_dmm.h, v_dm)
                if g_idx >= 2:
                    v.wait_ge(s_yst.h, 16 * (g_idx - 1))
                v.tensor_copy(yt[dc][:], pd[dc][:]).then_inc(s_dev.h, 1)
            v_de = s_dev.add()

            gfl = p.flu["gpsimd"].add()

            @p.on("gpsimd")
            def _(g_, tt=tt, dc=dc, g_idx=g_idx, v_de=v_de, v_xr=v_xr,
                  gfl=gfl):
                g_.wait_ge(s_dev.h, v_de)
                g_.wait_ge(s_xr.h, v_xr)
                g_.tensor_tensor(yt[dc][:], yt[dc][:],
                                 xr[tt % 2][:, dc * 512:(dc + 1) * 512],
                                 ALU.add).then_inc(p.flu["gpsimd"].h, 1)
                g_.wait_ge(p.flu["gpsimd"].h, gfl)
                if g_idx >= 1:
                    g_.wait_ge(s_yst.h, 16 * g_idx)
                g_.dma_start(out=y_out[tt * 128:(tt + 1) * 128,
                                       dc * 512:(dc + 1) * 512],
                             in_=yt[dc][:]).then_inc(s_yst.h, 16)
            s_yst.add(16)
    scD.close()

    if DEBUG_F1:
        c_db = chain(p, "vector", [c_so, (s_yst, 32 * NT)], [
            lambda v: v.memset(dbsc[:], 0.0),
            lambda v: v.tensor_copy(dbsc[:, 0:1], sq_x[:]),
            lambda v: v.tensor_copy(dbsc[:, 1:2], inv_x[:]),
            lambda v: v.tensor_copy(dbsc[:, 2:3], sq_u[:]),
            lambda v: v.tensor_copy(dbsc[:, 3:4], inv_u[:]),
            lambda v: v.tensor_copy(dbsc[:, 4:5], sq_g[:]),
            lambda v: v.tensor_copy(dbsc[:, 5:6], inv_g[:]),
            lambda v: v.tensor_copy(dbsc[:, 6:7], sq_g2[:]),
            lambda v: v.tensor_copy(dbsc[:, 7:8], inv_g2[:]),
            lambda v: v.tensor_copy(dbsc[:, 8:9], sq_o[:]),
            lambda v: v.tensor_copy(dbsc[:, 9:10], inv_o[:]),
        ], flush="all")
        s_db = p.sem()

        @p.on("gpsimd")
        def _(g, c_db=c_db):
            g.wait_ge(c_db[0].h, c_db[1])
            g.dma_start(out=dbg_sc, in_=dbsc[:]).then_inc(s_db.h, 16)
            g.dma_start(out=dbg_xq, in_=XQ[0][:, 0:128]).then_inc(s_db.h, 16)
            g.dma_start(out=dbg_wuT, in_=wuT[0:128, 0:512]).then_inc(s_db.h, 16)
            g.wait_ge(s_db.h, 48)
        s_db.add(48)

    @p.on("gpsimd")
    def _(g_):
        g_.wait_ge(s_yst.h, 32 * NT)



FFNB = 4096
CHUNK = 64
DEBUG_B1 = False


def emit_b1(nc, p, io, ncores=None):
    if ncores is None:
        ncores = NCORE
    groups = [[i, i + 1] for i in range(0, ncores, 2)]
    x_in = io.inp("x", [T, D])
    xl3_in = io.inp("xl3", [3, D])
    iso_in = io.inp("is_odd", [1, 1])
    pnw_in = io.inp("pnw", [1, D])
    cw4_in = io.inp("cw4", [4, D])
    cb_in = io.inp("cb", [1, D])
    waT_in = io.inp("waT", [D, D])
    wiT_in = io.inp("wiT", [D, D])
    wvT_in = io.inp("wvT", [D, D])
    db_in = io.inp("db", [1, D])
    woT_in = io.inp("woT", [D, D])
    fnw_in = io.inp("fnw", [1, D])
    wguT_in = io.inp("wguT", [D, 2 * FFNB])
    wdnT_in = io.inp("wdnT", [FFNB, D])
    identb_in = io.inp("ident_bf", [128, 128], BF16)
    identf_in = io.inp("ident_f", [128, 128])
    y_out = io.out("y", [T, D])

    if DEBUG_B1:
        dbg_afm = io.out("dbg_afm", [128, T])
        dbg_ufm = io.out("dbg_ufm", [128, T])
        dbg_sr = io.out("dbg_sr", [128, T])
        dbg_x2 = io.out("dbg_x2", [128, D])
        dbg_hbfm = io.out("dbg_hbfm", [128, T], BF16)
    a_fm = io.internal("a_fm", [D, T])
    u_fm = io.internal("u_fm", [D, T])
    hb_fm = io.internal("hb_fm", [FFNB, T], BF16)
    ag_in = io.internal("ag_in", [1, D])
    ag_out = io.internal("ag_out", [2, D])

    NCH = T // CHUNK
    fl = p.flu["vector"]

    identb = p.sb([128, 128], BF16)
    identf = p.sb([128, 128], F32)
    pnw_bc = p.sb([128, D], F32)
    fnw_bc = p.sb([128, D], F32)
    db_bc = p.sb([128, D], F32)
    cwt = [p.sb([128, 4], F32) for _ in range(ND)]
    cbt = [p.sb([128, 1], F32) for _ in range(ND)]
    iso_pp = p.sb([128, 1], F32)
    s_misc = p.sem()

    @p.on("gpsimd")
    def _(g):
        n = [0]
        def nx(ins):
            n[0] += 16
            ins.then_inc(s_misc.h, 16)
            g.wait_ge(s_misc.h, n[0])
        nx(g.dma_start(out=identb[:], in_=identb_in))
        nx(g.dma_start(out=identf[:], in_=identf_in))
        nx(g.dma_start(out=pnw_bc[:], in_=pnw_in.broadcast_to((128, D))))
        nx(g.dma_start(out=fnw_bc[:], in_=fnw_in.broadcast_to((128, D))))
        nx(g.dma_start(out=db_bc[:], in_=db_in.broadcast_to((128, D))))
        nx(g.dma_start(out=iso_pp[:], in_=iso_in.broadcast_to((128, 1))))
        for c in range(ND):
            for k in range(4):
                nx(g.dma_start(out=cwt[c][:, k:k + 1],
                               in_=cw4_in[k:k + 1, c * 128:(c + 1) * 128]))
            nx(g.dma_start(out=cbt[c][:], in_=cb_in[0:1, c * 128:(c + 1) * 128]))
    s_misc.add(16 * (6 + 5 * ND))
    v_misc = s_misc.v

    x_t = [p.sb([128, D], F32) for _ in range(NT)]
    scC = p.scope()
    cfm = [p.sb_in(scC, [128, T], BF16) for _ in range(ND)]
    sc1 = p.scope()
    ext = [p.sb_in(sc1, [128, T + 3], F32) for _ in range(ND)]
    xl3t = p.sb_in(sc1, [128, D], F32)
    xnr = [p.sb_in(sc1, [128, D], F32) for _ in range(2)]
    ptf = [p.ps_in(sc1, [128, 128], F32) for _ in range(2)]
    s_ld = emit_load_tiles(p, x_in, x_t)
    s_oln = emit_ln(p, x_t, x_t, lambda i: [(s_ld, 16 * (i + 1))],
                    rms=False, eps=1e-5, sc=sc1, nb=1)
    s_l3 = p.sem()
    c_z3 = chain(p, "vector", [], [lambda v: v.memset(xl3t[:], 0.0)])

    @p.on("sync")
    def _(sy, c_z3=c_z3):
        sy.wait_ge(c_z3[0].h, c_z3[1])
        sy.dma_start(out=xl3t[0:3, :], in_=xl3_in).then_inc(s_l3.h, 16)
    s_l3.add(16)

    s_tp1 = p.sem()
    s_tm1 = p.sem()

    s_oln3 = emit_ln(p, [xl3t], [xl3t], lambda i: [(s_l3, 16)], rms=False,
                     eps=1e-5, n_tiles=1, sc=sc1, nb=1)
    s_ln3 = emit_ln(p, [xl3t], [xl3t], lambda i: [(s_oln3, 1)], rms=True,
                    eps=1e-6, n_tiles=1, sc=sc1, nb=1)
    c_w3 = chain(p, "vector", [(s_ln3, 1), (s_misc, v_misc)], [
        lambda v: v.tensor_tensor(xl3t[:], xl3t[:], pnw_bc[:], ALU.mult)])
    for c in range(ND):
        @p.on("tensor")
        def _(t, c=c, c_w3=c_w3):
            if c == 0:
                t.wait_ge(c_w3[0].h, c_w3[1])
                t.wait_ge(s_misc.h, 32)
            if c >= 2:
                t.wait_ge(s_tp1.h, c - 1)
            t.transpose(ptf[c % 2][:], xl3t[:, c * 128:(c + 1) * 128],
                        identf[:]).then_inc(s_tm1.h, 1)
        v_tm = s_tm1.add()

        @p.on("vector")
        def _(v, c=c, v_tm=v_tm):
            v.wait_ge(s_tm1.h, v_tm)
            v.tensor_copy(ext[c][:, 0:3], ptf[c % 2][:, 0:3]).then_inc(s_tp1.h, 1)
        s_tp1.add()
    base_tp1 = s_tp1.v

    def at1(i, s_fin, fin_v):
        c_wm = chain(p, "vector", [(s_fin, fin_v), (s_misc, v_misc)], [
            lambda v, i=i: v.tensor_tensor(xnr[i % 2][:], xnr[i % 2][:],
                                           pnw_bc[:], ALU.mult)])
        for c in range(ND):
            k = i * ND + c

            @p.on("tensor")
            def _(t, i=i, c=c, k=k, c_wm=c_wm):
                t.wait_ge(c_wm[0].h, c_wm[1])
                if base_tp1 + k >= 2:
                    t.wait_ge(s_tp1.h, base_tp1 + k - 1)
                t.transpose(ptf[k % 2][:], xnr[i % 2][:, c * 128:(c + 1) * 128],
                            identf[:]).then_inc(s_tm1.h, 1)
            v_tm = s_tm1.add()

            @p.on("vector")
            def _(v, i=i, c=c, k=k, v_tm=v_tm):
                v.wait_ge(s_tm1.h, v_tm)
                v.tensor_copy(ext[c][:, 3 + i * 128:3 + (i + 1) * 128],
                              ptf[k % 2][:]).then_inc(s_tp1.h, 1)
            s_tp1.add()

    s_ln1 = emit_ln(p, x_t, [xnr[i % 2] for i in range(NT)],
                    lambda i: [(s_oln, i + 1)] +
                              ([(s_tp1, base_tp1 + (i - 1) * ND)]
                               if i >= 2 else []),
                    rms=True, eps=1e-6, sc=sc1, nb=1, after_tile=at1)
    v_tp1 = base_tp1 + NT * ND

    HW_ = T // 2
    ctmp = p.sb_in(sc1, [128, HW_], F32)
    s_cv = p.sem()
    for c in range(ND):
        for hh in range(2):
            ic = c * 2 + hh
            fl0 = fl.v

            @p.on("vector")
            def _(v, c=c, hh=hh, ic=ic, fl0=fl0):
                if ic == 0:
                    v.wait_ge(s_tp1.h, v_tp1)
                else:
                    v.wait_ge(s_cv.h, ic)
                o0 = hh * HW_
                v.scalar_tensor_tensor(ctmp[:], ext[c][:, o0:o0 + HW_],
                                       cwt[c][:, 0:1],
                                       ext[c][:, o0:o0 + HW_],
                                       ALU.mult, ALU.bypass).then_inc(fl.h, 1)
                v.wait_ge(fl.h, fl0 + 1)
                for k in (1, 2, 3):
                    v.scalar_tensor_tensor(ctmp[:], ext[c][:, o0 + k:o0 + k + HW_],
                                           cwt[c][:, k:k + 1], ctmp[:],
                                           ALU.mult, ALU.add).then_inc(fl.h, 1)
                    v.wait_ge(fl.h, fl0 + k + 1)
                v.scalar_tensor_tensor(cfm[c][:, o0:o0 + HW_], ctmp[:],
                                       cbt[c][:], ctmp[:],
                                       ALU.add, ALU.bypass).then_inc(s_cv.h, 1)
            fl.add(4)
            s_cv.add()
    v_cv = s_cv.v
    sc1.close()

    sc3 = p.scope()
    wa_sb = [p.sb_in(sc3, [128, D], BF16) for _ in range(ND)]
    wi_sb = [p.sb_in(sc3, [128, D], BF16) for _ in range(ND)]
    wv_sb = [p.sb_in(sc3, [128, D], BF16) for _ in range(ND)]
    s_w3 = emit_weight_load_bf16(p, waT_in, wa_sb, wait=[(s_cv, v_cv)])
    s_w3b = emit_weight_load_bf16(p, wiT_in, wi_sb, wait=[(s_w3, 16 * ND)])
    s_w3c = emit_weight_load_bf16(p, wvT_in, wv_sb, wait=[(s_w3b, 16 * ND)])
    pa = [p.ps_in(sc3, [128, 512], F32) for _ in range(2)]
    pi = [p.ps_in(sc3, [128, 512], F32) for _ in range(2)]
    pv = [p.ps_in(sc3, [128, 512], F32) for _ in range(2)]
    ptf3 = [p.ps_in(sc3, [128, 128], F32) for _ in range(2)]
    t1 = p.sb_in(sc3, [128, 512], F32)
    av = [p.sb_in(sc3, [128, 512], F32) for _ in range(2)]
    ig = p.sb_in(sc3, [128, 512], F32)
    vv = [p.sb_in(sc3, [128, 512], F32) for _ in range(2)]
    etf = [p.sb_in(sc3, [128, 128], F32) for _ in range(2)]
    s_mm3 = p.sem()
    s_sg3 = p.sem()
    s_ev3 = p.sem()
    s_tm3 = p.sem()
    s_te3 = p.sem()
    s_st3 = p.sem()
    for tt in range(NT):
        for oc in range(2):
            g_idx = tt * 2 + oc

            @p.on("tensor")
            def _(t, tt=tt, oc=oc, g_idx=g_idx):
                if g_idx == 0:
                    t.wait_ge(s_cv.h, v_cv)
                    t.wait_ge(s_w3c.h, 16 * ND)
                if g_idx >= 2:
                    t.wait_ge(s_ev3.h, 3 * (g_idx - 1))
                for wsb, ps in ((wa_sb, pa), (wi_sb, pi), (wv_sb, pv)):
                    ins = None
                    for c in range(ND):
                        ins = t.matmul(ps[g_idx % 2][:],
                                       cfm[c][:, tt * 128:(tt + 1) * 128],
                                       wsb[c][:, oc * 512:(oc + 1) * 512],
                                       start=(c == 0), stop=(c == ND - 1))
                    ins.then_inc(s_mm3.h, 1)
            v_mm = s_mm3.add(3)

            @p.on("vector")
            def _(v, tt=tt, oc=oc, g_idx=g_idx, v_mm=v_mm):
                v.wait_ge(s_mm3.h, v_mm)
                if g_idx >= 2:
                    v.wait_ge(s_te3.h, 8 * (g_idx - 1))
                v.tensor_tensor(t1[:], pa[g_idx % 2][:],
                                db_bc[:, oc * 512:(oc + 1) * 512], ALU.add
                                ).then_inc(s_ev3.h, 1)
            s_ev3.add()
            v_e1 = s_ev3.v

            sfl = p.flu["scalar"].add()

            @p.on("scalar")
            def _(s, g_idx=g_idx, v_e1=v_e1, sfl=sfl):
                s.wait_ge(s_ev3.h, v_e1)
                s.activation(av[g_idx % 2][:], t1[:], AF.Sigmoid
                             ).then_inc(p.flu["scalar"].h, 1)
                s.activation(ig[:], pi[g_idx % 2][:], AF.Sigmoid)
                s.wait_ge(p.flu["scalar"].h, sfl)
                s.activation(t1[:], av[g_idx % 2][:], AF.Square
                             ).then_inc(s_sg3.h, 1)
            v_sg = s_sg3.add()

            fl0 = fl.v

            @p.on("vector")
            def _(v, g_idx=g_idx, v_sg=v_sg, fl0=fl0):
                v.wait_ge(s_sg3.h, v_sg)
                v.tensor_scalar(t1[:], t1[:], -1.0, 1.0, ALU.mult, ALU.add
                                ).then_inc(fl.h, 1)
                v.wait_ge(fl.h, fl0 + 1)
                v.tensor_scalar(t1[:], t1[:], 1e-8, 0.0, ALU.max, ALU.add
                                ).then_inc(fl.h, 1)
                v.wait_ge(fl.h, fl0 + 2)
                v.tensor_copy(vv[g_idx % 2][:], pv[g_idx % 2][:]
                              ).then_inc(s_ev3.h, 1)
            fl.add(2)
            s_ev3.add()
            v_e2 = s_ev3.v

            @p.on("scalar")
            def _(s, v_e2=v_e2):
                s.wait_ge(s_ev3.h, v_e2)
                s.wait_ge(p.s_cpool.h, 16)
                s.activation(t1[:], t1[:], AF.Sqrt, bias=p.constf(0.0)
                             ).then_inc(s_sg3.h, 1)
            v_sg2 = s_sg3.add()

            fl0 = fl.v

            @p.on("vector")
            def _(v, g_idx=g_idx, v_sg2=v_sg2, fl0=fl0):
                v.wait_ge(s_sg3.h, v_sg2)
                v.tensor_tensor(vv[g_idx % 2][:], ig[:], vv[g_idx % 2][:],
                                ALU.mult).then_inc(fl.h, 1)
                v.wait_ge(fl.h, fl0 + 1)
                v.tensor_tensor(vv[g_idx % 2][:], vv[g_idx % 2][:], t1[:],
                                ALU.mult).then_inc(s_ev3.h, 1)
            fl.add(1)
            s_ev3.add()
            v_e3 = s_ev3.v

            for q in range(4):
                for which in range(2):
                    src = av if which == 0 else vv
                    kq = (g_idx * 4 + q) * 2 + which

                    @p.on("tensor")
                    def _(t, g_idx=g_idx, q=q, src=src, kq=kq, v_e3=v_e3):
                        t.wait_ge(s_ev3.h, v_e3)
                        if kq >= 2:
                            t.wait_ge(s_te3.h, kq - 1)
                        t.transpose(ptf3[kq % 2][:],
                                    src[g_idx % 2][:, q * 128:(q + 1) * 128],
                                    identf[:]).then_inc(s_tm3.h, 1)
                    v_tm = s_tm3.add()

                    @p.on("vector")
                    def _(v, kq=kq, v_tm=v_tm):
                        v.wait_ge(s_tm3.h, v_tm)
                        if kq >= 2:
                            v.wait_ge(s_st3.h, 16 * (kq - 1))
                        v.tensor_copy(etf[kq % 2][:], ptf3[kq % 2][:]
                                      ).then_inc(s_te3.h, 1)
                    v_te = s_te3.add()

                    @p.on("gpsimd")
                    def _(g, tt=tt, oc=oc, q=q, which=which, kq=kq, v_te=v_te):
                        g.wait_ge(s_te3.h, v_te)
                        if kq >= 1:
                            g.wait_ge(s_st3.h, 16 * kq)
                        dst = a_fm if which == 0 else u_fm
                        dr = (oc * 4 + q) * 128
                        g.dma_start(out=dst[dr:dr + 128,
                                            tt * 128:(tt + 1) * 128],
                                    in_=etf[kq % 2][:]).then_inc(s_st3.h, 16)
                    s_st3.add(16)
    v_st3 = s_st3.v
    sc3.close()

    scS = p.scope()
    SR = [p.sb_in(scS, [128, T], BF16) for _ in range(ND)]
    sc4 = p.scope()
    afs = [p.sb_in(sc4, [128, T], F32) for _ in range(2)]
    ufs = [p.sb_in(sc4, [128, T], F32) for _ in range(2)]
    ac = p.sb_in(sc4, [128, CHUNK], F32)
    cd = p.sb_in(sc4, [128, CHUNK], F32)
    cdc = p.sb_in(sc4, [128, CHUNK], F32)
    u2 = p.sb_in(sc4, [128, CHUNK], F32)
    ci = p.sb_in(sc4, [128, CHUNK], F32)
    stf = p.sb_in(sc4, [128, CHUNK], F32)
    st = p.sb_in(sc4, [128, 1], F32)
    stall = p.sb_in(sc4, [128, ND], F32)
    stin = p.sb_in(sc4, [128, 1], F32)
    dum4 = p.sb_in(sc4, [1, 1], F32)
    s_sl = p.sem()
    s_sc = p.sem()
    s_agd = p.sem()
    s_agc = p.sem()
    s_sti = p.sem()

    def scan_pass(pass_no, stin_vals=None):
        for c in range(ND):
            li = pass_no * ND + c

            @p.on("sync")
            def _(sy, c=c, li=li):
                if li == 0:
                    sy.wait_ge(s_st3.h, v_st3)
                if li >= 1:
                    sy.wait_ge(s_sl.h, 32 * li)
                if li >= 2:
                    sy.wait_ge(s_sc.h, li - 1)
                sy.dma_start(out=afs[li % 2][:],
                             in_=a_fm[c * 128:(c + 1) * 128, :]
                             ).then_inc(s_sl.h, 16)
                sy.dma_start(out=ufs[li % 2][:],
                             in_=u_fm[c * 128:(c + 1) * 128, :]
                             ).then_inc(s_sl.h, 16)
            s_sl.add(32)
            v_sl = s_sl.v

            fl0 = fl.v
            sv = stin_vals[c] if stin_vals else None

            @p.on("vector")
            def _(v, c=c, li=li, v_sl=v_sl, fl0=fl0, pass_no=pass_no, sv=sv):
                v.wait_ge(s_sl.h, v_sl)
                v.wait_ge(p.s_cpool.h, 16)
                if sv is not None:
                    v.wait_ge(s_sti.h, sv)
                k = [fl0]

                def st_(ins):
                    k[0] += 1
                    ins.then_inc(fl.h, 1)
                    v.wait_ge(fl.h, k[0])
                if pass_no == 0:
                    st_(v.memset(st[:], 0.0))
                else:
                    st_(v.tensor_tensor(st[:], stin[:], iso_pp[:], ALU.mult))
                a_, u_ = afs[li % 2], ufs[li % 2]
                for ch in range(NCH):
                    w0 = ch * CHUNK
                    st_(v.tensor_scalar(ac[:], a_[:, w0:w0 + CHUNK], 1e-10,
                                        0.0, ALU.max, ALU.add))
                    st_(v.tensor_tensor_scan(cd[:], ac[:], ac[:],
                                             p.constf(1.0), ALU.mult,
                                             ALU.bypass))
                    st_(v.tensor_scalar(cdc[:], cd[:], 1e-10, 0.0,
                                        ALU.max, ALU.add))
                    st_(v.reciprocal(cdc[:], cdc[:]))
                    st_(v.tensor_tensor(u2[:], u_[:, w0:w0 + CHUNK], cdc[:],
                                        ALU.mult))
                    st_(v.tensor_tensor_scan(ci[:], u2[:], u2[:],
                                             p.constf(0.0), ALU.add,
                                             ALU.bypass))
                    st_(v.scalar_tensor_tensor(stf[:], ci[:], st[:], cd[:],
                                               ALU.add, ALU.mult))
                    st_(v.tensor_copy(st[:], stf[:, CHUNK - 1:CHUNK]))
                    st_(v.tensor_copy(SR[c][:, w0:w0 + CHUNK], stf[:]))
                if pass_no == 0:
                    st_(v.tensor_copy(stall[:, c:c + 1], st[:]))
                v.memset(dum4[:], 0.0).then_inc(s_sc.h, 1)
            fl.add(1 + NCH * 9 + (1 if pass_no == 0 else 0))
            s_sc.add()

    scan_pass(0)
    v_p1 = s_sc.v

    for c in range(ND):
        @p.on("gpsimd")
        def _(g, c=c, v_p1=v_p1):
            if c == 0:
                g.wait_ge(s_sc.h, v_p1)
            else:
                g.wait_ge(s_agd.h, 16 * c)
            g.dma_start(out=ag_in[0:1, c * 128:(c + 1) * 128],
                        in_=stall[:, c:c + 1]).then_inc(s_agd.h, 16)
        s_agd.add(16)

    @p.on("gpsimd")
    def _(g, v_agd=s_agd.v):
        g.wait_ge(s_agd.h, v_agd)
        g.collective_compute(
            "AllGather", ALU.bypass, replica_groups=groups,
            ins=[ag_in.opt()], outs=[ag_out.opt()],
        ).then_inc(s_agc.h, 1)
    s_agc.add(1)

    stin_vals = []
    for c in range(ND):
        @p.on("gpsimd")
        def _(g, c=c, v0=s_sti.v):
            g.wait_ge(s_agc.h, 1)
            if c >= 1:
                g.wait_ge(s_sti.h, v0)
                g.wait_ge(s_sc.h, ND + c)
            g.dma_start(out=stin[:], in_=ag_out[0:1, c * 128:(c + 1) * 128]
                        ).then_inc(s_sti.h, 16)
        stin_vals.append(s_sti.add(16))
    scan_pass(1, stin_vals)
    v_scan = s_sc.v
    sc4.close()

    sc5 = p.scope()
    wo_sb = [p.sb_in(sc5, [128, D], BF16) for _ in range(ND)]
    s_w5 = emit_weight_load_bf16(p, woT_in, wo_sb, wait=[(s_sc, v_scan)])
    po = [p.ps_in(sc5, [128, 512], F32) for _ in range(2)]
    s_mm5 = p.sem()
    s_ev5 = p.sem()
    for tt in range(NT):
        for oc in range(2):
            g_idx = tt * 2 + oc

            @p.on("tensor")
            def _(t, tt=tt, oc=oc, g_idx=g_idx):
                if g_idx == 0:
                    t.wait_ge(s_sc.h, v_scan)
                    t.wait_ge(s_w5.h, 16 * ND)
                if g_idx >= 2:
                    t.wait_ge(s_ev5.h, g_idx - 1)
                ins = None
                for c in range(ND):
                    ins = t.matmul(po[g_idx % 2][:],
                                   SR[c][:, tt * 128:(tt + 1) * 128],
                                   wo_sb[c][:, oc * 512:(oc + 1) * 512],
                                   start=(c == 0), stop=(c == ND - 1))
                ins.then_inc(s_mm5.h, 1)
            v_mm = s_mm5.add()

            @p.on("vector")
            def _(v, tt=tt, oc=oc, g_idx=g_idx, v_mm=v_mm):
                v.wait_ge(s_mm5.h, v_mm)
                v.tensor_tensor(x_t[tt][:, oc * 512:(oc + 1) * 512],
                                x_t[tt][:, oc * 512:(oc + 1) * 512],
                                po[g_idx % 2][:], ALU.add
                                ).then_inc(s_ev5.h, 1)
            s_ev5.add()
    v_x2 = s_ev5.v
    if DEBUG_B1:
        s_db5 = p.sem()

        @p.on("gpsimd")
        def _(g):
            g.wait_ge(s_ev5.h, v_x2)
            g.dma_start(out=dbg_afm, in_=a_fm[0:128, :]).then_inc(s_db5.h, 16)
            g.wait_ge(s_db5.h, 16)
            g.dma_start(out=dbg_ufm, in_=u_fm[0:128, :]).then_inc(s_db5.h, 16)
            g.wait_ge(s_db5.h, 32)
            g.dma_start(out=dbg_sr, in_=SR[0][:]).then_inc(s_db5.h, 16)
            g.wait_ge(s_db5.h, 48)
            g.dma_start(out=dbg_x2, in_=x_t[0][:]).then_inc(s_db5.h, 16)
            g.wait_ge(s_db5.h, 64)
        s_db5.add(64)
    sc5.close()
    scS.close()
    scC.close()

    scH = p.scope()
    HN = [p.sb_in(scH, [128, T], BF16) for _ in range(ND)]
    sc6 = p.scope()
    xn6 = [p.sb_in(sc6, [128, D], F32) for _ in range(2)]
    ptf6 = [p.ps_in(sc6, [128, 128], BF16) for _ in range(2)]
    xnb6 = p.sb_in(sc6, [128, D], BF16)
    s_tm6 = p.sem()
    s_tp6 = p.sem()

    def at6(i, s_fin, fin_v):
        c_wm = chain(p, "vector", [(s_fin, fin_v)] +
                     ([(s_tp6, i * ND)] if i >= 1 else []), [
            lambda v, i=i: v.tensor_tensor(xnb6[:], xn6[i % 2][:],
                                           fnw_bc[:], ALU.mult)])
        for c in range(ND):
            k = i * ND + c

            @p.on("tensor")
            def _(t, i=i, c=c, k=k, c_wm=c_wm):
                t.wait_ge(c_wm[0].h, c_wm[1])
                if k >= 2:
                    t.wait_ge(s_tp6.h, k - 1)
                t.transpose(ptf6[k % 2][:], xnb6[:, c * 128:(c + 1) * 128],
                            identb[:]).then_inc(s_tm6.h, 1)
            v_tm = s_tm6.add()

            @p.on("vector")
            def _(v, i=i, c=c, k=k, v_tm=v_tm):
                v.wait_ge(s_tm6.h, v_tm)
                v.tensor_copy(HN[c][:, i * 128:(i + 1) * 128],
                              ptf6[k % 2][:]).then_inc(s_tp6.h, 1)
            s_tp6.add()

    s_ln6 = emit_ln(p, x_t, [xn6[i % 2] for i in range(NT)],
                    lambda i: [(s_ev5, v_x2)] +
                              ([(s_tp6, (i - 1) * ND)] if i >= 2 else []),
                    rms=True, eps=1e-6, sc=sc6, nb=1, after_tile=at6)
    v_hn = NT * ND
    sc6.close()

    sc7 = p.scope()
    wg7 = [p.sb_in(sc7, [128, 512], BF16) for _ in range(ND)]
    wu7 = [p.sb_in(sc7, [128, 512], BF16) for _ in range(ND)]
    pg7 = [p.ps_in(sc7, [128, 512], F32) for _ in range(2)]
    pu7 = [p.ps_in(sc7, [128, 512], F32) for _ in range(2)]
    ptb7 = [p.ps_in(sc7, [128, 128], BF16) for _ in range(2)]
    sgb = p.sb_in(sc7, [128, 512], F32)
    hbb = [p.sb_in(sc7, [128, 512], BF16) for _ in range(2)]
    etb7 = [p.sb_in(sc7, [128, 128], BF16) for _ in range(2)]
    s_wl7 = p.sem()
    s_mm7 = p.sem()
    s_sg7 = p.sem()
    s_hb7 = p.sem()
    s_tm7 = p.sem()
    s_te7 = p.sem()
    s_st7 = p.sem()
    NJ = FFNB // 512
    for j in range(NJ):
        @p.on("gpsimd")
        def _(sy, j=j):
            if j == 0:
                sy.wait_ge(s_tp6.h, v_hn)
            if j >= 1:
                sy.wait_ge(s_wl7.h, 16 * 2 * ND * j)
                sy.wait_ge(s_mm7.h, 2 * NT * j)
            for c in range(ND):
                if j or c:
                    sy.wait_ge(s_wl7.h, 16 * (2 * ND * j + c))
                sy.dma_start(out=wg7[c][:],
                             in_=wguT_in[c * 128:(c + 1) * 128,
                                         j * 512:(j + 1) * 512]
                             ).then_inc(s_wl7.h, 16)
            for c in range(ND):
                sy.wait_ge(s_wl7.h, 16 * (2 * ND * j + ND + c))
                sy.dma_start(out=wu7[c][:],
                             in_=wguT_in[c * 128:(c + 1) * 128,
                                         FFNB + j * 512:FFNB + (j + 1) * 512]
                             ).then_inc(s_wl7.h, 16)
        s_wl7.add(16 * 2 * ND)
        v_wl = s_wl7.v

        for tt in range(NT):
            g_idx = j * NT + tt

            @p.on("tensor")
            def _(t, j=j, tt=tt, g_idx=g_idx, v_wl=v_wl):
                if g_idx == 0:
                    t.wait_ge(s_tp6.h, v_hn)
                t.wait_ge(s_wl7.h, v_wl)
                if g_idx >= 2:
                    t.wait_ge(s_hb7.h, g_idx - 1)
                for wsb, ps in ((wg7, pg7), (wu7, pu7)):
                    ins = None
                    for c in range(ND):
                        ins = t.matmul(ps[g_idx % 2][:],
                                       HN[c][:, tt * 128:(tt + 1) * 128],
                                       wsb[c][:], start=(c == 0),
                                       stop=(c == ND - 1))
                    ins.then_inc(s_mm7.h, 1)
            v_mm = s_mm7.add(2)

            @p.on("scalar")
            def _(s, g_idx=g_idx, v_mm=v_mm):
                s.wait_ge(s_mm7.h, v_mm)
                s.activation(sgb[:], pg7[g_idx % 2][:], AF.Silu
                             ).then_inc(s_sg7.h, 1)
            v_sg = s_sg7.add()

            @p.on("vector")
            def _(v, g_idx=g_idx, v_sg=v_sg):
                v.wait_ge(s_sg7.h, v_sg)
                if g_idx >= 2:
                    v.wait_ge(s_te7.h, 4 * (g_idx - 1))
                v.tensor_tensor(hbb[g_idx % 2][:], sgb[:], pu7[g_idx % 2][:],
                                ALU.mult).then_inc(s_hb7.h, 1)
            v_hb = s_hb7.add()

            for q in range(4):
                kq = g_idx * 4 + q

                @p.on("tensor")
                def _(t, g_idx=g_idx, q=q, kq=kq, v_hb=v_hb):
                    t.wait_ge(s_hb7.h, v_hb)
                    if kq >= 2:
                        t.wait_ge(s_te7.h, kq - 1)
                    t.transpose(ptb7[kq % 2][:],
                                hbb[g_idx % 2][:, q * 128:(q + 1) * 128],
                                identb[:]).then_inc(s_tm7.h, 1)
                v_tm = s_tm7.add()

                @p.on("vector")
                def _(v, kq=kq, v_tm=v_tm):
                    v.wait_ge(s_tm7.h, v_tm)
                    if kq >= 2:
                        v.wait_ge(s_st7.h, 16 * (kq - 1))
                    v.tensor_copy(etb7[kq % 2][:], ptb7[kq % 2][:]
                                  ).then_inc(s_te7.h, 1)
                v_te = s_te7.add()

                @p.on("gpsimd")
                def _(g, j=j, tt=tt, q=q, kq=kq, v_te=v_te):
                    g.wait_ge(s_te7.h, v_te)
                    if kq >= 1:
                        g.wait_ge(s_st7.h, 16 * kq)
                    fr = (j * 4 + q) * 128
                    g.dma_start(out=hb_fm[fr:fr + 128,
                                          tt * 128:(tt + 1) * 128],
                                in_=etb7[kq % 2][:]).then_inc(s_st7.h, 16)
                s_st7.add(16)
    v_hbst = s_st7.v
    sc7.close()
    scH.close()

    sc8 = p.scope()
    NF7 = FFNB // 128
    wd7 = [p.sb_in(sc8, [128, D], BF16) for _ in range(NF7)]
    hbc = [p.sb_in(sc8, [128, 128], BF16) for _ in range(NF7)]
    pd7 = [p.ps_in(sc8, [128, 512], F32) for _ in range(2)]
    yt7 = [p.sb_in(sc8, [128, 512], F32) for _ in range(2)]
    xo8 = [p.sb_in(sc8, [128, D], F32) for _ in range(2)]
    s_hl8 = p.sem()
    s_xo8 = p.sem()
    s_mm8 = p.sem()
    s_ev8 = p.sem()
    s_yst = p.sem()
    s_wl8 = emit_weight_load_bf16(p, wdnT_in, wd7, wait=[(s_st7, v_hbst)])

    for tt in range(NT):
        @p.on("sync")
        def _(sy, tt=tt):
            if tt == 0:
                sy.wait_ge(s_st7.h, v_hbst)
            else:
                sy.wait_ge(s_hl8.h, 16 * NF7 * tt)
                sy.wait_ge(s_mm8.h, 2 * tt)
            for c in range(NF7):
                if c >= 1:
                    sy.wait_ge(s_hl8.h, 16 * (NF7 * tt + c))
                sy.dma_start(out=hbc[c][:],
                             in_=hb_fm[c * 128:(c + 1) * 128,
                                       tt * 128:(tt + 1) * 128]
                             ).then_inc(s_hl8.h, 16)
        s_hl8.add(16 * NF7)
        v_hl = s_hl8.v

        @p.on("sync")
        def _(sy, tt=tt):
            if tt >= 1:
                sy.wait_ge(s_xo8.h, 16 * tt)
            if tt >= 2:
                sy.wait_ge(s_yst.h, 32 * (tt - 1))
            sy.dma_start(out=xo8[tt % 2][:], in_=x_in[tt * 128:(tt + 1) * 128, :]
                         ).then_inc(s_xo8.h, 16)
        s_xo8.add(16)
        v_xo = s_xo8.v

        for dc in range(2):
            g_idx = tt * 2 + dc

            @p.on("tensor")
            def _(t, tt=tt, dc=dc, g_idx=g_idx, v_hl=v_hl):
                t.wait_ge(s_hl8.h, v_hl)
                if g_idx == 0:
                    t.wait_ge(s_wl8.h, 16 * NF7)
                if g_idx >= 2:
                    t.wait_ge(s_ev8.h, g_idx - 1)
                ins = None
                for c in range(NF7):
                    ins = t.matmul(pd7[dc][:], hbc[c][:],
                                   wd7[c][:, dc * 512:(dc + 1) * 512],
                                   start=(c == 0), stop=(c == NF7 - 1))
                ins.then_inc(s_mm8.h, 1)
            v_mm = s_mm8.add()

            @p.on("vector")
            def _(v, tt=tt, dc=dc, g_idx=g_idx, v_mm=v_mm):
                v.wait_ge(s_mm8.h, v_mm)
                if g_idx >= 2:
                    v.wait_ge(s_yst.h, 16 * (g_idx - 1))
                v.tensor_tensor(yt7[dc][:], pd7[dc][:],
                                x_t[tt][:, dc * 512:(dc + 1) * 512],
                                ALU.add).then_inc(s_ev8.h, 1)
            v_ev = s_ev8.add()

            gfl8 = p.flu["gpsimd"].add()

            @p.on("gpsimd")
            def _(g, tt=tt, dc=dc, g_idx=g_idx, v_ev=v_ev, v_xo=v_xo,
                  gfl8=gfl8):
                g.wait_ge(s_ev8.h, v_ev)
                g.wait_ge(s_xo8.h, v_xo)
                g.tensor_tensor(yt7[dc][:], yt7[dc][:],
                                xo8[tt % 2][:, dc * 512:(dc + 1) * 512],
                                ALU.add).then_inc(p.flu["gpsimd"].h, 1)
                g.wait_ge(p.flu["gpsimd"].h, gfl8)
                if g_idx >= 1:
                    g.wait_ge(s_yst.h, 16 * g_idx)
                g.dma_start(out=y_out[tt * 128:(tt + 1) * 128,
                                      dc * 512:(dc + 1) * 512],
                            in_=yt7[dc][:]).then_inc(s_yst.h, 16)
            s_yst.add(16)
    sc8.close()
    if DEBUG_B1:
        s_db8 = p.sem()

        @p.on("gpsimd")
        def _(g):
            g.wait_ge(s_yst.h, 32 * NT)
            g.dma_start(out=dbg_hbfm, in_=hb_fm[0:128, :]).then_inc(s_db8.h, 16)
            g.wait_ge(s_db8.h, 16)
        s_db8.add(16)

    @p.on("gpsimd")
    def _(g):
        g.wait_ge(s_yst.h, 32 * NT)




DBG_B0_SKIP = False
DBG_B0_PHASE = 0


def emit_b0(nc, p, io):
    x_in = io.inp("x", [T, D])
    xprev_in = io.inp("xprev", [63, D])
    iseven_in = io.inp("is_even", [1, 1])
    wdispT_in = io.inp("wdispT", [256, D])
    wfu_in = io.inp("wfu", [16, 32])
    bfu_in = io.inp("bfu", [1, 32])
    rpeT_in = io.inp("rpeT", [16, 64])
    wpos_in = io.inp("wpos", [16, 16])
    wbdm_in = io.inp("wbdm", [128, 8], BF16)
    bbond_in = io.inp("b_bond", [1, 1])
    bdmg_in = io.inp("b_dmg", [1, 1])
    wvT_in = io.inp("wvT", [D, D])
    bval_in = io.inp("b_val", [1, D])
    wcT_in = io.inp("wcT", [D, D])
    bcp_in = io.inp("b_cproj", [1, D])
    identb_in = io.inp("ident_bf", [128, 128], BF16)
    i2_in = io.inp("ident2", [128, 256], BF16)
    y_out = io.out("y", [T, D])

    GW = 512
    NFM = 4
    ash = io.internal("ash", [16 * 128 * 196 + 256], BF16)
    pfb_scr = io.internal("pfb_scr", [128, 64])
    fl = p.flu["vector"]
    gfl = p.flu["gpsimd"]
    sfl = p.flu["scalar"]

    identb = p.sb([128, 128], BF16)
    wbdm = p.sb([128, 8], BF16)
    bval_bc = p.sb([128, D], BF16)
    bcp_bc = p.sb([128, D], BF16)
    bb_pp = p.sb([128, 1], F32)
    bd_pp = p.sb([128, 1], F32)
    ise_pp = p.sb([128, 1], F32)
    PFB = p.sb([128, 64], F32)
    s_misc = p.sem()

    @p.on("gpsimd")
    def _(g):
        n = [0]
        def nx(ins):
            n[0] += 16
            ins.then_inc(s_misc.h, 16)
            g.wait_ge(s_misc.h, n[0])
        nx(g.dma_start(out=identb[:], in_=identb_in))
        nx(g.dma_start(out=wbdm[:], in_=wbdm_in))
        nx(g.dma_start(out=bval_bc[:], in_=bval_in.broadcast_to((128, D))))
        nx(g.dma_start(out=bcp_bc[:], in_=bcp_in.broadcast_to((128, D))))
        nx(g.dma_start(out=bb_pp[:], in_=bbond_in.broadcast_to((128, 1))))
        nx(g.dma_start(out=bd_pp[:], in_=bdmg_in.broadcast_to((128, 1))))
        nx(g.dma_start(out=ise_pp[:], in_=iseven_in.broadcast_to((128, 1))))
        if DBG_B0_SKIP:
            pass
        else:
            nx(g.dma_start(out=pfb_scr[16:32, :].rearrange("a b -> b a"),
                           in_=bfu_in[0:1, 16:32].broadcast_to((64, 16)),
                           allow_slow_non_contiguous=True))
        if not DBG_B0_SKIP:
            nx(g.dma_start(out=pfb_scr[48:64, :].rearrange("a b -> b a"),
                           in_=bfu_in[0:1, 16:32].broadcast_to((64, 16)),
                           allow_slow_non_contiguous=True))
        if not DBG_B0_SKIP:
            nx(g.dma_start(out=pfb_scr[80:96, :].rearrange("a b -> b a"),
                           in_=bfu_in[0:1, 16:32].broadcast_to((64, 16)),
                           allow_slow_non_contiguous=True))
        if not DBG_B0_SKIP:
            nx(g.dma_start(out=pfb_scr[112:128, :].rearrange("a b -> b a"),
                           in_=bfu_in[0:1, 16:32].broadcast_to((64, 16)),
                           allow_slow_non_contiguous=True))
    s_misc.add(16 * (11 - (4 if DBG_B0_SKIP else 0)))
    v_misc = s_misc.v

    sc0 = p.scope()
    rpe_sb = p.sb_in(sc0, [128, 64], F32)
    wpos_sb = p.sb_in(sc0, [16, 16], F32)
    bfub = p.sb_in(sc0, [16, 1], F32)
    pfev = p.sb_in(sc0, [128, 64], F32)
    pftp = p.sb_in(sc0, [16, 64], F32)
    identf0 = p.sb_in(sc0, [128, 128], F32)
    pp0 = p.ps_in(sc0, [128, 512], F32)
    pp1f = p.ps_in(sc0, [128, 512], F32)
    pp1 = pp1f[0:16, 0:128]
    s_p0 = p.sem()

    @p.on("sync")
    def _(sy):
        sy.dma_start(out=rpe_sb[0:16, :], in_=rpeT_in).then_inc(s_p0.h, 16)
        sy.wait_ge(s_p0.h, 16)
        sy.dma_start(out=wpos_sb[:], in_=wpos_in).then_inc(s_p0.h, 16)
        sy.wait_ge(s_p0.h, 32)
        sy.dma_start(out=bfub[:], in_=bfu_in[0:1, 0:16]).then_inc(s_p0.h, 16)
    s_p0.add(48)

    @p.on("gpsimd")
    def _(g):
        g.wait_ge(s_misc.h, v_misc)
        g.memset(identf0[:], 1.0).then_inc(gfl.h, 1)
    vg2 = gfl.add()

    s_idf = p.sem()

    @p.on("gpsimd")
    def _(g, vg2=vg2):
        g.wait_ge(gfl.h, vg2)
        g.affine_select(out=identf0[:], in_=identf0[:], pattern=[[-1, 128]],
                        compare_op=ALU.is_equal, fill=0.0, base=0,
                        channel_multiplier=1).then_inc(s_idf.h, 1)
    s_idf.add(1)

    s_pf = p.sem()

    @p.on("tensor")
    def _(t):
        t.wait_ge(s_p0.h, 48)
        t.wait_ge(s_idf.h, 1)
        t.matmul(pp0[0:64, :16], rpe_sb[0:16, :], wpos_sb[:],
                 start=True, stop=True).then_inc(s_pf.h, 1)
    v1 = s_pf.add()
    c_e0 = chain(p, "vector", [(s_pf, v1)], [
        lambda v: v.tensor_copy(pfev[0:64, :16], pp0[0:64, :16])])

    @p.on("tensor")
    def _(t, c_e0=c_e0):
        t.wait_ge(c_e0[0].h, c_e0[1])
        t.transpose(pp1, pfev[:, 0:16], identf0[:]).then_inc(s_pf.h, 1)
    v2 = s_pf.add()
    c_e1 = chain(p, "vector", [(s_pf, v2)], [
        lambda v: v.tensor_copy(pftp[:], pp1f[0:16, 0:64]),
        lambda v: v.scalar_tensor_tensor(pftp[:], pftp[:], bfub[:],
                                         pftp[:], ALU.add, ALU.bypass),
    ], flush=[True, True])

    s_pfd = p.sem()

    @p.on("gpsimd")
    def _(g, c_e1=c_e1):
        g.wait_ge(c_e1[0].h, c_e1[1])
        for hh in range(4):
            if hh >= 1:
                g.wait_ge(s_pfd.h, 16 * hh)
            g.dma_start(out=pfb_scr[hh * 32:hh * 32 + 16, :], in_=pftp[:]
                        ).then_inc(s_pfd.h, 16)
        g.wait_ge(s_pfd.h, 64)
        g.dma_start(out=PFB[:], in_=pfb_scr).then_inc(s_pfd.h, 16)
    s_pfd.add(80)
    v_pfb = s_pfd.v
    sc0.close()

    G_FM = [p.sb([128, 63 + T], BF16) for _ in range(NFM)]
    VALT = [p.sb([128, D], BF16) for _ in range(NT + 1)]
    VALB = [p.sb([128, D], BF16) for _ in range(NT + 1)]
    i2_sb = p.sb([128, 256], BF16)
    scX = p.scope()
    XFM = [p.sb_in(scX, [128, T], BF16) for _ in range(ND)]
    XPFM = [p.sb_in(scX, [128, 128], BF16) for _ in range(ND)]
    sc1 = p.scope()
    x_t = [p.sb_in(sc1, [128, D], F32) for _ in range(NT)]
    xpt = p.sb_in(sc1, [128, D], F32)
    xnr = [p.sb_in(sc1, [128, D], BF16) for _ in range(2)]
    ptb_f = [p.ps_in(sc1, [128, 1024], BF16) for _ in range(2)]
    ptb = [ptb_f[0][:, 0:128], ptb_f[1][:, 0:128]]
    s_ld = emit_load_tiles(p, x_in, x_t, waits=[(s_pfd, v_pfb)])
    s_lp = p.sem()
    c_zp = chain(p, "vector", [(s_pfd, v_pfb)], [
        lambda v: v.memset(xpt[:], 0.0)])

    @p.on("sync")
    def _(sy, c_zp=c_zp):
        sy.wait_ge(c_zp[0].h, c_zp[1])
        sy.dma_start(out=xpt[65:128, :], in_=xprev_in).then_inc(s_lp.h, 16)
    s_lp.add(16)

    s_tm1 = p.sem()
    s_tp1 = p.sem()
    s_lnp = emit_ln(p, [xpt], [xpt], lambda i: [(s_lp, 16)], rms=False,
                    eps=1e-5, n_tiles=1, sc=sc1, nb=1)
    c_cvp = chain(p, "vector", [(s_lnp, 1)], [
        lambda v: v.tensor_copy(xnr[0][:], xpt[:])])
    for c in range(ND):
        @p.on("tensor")
        def _(t, c=c, c_cvp=c_cvp):
            if c == 0:
                t.wait_ge(c_cvp[0].h, c_cvp[1])
                t.wait_ge(s_misc.h, 16)
            if c >= 2:
                t.wait_ge(s_tp1.h, c - 1)
            t.transpose(ptb[c % 2][:], xnr[0][:, c * 128:(c + 1) * 128],
                        identb[:]).then_inc(s_tm1.h, 1)
        v_tm = s_tm1.add()

        @p.on("vector")
        def _(v, c=c, v_tm=v_tm):
            v.wait_ge(s_tm1.h, v_tm)
            v.tensor_copy(XPFM[c][:], ptb[c % 2][:]).then_inc(s_tp1.h, 1)
        s_tp1.add()
    base1 = s_tp1.v

    def atX(i, s_fin, fin_v):
        for c in range(ND):
            k = base1 + i * ND + c

            @p.on("tensor")
            def _(t, i=i, c=c, k=k, s_fin=s_fin, fin_v=fin_v):
                t.wait_ge(s_fin.h, fin_v)
                if k >= 2:
                    t.wait_ge(s_tp1.h, k - 1)
                t.transpose(ptb[k % 2][:], xnr[i % 2][:, c * 128:(c + 1) * 128],
                            identb[:]).then_inc(s_tm1.h, 1)
            v_tm = s_tm1.add()

            @p.on("vector")
            def _(v, i=i, c=c, k=k, v_tm=v_tm):
                v.wait_ge(s_tm1.h, v_tm)
                v.tensor_copy(XFM[c][:, i * 128:(i + 1) * 128],
                              ptb[k % 2][:]).then_inc(s_tp1.h, 1)
            s_tp1.add()

    s_ln1 = emit_ln(p, x_t, [xnr[i % 2] for i in range(NT)],
                    lambda i: [(s_ld, 16 * (i + 1))] +
                              ([(s_tp1, base1 + (i - 1) * ND)]
                               if i >= 2 else []),
                    rms=False, eps=1e-5, sc=sc1, nb=1, after_tile=atX)
    v_xfm = base1 + NT * ND
    sc1.close()

    sc2 = p.scope()
    wdisp_sb = [p.sb_in(sc2, [16, 128], F32) for _ in range(2)]
    wfu2 = p.sb_in(sc2, [16, 32], F32)
    wdf = [p.sb_in(sc2, [128, GW], BF16) for _ in range(ND)]
    wv_sb = [p.sb_in(sc2, [128, D], BF16) for _ in range(ND)]
    gtok = p.sb_in(sc2, [128, GW], BF16)
    pw2f = p.ps_in(sc2, [128, 512], F32)
    pw2 = pw2f[:, 0:32]
    pg2_ = p.ps_in(sc2, [128, GW], F32)
    pg2 = [pg2_, pg2_]
    pv2 = [p.ps_in(sc2, [128, 512], F32) for _ in range(4)]
    ptg_f = [p.ps_in(sc2, [128, 1024], BF16) for _ in range(2)]
    ptg = [ptg_f[0][:, 0:128], ptg_f[1][:, 0:128]]
    s_w2 = p.sem()
    s_wm = p.sem()
    s_wev = p.sem()
    s_gm = p.sem()
    s_gev = p.sem()
    s_gtm = p.sem()
    s_gtp = p.sem()
    s_vm = p.sem()
    s_vev = p.sem()

    @p.on("sync")
    def _(sy):
        sy.wait_ge(s_tp1.h, v_xfm)
        sy.dma_start(out=wfu2[:], in_=wfu_in).then_inc(s_w2.h, 16)
        sy.wait_ge(s_w2.h, 16)
        sy.dma_start(out=i2_sb[:], in_=i2_in).then_inc(s_w2.h, 16)
    s_w2.add(32)
    for h in range(16):
        for ic in range(ND):
            k = h * ND + ic

            @p.on("sync")
            def _(sy, h=h, ic=ic, k=k):
                if k == 0:
                    sy.wait_ge(s_tp1.h, v_xfm)
                sy.wait_ge(s_w2.h, 16 * (k + 2))
                if k >= 2:
                    sy.wait_ge(s_wm.h, k - 1)
                sy.dma_start(out=wdisp_sb[k % 2][:],
                             in_=wdispT_in[h * 16:(h + 1) * 16,
                                           ic * 128:(ic + 1) * 128]
                             ).then_inc(s_w2.h, 16)
            v_l = s_w2.add(16)
            if True:
                pass

            @p.on("tensor")
            def _(t, h=h, ic=ic, k=k, v_l=v_l):
                t.wait_ge(s_w2.h, v_l)
                if k >= 1:
                    t.wait_ge(s_wev.h, k)
                t.matmul(pw2, wdisp_sb[k % 2][:], wfu2[:],
                         start=True, stop=True).then_inc(s_wm.h, 1)
            v_m = s_wm.add()

            @p.on("vector")
            def _(v, h=h, ic=ic, v_m=v_m):
                v.wait_ge(s_wm.h, v_m)
                v.tensor_copy(wdf[ic][:, h * 32:(h + 1) * 32], pw2
                              ).then_inc(s_wev.h, 1)
            s_wev.add()
    v_wdf = s_wev.v
    s_wv = emit_weight_load_bf16(p, wvT_in, wv_sb,
                                 wait=[(s_misc, v_misc), (s_tp1, v_xfm)])

    for gi in range(NT + 1):
        @p.on("tensor")
        def _(t, gi=gi):
            if gi == 0:
                t.wait_ge(s_tp1.h, v_xfm)
                t.wait_ge(s_wev.h, v_wdf)
                t.wait_ge(s_wv.h, 16 * ND)
            if gi >= 1:
                t.wait_ge(s_gev.h, gi)
            src = XPFM if gi == 0 else XFM
            off = 0 if gi == 0 else (gi - 1) * 128
            ins = None
            for c in range(ND):
                sl = src[c][:, :] if gi == 0 else src[c][:, off:off + 128]
                ins = t.matmul(pg2[gi % 2][:], sl, wdf[c][:],
                               start=(c == 0), stop=(c == ND - 1))
            ins.then_inc(s_gm.h, 1)
        v_gm = s_gm.add()

        @p.on("vector")
        def _(v, gi=gi, v_gm=v_gm):
            v.wait_ge(s_gm.h, v_gm)
            if gi >= 2:
                v.wait_ge(s_gtm.h, (gi - 1) * NFM)
            v.tensor_copy(gtok[:], pg2[gi % 2][:]).then_inc(s_gev.h, 1)
        v_ge = s_gev.add()

        for fc in range(NFM):
            k = gi * NFM + fc

            @p.on("tensor")
            def _(t, gi=gi, fc=fc, k=k, v_ge=v_ge):
                t.wait_ge(s_gev.h, v_ge)
                if k >= 2:
                    t.wait_ge(s_gtp.h, k - 1)
                t.transpose(ptg[k % 2], gtok[:, fc * 128:(fc + 1) * 128],
                            identb[:]).then_inc(s_gtm.h, 1)
            v_tm = s_gtm.add()

            @p.on("vector")
            def _(v, gi=gi, fc=fc, k=k, v_tm=v_tm):
                v.wait_ge(s_gtm.h, v_tm)
                if gi == 0:
                    v.tensor_copy(G_FM[fc][:, 0:63],
                                  ptg_f[k % 2][:, 65:128]).then_inc(s_gtp.h, 1)
                else:
                    v.tensor_copy(G_FM[fc][:, 63 + (gi - 1) * 128:
                                           63 + gi * 128],
                                  ptg[k % 2]).then_inc(s_gtp.h, 1)
            s_gtp.add()

        for oc in range(2):
            @p.on("tensor")
            def _(t, gi=gi, oc=oc):
                if gi == 0:
                    t.wait_ge(s_wev.h, v_wdf)
                if gi * 2 + oc >= 4:
                    t.wait_ge(s_vev.h, gi * 2 + oc - 3)
                src = XPFM if gi == 0 else XFM
                off = 0 if gi == 0 else (gi - 1) * 128
                ins = None
                for c in range(ND):
                    sl = src[c][:, :] if gi == 0 else src[c][:, off:off + 128]
                    ins = t.matmul(pv2[(gi * 2 + oc) % 4][:], sl,
                                   wv_sb[c][:, oc * 512:(oc + 1) * 512],
                                   start=(c == 0), stop=(c == ND - 1))
                ins.then_inc(s_vm.h, 1)
            v_vm = s_vm.add()

            @p.on("vector")
            def _(v, gi=gi, oc=oc, v_vm=v_vm):
                v.wait_ge(s_vm.h, v_vm)
                v.tensor_tensor(VALT[gi][:, oc * 512:(oc + 1) * 512],
                                pv2[(gi * 2 + oc) % 4][:],
                                bval_bc[:, oc * 512:(oc + 1) * 512],
                                ALU.add).then_inc(s_vev.h, 1)
            s_vev.add()
    v_gfm = s_gtp.v
    v_val = s_vev.v

    s_bm = p.sem()
    s_bev = p.sem()
    for i in range(NT + 1):
        for nc2 in range(2):
            kb = i * 2 + nc2

            @p.on("tensor")
            def _(t, i=i, nc2=nc2, kb=kb):
                if kb <= 1:
                    t.wait_ge(s_vev.h, v_val)
                if kb == 0:
                    t.wait_ge(s_w2.h, 32)
                if kb >= 4:
                    t.wait_ge(s_bev.h, kb - 3)
                ins = t.matmul(pv2[kb % 4][:], i2_sb[:, 128:256],
                               VALT[i][:, nc2 * 512:(nc2 + 1) * 512],
                               start=True, stop=(i == NT))
                if i < NT:
                    ins = t.matmul(pv2[kb % 4][:], i2_sb[:, 0:128],
                                   VALT[i + 1][:, nc2 * 512:(nc2 + 1) * 512],
                                   start=False, stop=True)
                ins.then_inc(s_bm.h, 1)
            v_bm = s_bm.add()

            @p.on("vector")
            def _(v, i=i, nc2=nc2, v_bm=v_bm):
                v.wait_ge(s_bm.h, v_bm)
                if i == NT:
                    v.tensor_copy(VALB[i][0:64, nc2 * 512:(nc2 + 1) * 512],
                                  pv2[(i * 2 + nc2) % 4][0:64, :]
                                  ).then_inc(s_bev.h, 1)
                else:
                    v.tensor_copy(VALB[i][:, nc2 * 512:(nc2 + 1) * 512],
                                  pv2[(i * 2 + nc2) % 4][:]
                                  ).then_inc(s_bev.h, 1)
            s_bev.add()
    v_vb = s_bev.v
    if DBG_B0_PHASE == 2:
        s_dy = p.sem()

        @p.on("gpsimd")
        def _(g):
            g.wait_ge(s_bev.h, v_vb)
            g.wait_ge(s_gtp.h, v_gfm)
            for i in range(NT):
                if i:
                    g.wait_ge(s_dy.h, 16 * i)
                g.dma_start(out=y_out[i * 128:(i + 1) * 128, :],
                            in_=VALT[i + 1][:]).then_inc(s_dy.h, 16)
            g.wait_ge(s_dy.h, 16 * NT)
        s_dy.add(16 * NT)
        sc2.close()
        scX.close()
        return
    sc2.close()
    scX.close()

    zfill = io.inp("zrow", [1, 256])
    sc3 = p.scope()
    dif = [p.sb_in(sc3, [128, 128], F32) for _ in range(2)]
    gel = [p.sb_in(sc3, [128, 128], BF16) for _ in range(2)]
    S_ALL = p.sb_in(sc3, [128, 1024], F32)
    DMG = p.sb_in(sc3, [128, 1024], F32)
    M0 = p.sb_in(sc3, [128, 64], F32)
    m16 = p.sb_in(sc3, [128, 16], F32)
    den16 = p.sb_in(sc3, [128, 16], F32)
    A_E = p.sb_in(sc3, [128, 1024], F32)
    A_N = p.sb_in(sc3, [128, 1024], BF16)
    asht = [p.sb_in(sc3, [128, 16 * 195], BF16) for _ in range(2)]
    at0 = [p.sb_in(sc3, [128, 128], BF16) for _ in range(2)]
    at1 = [p.sb_in(sc3, [128, 128], BF16) for _ in range(2)]
    out_t = [p.sb_in(sc3, [128, D], F32) for _ in range(2)]
    obfb = p.sb_in(sc3, [128, D], BF16)
    ofm8 = [p.sb_in(sc3, [128, 128], BF16) for _ in range(ND)]
    dum3 = p.sb_in(sc3, [1, 1], F32)
    wc_sb = [p.sb_in(sc3, [128, D], BF16) for _ in range(ND)]
    xr3 = [p.sb_in(sc3, [128, D], F32) for _ in range(2)]
    yt3 = [p.sb_in(sc3, [128, 512], F32) for _ in range(2)]
    psg = [p.ps_in(sc3, [128, 512], F32) for _ in range(2)]
    pat = [p.ps_in(sc3, [128, 1024], BF16) for _ in range(2)]
    pta = [pat[0][:, 0:128], pat[1][:, 0:128]]
    pvvf = p.ps_in(sc3, [128, 512], F32)
    pvv = [pvvf[:, 0:64], pvvf[:, 64:128]]
    ptof = p.ps_in(sc3, [128, 1024], BF16)
    pto = [ptof[:, 0:128], ptof[:, 128:256]]
    pcp = [p.ps_in(sc3, [128, 512], F32) for _ in range(2)]
    s_wc = emit_weight_load_bf16(p, wcT_in, wc_sb, wait=[(s_vev, v_val)])
    s_df = p.sem()
    s_gl = p.sem()
    s_dot = p.sem()
    s_sev = p.sem()
    s_sm = p.sem()
    s_spr = p.sem()
    s_arl = p.sem()
    s_atm = p.sem()
    s_ate = p.sem()
    s_vvm = p.sem()
    s_ovev = p.sem()
    s_otm = p.sem()
    s_otp = p.sem()
    s_cm = p.sem()
    s_cev = p.sem()
    s_xr3 = p.sem()
    s_yst = p.sem()
    s_zf = p.sem()
    s_m0 = p.sem()

    @p.on("gpsimd")
    def _(g):
        g.wait_ge(s_pfd.h, v_pfb)
        g.dma_start(out=ash[0:16 * 128 * 196].rearrange("(t q) -> t q", q=196),
                    in_=zfill[0:1, 0:196].broadcast_to((16 * 128, 196))
                    ).then_inc(s_zf.h, 16)
        g.memset(M0[:], 0.0).then_inc(gfl.h, 1)
    vgm = gfl.add()

    @p.on("gpsimd")
    def _(g, vgm=vgm):
        g.wait_ge(gfl.h, vgm)
        g.affine_select(out=M0[:], in_=M0[:], pattern=[[1, 64]],
                        compare_op=ALU.is_ge, fill=-1e30, base=-63,
                        channel_multiplier=1).then_inc(s_m0.h, 1)
    s_zf.add(16)
    s_m0.add(1)
    v_zf = s_zf.v

    NJ0 = 64
    for tt in range(NT):
        for fm in range(NFM):
            grp = tt * NFM + fm
            for j in range(NJ0):
                it = grp * NJ0 + j

                @p.on("gpsimd")
                def _(g, tt=tt, fm=fm, j=j, it=it):
                    if it == 0:
                        g.wait_ge(s_gtp.h, v_gfm)
                        g.wait_ge(s_zf.h, v_zf)
                    if it >= 2:
                        g.wait_ge(s_gl.h, it - 1)
                    g.tensor_tensor(dif[it % 2][:],
                                    G_FM[fm][:, tt * 128 + j:tt * 128 + j + 128],
                                    G_FM[fm][:, 63 + tt * 128:63 + tt * 128 + 128],
                                    ALU.subtract).then_inc(s_df.h, 1)
                v_df = s_df.add()

                @p.on("scalar")
                def _(s, j=j, it=it, v_df=v_df):
                    s.wait_ge(s_df.h, v_df)
                    if it >= 2:
                        s.wait_ge(s_dot.h, it - 1)
                    s.activation(gel[it % 2][:], dif[it % 2][:], AF.Gelu,
                                 bias=PFB[:, j:j + 1]).then_inc(s_gl.h, 1)
                v_gl = s_gl.add()

                @p.on("tensor")
                def _(t, j=j, it=it, grp=grp, v_gl=v_gl):
                    t.wait_ge(s_gl.h, v_gl)
                    if j == 0 and grp >= 2:
                        t.wait_ge(s_sev.h, 2 * (grp - 1))
                    t.matmul(psg[grp % 2][:, j * 8:j * 8 + 8],
                             gel[it % 2][:], wbdm[:],
                             start=(j == 0), stop=(j == NJ0 - 1)
                             ).then_inc(s_dot.h, 1)
                v_dt = s_dot.add()

            @p.on("vector")
            def _(v, tt=tt, fm=fm, grp=grp, v_dt=s_dot.v):
                v.wait_ge(s_dot.h, v_dt)
                if fm == 0 and tt >= 1:
                    v.wait_ge(s_sm.h, (tt - 1) * 4 + 3)
                pv = psg[grp % 2][:].rearrange("p (j g) -> p g j", g=8)
                v.tensor_copy(S_ALL[:, fm * 256:(fm + 1) * 256].rearrange(
                    "p (hh j) -> p hh j", hh=4), pv[:, 0:4, :])
                v.tensor_copy(DMG[:, fm * 256:(fm + 1) * 256].rearrange(
                    "p (hh j) -> p hh j", hh=4), pv[:, 4:8, :]
                              ).then_inc(s_sev.h, 2)
            s_sev.add(2)
        v_sc = s_sev.v

        if DBG_B0_PHASE == 4:
            @p.on("gpsimd")
            def _(g, tt=tt, v_sc=v_sc):
                g.wait_ge(s_sev.h, v_sc)
                if tt:
                    g.wait_ge(s_yst.h, 16 * tt)
                g.dma_start(out=y_out[tt * 128:(tt + 1) * 128, :],
                            in_=S_ALL[:]).then_inc(s_yst.h, 16)
                g.wait_ge(s_yst.h, 16 * (tt + 1))
            s_yst.add(16)
            @p.on("vector")
            def _(v, tt=tt):
                v.wait_ge(s_yst.h, 16 * (tt + 1))
                v.memset(dum3[:], 0.0).then_inc(s_sm.h, 4)
            s_sm.add(4)
            continue

        fl0 = fl.v

        @p.on("scalar")
        def _(s, tt=tt, v_sc=v_sc):
            s.wait_ge(s_sev.h, v_sc)
            s.activation(DMG[:], DMG[:], AF.Sigmoid, bias=bd_pp[:]
                         ).then_inc(s_sm.h, 1)
        s_sm.add()
        v_sg = s_sm.v

        @p.on("vector")
        def _(v, tt=tt, v_sg=v_sg, fl0=fl0):
            v.wait_ge(s_sm.h, v_sg)
            if tt == 0:
                v.wait_ge(s_m0.h, 1)
            v.scalar_tensor_tensor(S_ALL[:], DMG[:], p.constf(-10.0),
                                   S_ALL[:], ALU.mult, ALU.add
                                   ).then_inc(fl.h, 1)
            v.wait_ge(fl.h, fl0 + 1)
            v.scalar_tensor_tensor(S_ALL[:], S_ALL[:], bb_pp[:], S_ALL[:],
                                   ALU.add, ALU.bypass).then_inc(fl.h, 1)
            v.wait_ge(fl.h, fl0 + 2)
            if tt == 0:
                for h in range(16):
                    v.scalar_tensor_tensor(
                        S_ALL[:, h * 64:(h + 1) * 64], M0[:], ise_pp[:],
                        S_ALL[:, h * 64:(h + 1) * 64], ALU.mult, ALU.add)
            v.memset(dum3[:], 0.0).then_inc(fl.h, 1)
            v.wait_ge(fl.h, fl0 + 3)
            v.tensor_reduce(m16[:].rearrange("p (g o) -> p g o", o=1),
                            S_ALL[:].rearrange("p (g k) -> p g k", g=16),
                            AX.X, ALU.max).then_inc(fl.h, 1)
            v.wait_ge(fl.h, fl0 + 4)
            v.tensor_scalar(m16[:], m16[:], -1.0, 0.0, ALU.mult, ALU.add
                            ).then_inc(s_sm.h, 1)
        fl.add(4)
        s_sm.add()
        v_ng = s_sm.v

        @p.on("scalar")
        def _(s, tt=tt, v_ng=v_ng):
            s.wait_ge(s_sm.h, v_ng)
            ins = None
            for h in range(16):
                ins = s.activation(A_E[:, h * 64:(h + 1) * 64],
                                   S_ALL[:, h * 64:(h + 1) * 64], AF.Exp,
                                   bias=m16[:, h:h + 1],
                                   accum_out=den16[:, h:h + 1])
            ins.then_inc(s_sm.h, 1)
        s_sm.add()
        v_ex = s_sm.v

        fl0 = fl.v

        @p.on("vector")
        def _(v, tt=tt, v_ex=v_ex, fl0=fl0):
            v.wait_ge(s_sm.h, v_ex)
            if tt >= 1:
                v.wait_ge((s_yst if DBG_B0_PHASE == 3 else s_spr).h, 16 * tt)
            v.reciprocal(den16[:], den16[:]).then_inc(fl.h, 1)
            v.wait_ge(fl.h, fl0 + 1)
            ins = None
            for h in range(16):
                ins = v.scalar_tensor_tensor(A_N[:, h * 64:(h + 1) * 64],
                                             A_E[:, h * 64:(h + 1) * 64],
                                             den16[:, h:h + 1],
                                             A_E[:, h * 64:(h + 1) * 64],
                                             ALU.mult, ALU.bypass)
            ins.then_inc(s_sm.h, 1)
        fl.add(1)
        s_sm.add()
        v_an = s_sm.v

        if DBG_B0_PHASE == 3:
            @p.on("gpsimd")
            def _(g, tt=tt, v_an=v_an):
                g.wait_ge(s_sm.h, v_an)
                if tt:
                    g.wait_ge(s_yst.h, 16 * tt)
                g.dma_start(out=y_out[tt * 128:(tt + 1) * 128, :],
                            in_=A_N[:]).then_inc(s_yst.h, 16)
            s_yst.add(16)
            v_dummy_sp = s_spr.add(16)

            @p.on("gpsimd")
            def _(g, tt=tt):
                g.then_inc_dummy if False else None
                g.wait_ge(s_yst.h, 16 * (tt + 1))
            continue_marker = True
            continue
        @p.on("gpsimd")
        def _(g, tt=tt, v_an=v_an):
            g.wait_ge(s_sm.h, v_an)
            if tt >= 2:
                g.wait_ge(s_arl.h, 16 * tt)
            g.dma_start(
                out=ash[0:16 * 128 * 196].rearrange("(h tq) -> h tq", h=16
                ).rearrange("h (t q) -> t h q", q=196)[:, :, 1:65],
                in_=A_N[:].rearrange("p (h j) -> p h j", h=16)
            ).then_inc(s_spr.h, 16)
        s_spr.add(16)
        v_sp = s_spr.v

        @p.on("sync")
        def _(sy, tt=tt, v_sp=v_sp):
            sy.wait_ge(s_spr.h, v_sp)
            if tt >= 2:
                sy.wait_ge(s_atm.h, (tt - 1) * 16)
            sy.dma_start(
                out=asht[tt % 2][:],
                in_=ash[0:16 * 128 * 196].rearrange("(h tq) -> h tq", h=16
                )[:, 0:195 * 128].rearrange("h (t q) -> t h q", q=195)
            ).then_inc(s_arl.h, 16)
        s_arl.add(16)
        v_rl = s_arl.v

        for h in range(16):
            kq = tt * 16 + h

            @p.on("tensor")
            def _(t, tt=tt, h=h, kq=kq, v_rl=v_rl):
                t.wait_ge(s_arl.h, v_rl)
                if kq >= 2:
                    t.wait_ge(s_ate.h, kq - 1)
                t.transpose(pta[kq % 2],
                            asht[tt % 2][:, h * 195:h * 195 + 128],
                            identb[:])
                t.transpose(pat[kq % 2][0:67, 128:256] if False else
                            pat[kq % 2][:, 128:256][0:67, :],
                            asht[tt % 2][:, h * 195 + 128:h * 195 + 195],
                            identb[:]).then_inc(s_atm.h, 1)
            v_tm = s_atm.add()

            @p.on("vector")
            def _(v, h=h, kq=kq, v_tm=v_tm):
                v.wait_ge(s_atm.h, v_tm)
                if kq >= 2:
                    v.wait_ge(s_vvm.h, kq - 1)
                v.tensor_copy(at0[kq % 2][:], pta[kq % 2])
                v.tensor_copy(at1[kq % 2][0:67, :], pat[kq % 2][0:67, 128:256]
                              ).then_inc(s_ate.h, 1)
            v_te = s_ate.add()

            @p.on("tensor")
            def _(t, tt=tt, h=h, kq=kq, v_te=v_te):
                t.wait_ge(s_ate.h, v_te)
                if kq == 0:
                    t.wait_ge(s_bev.h, v_vb)
                if kq >= 2:
                    t.wait_ge(s_ovev.h, kq - 1)
                t.matmul(pvv[kq % 2], at0[kq % 2][:],
                         VALB[tt][:, h * 64:(h + 1) * 64],
                         start=True, stop=False)
                t.matmul(pvv[kq % 2], at1[kq % 2][0:64, :],
                         VALB[tt + 1][0:64, h * 64:(h + 1) * 64],
                         start=False, stop=True).then_inc(s_vvm.h, 1)
            v_vv = s_vvm.add()

            @p.on("vector")
            def _(v, tt=tt, h=h, kq=kq, v_vv=v_vv):
                v.wait_ge(s_vvm.h, v_vv)
                if h == 0 and tt >= 2:
                    v.wait_ge(s_otp.h, (tt - 1) * ND)
                v.tensor_copy(out_t[tt % 2][:, h * 64:(h + 1) * 64],
                              pvv[kq % 2]).then_inc(s_ovev.h, 1)
            s_ovev.add()
        v_out = s_ovev.v

        obf = out_t[tt % 2]

        @p.on("sync")
        def _(sy, tt=tt):
            if tt >= 1:
                sy.wait_ge(s_xr3.h, 16 * tt)
            if tt >= 2:
                sy.wait_ge(s_yst.h, 32 * (tt - 1))
            sy.dma_start(out=xr3[tt % 2][:], in_=x_in[tt * 128:(tt + 1) * 128, :]
                         ).then_inc(s_xr3.h, 16)
        s_xr3.add(16)
        v_xr = s_xr3.v

        fl0 = fl.v

        @p.on("vector")
        def _(v, tt=tt, v_out=v_out, fl0=fl0):
            v.wait_ge(s_ovev.h, v_out)
            if tt >= 1:
                v.wait_ge(s_otm.h, tt * ND)
            v.tensor_copy(obfb[:], out_t[tt % 2][:]).then_inc(fl.h, 1)
        fl.add(1)
        v_ob = fl.v

        for c in range(ND):
            k3 = tt * ND + c

            @p.on("tensor")
            def _(t, tt=tt, c=c, k3=k3, v_ob=v_ob):
                t.wait_ge(fl.h, v_ob)
                if k3 >= 2:
                    t.wait_ge(s_otp.h, k3 - 1)
                t.transpose(pto[k3 % 2],
                            obfb[:, c * 128:(c + 1) * 128],
                            identb[:]).then_inc(s_otm.h, 1)
            v_tm = s_otm.add()

            @p.on("vector")
            def _(v, c=c, k3=k3, v_tm=v_tm):
                v.wait_ge(s_otm.h, v_tm)
                if k3 >= 2:
                    v.wait_ge(s_cm.h, max(0, (k3 // ND - 1) * 2))
                v.tensor_copy(ofm8[c][:], pto[k3 % 2]).then_inc(s_otp.h, 1)
            s_otp.add()
        v_ofm = s_otp.v

        for oc in range(2):
            g3 = tt * 2 + oc

            @p.on("tensor")
            def _(t, tt=tt, oc=oc, g3=g3, v_ofm=v_ofm):
                t.wait_ge(s_otp.h, v_ofm)
                if g3 == 0:
                    t.wait_ge(s_wc.h, 16 * ND)
                if g3 >= 2:
                    t.wait_ge(s_cev.h, g3 - 1)
                ins = None
                for c in range(ND):
                    ins = t.matmul(pcp[g3 % 2][:], ofm8[c][:],
                                   wc_sb[c][:, oc * 512:(oc + 1) * 512],
                                   start=(c == 0), stop=(c == ND - 1))
                ins.then_inc(s_cm.h, 1)
            v_cm = s_cm.add()

            fl0 = fl.v

            @p.on("vector")
            def _(v, tt=tt, oc=oc, g3=g3, v_cm=v_cm, v_xr=v_xr, fl0=fl0):
                v.wait_ge(s_cm.h, v_cm)
                v.wait_ge(s_xr3.h, v_xr)
                if g3 >= 2:
                    v.wait_ge(s_yst.h, 16 * (g3 - 1))
                v.tensor_tensor(yt3[oc][:], pcp[g3 % 2][:],
                                bcp_bc[:, oc * 512:(oc + 1) * 512], ALU.add
                                ).then_inc(fl.h, 1)
                v.wait_ge(fl.h, fl0 + 1)
                v.tensor_tensor(yt3[oc][:], yt3[oc][:],
                                xr3[tt % 2][:, oc * 512:(oc + 1) * 512],
                                ALU.add).then_inc(s_cev.h, 1)
            fl.add(1)
            v_ce = s_cev.add()

            @p.on("gpsimd")
            def _(g, tt=tt, oc=oc, g3=g3, v_ce=v_ce):
                g.wait_ge(s_cev.h, v_ce)
                if g3 >= 1:
                    g.wait_ge(s_yst.h, 16 * g3)
                g.dma_start(out=y_out[tt * 128:(tt + 1) * 128,
                                      oc * 512:(oc + 1) * 512],
                            in_=yt3[oc][:]).then_inc(s_yst.h, 16)
            s_yst.add(16)
    sc3.close()
    fin_yst = (16 * NT) if DBG_B0_PHASE in (3, 4) else 32 * NT

    @p.on("gpsimd")
    def _(g, fin_yst=fin_yst):
        g.wait_ge(s_yst.h, fin_yst)

_CACHE = {}


def _get_prog(key, emit):
    if key not in _CACHE:
        _CACHE[key] = build_program(emit)
    return _CACHE[key]


def shard_x(x_full):
    out = []
    for c in range(NCORE):
        b, h = c // 2, c % 2
        out.append(np.ascontiguousarray(x_full[b, h * T:(h + 1) * T, :]))
    return out


def unshard_x(shards):
    x = np.empty((4, 4096, D), np.float32)
    for c in range(NCORE):
        b, h = c // 2, c % 2
        x[b, h * T:(h + 1) * T, :] = shards[c]
    return x


def run_f0(xs, W, b, lora_down, lora_up, down_aux, up_aux):
    nc, cvals = _get_prog("f0", emit_f0)
    base = {
        "cpool": cpool_array(cvals),
        "WT": np.ascontiguousarray(W.T),
        "b": np.ascontiguousarray(b.reshape(1, D)),
        "lora_down": lora_down, "lora_up": lora_up,
        "lora_downT": np.ascontiguousarray(lora_down.T),
        "lora_upT": np.ascontiguousarray(lora_up.T),
        "down_aux": down_aux,
        "up_auxT": np.ascontiguousarray(up_aux.T),
        "ident_bf": IDENT_BF,
        "ones1": np.ones((1, 128), np.float32),
    }
    in_maps = [dict(base, x=xs[c]) for c in range(NCORE)]
    res = run_bass_kernel_spmd(nc, in_maps, core_ids=list(range(NCORE)))
    return [np.asarray(r["y"]) for r in res.results]


def split_qkv_l0(W_qkv, b_qkv):
    cols = np.arange(3072).reshape(16, 3, 64)
    Wq = np.ascontiguousarray(W_qkv[:, cols[:, 0, :].reshape(-1)])
    Wk = np.ascontiguousarray(W_qkv[:, cols[:, 1, :].reshape(-1)])
    Wv = np.ascontiguousarray(W_qkv[:, cols[:, 2, :].reshape(-1)])
    bq = np.ascontiguousarray(b_qkv[cols[:, 0, :].reshape(-1)].reshape(1, -1))
    bk = np.ascontiguousarray(b_qkv[cols[:, 1, :].reshape(-1)].reshape(1, -1))
    bv = np.ascontiguousarray(b_qkv[cols[:, 2, :].reshape(-1)].reshape(1, -1))
    return Wq, Wk, Wv, bq, bk, bv


def run_attn_a(xs, variant, **w):
    nc, cvals = _get_prog(f"attn{variant}",
                          lambda nc, p, io: emit_attn_a(nc, p, io, variant))
    if variant == 0:
        Wq, Wk, Wv, bq, bk, bv = split_qkv_l0(w["W_qkv"], w["b_qkv"])
        Wo, bo = w["W_out"], w["b_out"].reshape(1, D)
    else:
        Wq, Wk, Wv, Wo = w["Wq"], w["Wk"], w["Wv"], w["Wo"]
        bq, bk, bv = (w["bq"].reshape(1, D), w["bk"].reshape(1, D),
                      w["bv"].reshape(1, D))
        bo = w["bo"].reshape(1, D)
    base = {
        "cpool": cpool_array(cvals),
        "Wq": Wq, "Wk": Wk, "Wv": Wv, "Wo": Wo,
        "bq": bq, "bk": bk, "bv": bv, "bo": bo,
        "ident_bf": IDENT_BF, "ident_f": np.eye(128, dtype=np.float32),
        "ones1": np.ones((1, 128), np.float32),
    }
    in_maps = []
    for c in range(NCORE):
        pair = c ^ 1
        in_maps.append(dict(base, x=xs[c], xp=xs[pair]))
    res = run_bass_kernel_spmd(nc, in_maps, core_ids=list(range(NCORE)))
    return [np.asarray(r["y"]) for r in res.results]


def run_f1(xs, w_gate, w_up, w_down):
    nc, cvals = _get_prog("f1", lambda nc, p, io: emit_f1(nc, p, io))
    base = {"cpool": cpool_array(cvals), "w_gate": w_gate, "w_up": w_up,
            "w_down": w_down, "ident_bf": IDENT_BF}
    in_maps = [dict(base, x=xs[c]) for c in range(NCORE)]
    res = run_bass_kernel_spmd(nc, in_maps, core_ids=list(range(NCORE)))
    return [np.asarray(r["y"]) for r in res.results]


def run_b1(xs, w):
    nc, cvals = _get_prog("b1", lambda nc, p, io: emit_b1(nc, p, io))
    base = {
        "cpool": cpool_array(cvals),
        "pnw": w["pre_norm_w"].reshape(1, D),
        "cw4": np.ascontiguousarray(w["conv_w"][:, 0, :].T),
        "cb": w["conv_b"].reshape(1, D),
        "waT": np.ascontiguousarray(w["w_a"].T),
        "wiT": np.ascontiguousarray(w["w_i"].T),
        "wvT": np.ascontiguousarray(w["w_v"].T),
        "db": w["decay_bias"].reshape(1, D),
        "woT": np.ascontiguousarray(w["out_proj_w"].T),
        "fnw": w["ffn_norm_w"].reshape(1, D),
        "wguT": np.ascontiguousarray(w["w_gate_up"].T),
        "wdnT": np.ascontiguousarray(w["w_down"].T),
        "ident_bf": IDENT_BF, "ident_f": np.eye(128, dtype=np.float32),
    }
    zeros3 = np.zeros((3, D), np.float32)
    in_maps = []
    for c in range(NCORE):
        xl3 = np.ascontiguousarray(xs[c - 1][T - 3:]) if c % 2 else zeros3
        in_maps.append(dict(base, x=xs[c], xl3=xl3,
                            is_odd=np.array([[float(c % 2)]], np.float32)))
    res = run_bass_kernel_spmd(nc, in_maps, core_ids=list(range(NCORE)))
    return [np.asarray(r["y"]) for r in res.results]


IDENT2 = None


def _ident2():
    global IDENT2
    if IDENT2 is None:
        a = np.zeros((128, 256), ml_dtypes.bfloat16)
        for q in range(128):
            a[q, q + 64] = 1.0
        IDENT2 = a
    return IDENT2


def run_b0(xs, w):
    nc, cvals = _get_prog("b0", lambda nc, p, io: emit_b0(nc, p, io))
    wbdm = np.zeros((128, 8), np.float32)
    for hh in range(4):
        wbdm[hh * 32:hh * 32 + 16, hh] = w["W_bond"][:, 0]
        wbdm[hh * 32 + 16:hh * 32 + 32, 4 + hh] = w["W_dmg"][:, 0]
    base = {
        "cpool": cpool_array(cvals),
        "wdispT": np.ascontiguousarray(w["W_disp"].T),
        "wfu": w["W_fused"],
        "bfu": w["b_fused"].reshape(1, 32),
        "rpeT": np.ascontiguousarray(w["rel_pos_emb"].T),
        "wpos": w["W_pos"],
        "wbdm": wbdm.astype(ml_dtypes.bfloat16),
        "b_bond": w["b_bond"].reshape(1, 1),
        "b_dmg": w["b_dmg"].reshape(1, 1),
        "wvT": w["W_val"],
        "b_val": w["b_val"].reshape(1, D),
        "wcT": w["W_cproj"],
        "b_cproj": w["b_cproj"].reshape(1, D),
        "ident_bf": IDENT_BF,
        "ident2": _ident2(),
        "zrow": np.zeros((1, 256), np.float32),
    }
    zeros63 = np.zeros((63, D), np.float32)
    in_maps = []
    for c in range(NCORE):
        xprev = np.ascontiguousarray(xs[c - 1][T - 63:]) if c % 2 else zeros63
        in_maps.append(dict(base, x=xs[c], xprev=xprev,
                            is_even=np.array([[float(1 - c % 2)]], np.float32)))
    res = run_bass_kernel_spmd(nc, in_maps, core_ids=list(range(NCORE)))
    return [np.asarray(r["y"]) for r in res.results]


def kernel(**inputs):
    g = inputs
    xs = shard_x(np.asarray(g["L0_attn_a_X"], np.float32))
    xs = run_attn_a(xs, 0, W_qkv=g["L0_attn_a_W_qkv"], b_qkv=g["L0_attn_a_b_qkv"],
                    W_out=g["L0_attn_a_W_out"], b_out=g["L0_attn_a_b_out"])
    xs = run_f0(xs, g["L0_ffn_W"], g["L0_ffn_b"], g["L0_ffn_lora_down"],
                g["L0_ffn_lora_up"], g["L0_ffn_down_aux"], g["L0_ffn_up_aux"])
    xs = run_b0(xs, {k[len("L0_attn_b_"):]: np.asarray(v) for k, v in g.items()
                     if k.startswith("L0_attn_b_")})
    xs = run_f0(xs, g["L0_ffn_W"], g["L0_ffn_b"], g["L0_ffn_lora_down"],
                g["L0_ffn_lora_up"], g["L0_ffn_down_aux"], g["L0_ffn_up_aux"])
    xs = run_attn_a(xs, 1, Wq=g["L1_attn_a_Wq"], bq=g["L1_attn_a_bq"],
                    Wk=g["L1_attn_a_Wk"], bk=g["L1_attn_a_bk"],
                    Wv=g["L1_attn_a_Wv"], bv=g["L1_attn_a_bv"],
                    Wo=g["L1_attn_a_Wo"], bo=g["L1_attn_a_bo"])
    xs = run_f1(xs, g["L1_ffn_w_gate"], g["L1_ffn_w_up"], g["L1_ffn_w_down"])
    xs = run_b1(xs, {k[len("L1_attn_b_"):]: np.asarray(v) for k, v in g.items()
                     if k.startswith("L1_attn_b_")})
    xs = run_f1(xs, g["L1_ffn_w_gate"], g["L1_ffn_w_up"], g["L1_ffn_w_down"])
    return unshard_x([np.asarray(s, np.float32) for s in xs])


# version 03 (final)
# speedup vs baseline: 1.0029x; 1.0029x over previous
from collections import defaultdict
from contextlib import ExitStack
import numpy as np
import ml_dtypes
import concourse.bass as bass
import concourse.mybir as mybir
from concourse.bass_utils import run_bass_kernel_spmd

F32 = mybir.dt.float32
BF16 = mybir.dt.bfloat16
I32 = mybir.dt.int32
AF = mybir.ActivationFunctionType
ALU = mybir.AluOpType
AX = mybir.AxisListType

NCORE = 8
T = 2048
NT = T // 128
D = 1024
ND = D // 128
NQC = 4


def set_dims(t):
    global T, NT, NQC
    T = t
    NT = T // 128
    NQC = max(1, T // 512)

IDENT_BF = np.eye(128, dtype=ml_dtypes.bfloat16)


class Sem:
    def __init__(self, handle, name):
        self.h = handle
        self.name = name
        self.v = 0

    def add(self, n=1):
        self.v += n
        return self.v


class Prog:
    def __init__(self, nc, es):
        self.nc = nc
        self.es = es
        self.streams = defaultdict(list)
        self._n = 0
        self.sbuf_bytes = 0
        self.cpool_vals = []
        self.cpool_sb = None
        self.s_cpool = None
        self.flu = {}
        self._dscope = None
        self.dummy = None

    def constf(self, val):
        val = float(val)
        if val not in self.cpool_vals:
            assert len(self.cpool_vals) < 32
            self.cpool_vals.append(val)
        k = self.cpool_vals.index(val)
        return self.cpool_sb[:, k:k + 1]

    def sem(self, name=None):
        self._n += 1
        h = self.es.enter_context(self.nc.semaphore(name or f"s{self._n}"))
        return Sem(h, name or f"s{self._n}")

    def sb(self, shape, dtype, name=None):
        self._n += 1
        self.sbuf_bytes += int(np.prod(shape[1:])) * mybir.dt.size(dtype)
        tgt = self._dscope if self._dscope is not None else self.es
        return tgt.enter_context(
            self.nc.sbuf_tensor(name or f"t{self._n}", list(shape), dtype))

    def ps(self, shape, dtype=F32, name=None):
        self._n += 1
        tgt = self._dscope if self._dscope is not None else self.es
        return tgt.enter_context(
            self.nc.psum_tensor(name or f"p{self._n}", list(shape), dtype))

    def scope(self):
        return ExitStack()

    def sb_in(self, scope, shape, dtype, name=None):
        self._n += 1
        return scope.enter_context(
            self.nc.sbuf_tensor(name or f"t{self._n}", list(shape), dtype))

    def ps_in(self, scope, shape, dtype=F32, name=None):
        self._n += 1
        return scope.enter_context(
            self.nc.psum_tensor(name or f"p{self._n}", list(shape), dtype))

    def on(self, eng):
        def deco(fn):
            self.streams[eng].append(fn)
        return deco

    def finalize(self, block):
        for eng in ("sync", "vector", "scalar", "gpsimd", "tensor"):
            fns = list(self.streams[eng])
            def body(e, fns=fns):
                for fn in fns:
                    fn(e)
            getattr(block, eng)(body)


def chain(p, eng, waits, steps, flush="auto"):
    n = len(steps)
    if flush == "all":
        fl = [True] * n
    elif flush == "auto":
        fl = [True] * n
    else:
        fl = list(flush) + [True]
        fl = fl[:n - 1] + [True]
    sem = p.flu[eng]
    base = sem.v
    incs = [i for i in range(n) if fl[i] or i == n - 1]
    cum = {}
    c = base
    for i in range(n):
        if i in incs or fl[i]:
            c += 1
            cum[i] = c
    sem.v = c

    @p.on(eng)
    def _(e, steps=steps, waits=waits, fl=fl, cum=cum, n=n):
        for sm, val in waits:
            e.wait_ge(sm.h, val)
        for i, fn in enumerate(steps):
            ins = fn(e)
            if i in cum:
                ins.then_inc(sem.h, 1)
                if fl[i] and i < n - 1:
                    e.wait_ge(sem.h, cum[i])
    return (sem, c)


class IO:
    def __init__(self, nc, prefix="", redirect=None):
        self.nc = nc
        self.prefix = prefix
        self.redirect = redirect or {}

    def inp(self, name, shape, dtype=F32):
        if name in self.redirect:
            return self.redirect[name]
        return self.nc.dram_tensor(self.prefix + name, list(shape), dtype,
                                   kind="ExternalInput").ap()

    def out(self, name, shape, dtype=F32):
        if name in self.redirect:
            return self.redirect[name]
        return self.nc.dram_tensor(self.prefix + name, list(shape), dtype,
                                   kind="ExternalOutput").ap()

    def internal(self, name, shape, dtype=F32):
        return self.nc.dram_tensor(self.prefix + name, list(shape), dtype).ap()


def build_program(emit, debug=False):
    nc = bass.Bass("TRN2", target_bir_lowering=False, debug=debug)
    es = ExitStack()
    p = Prog(nc, es)
    cpool_dram = nc.dram_tensor("cpool", [128, 32], F32, kind="ExternalInput").ap()
    with es:
        block = es.enter_context(nc.Block())
        p.cpool_sb = p.sb([128, 32], F32, "cpool_sb")
        p.s_cpool = p.sem("cpool")
        for eng in ("vector", "scalar", "gpsimd", "tensor"):
            p.flu[eng] = p.sem("flu_" + eng)

        @p.on("sync")
        def _(sy):
            sy.dma_start(out=p.cpool_sb[:], in_=cpool_dram).then_inc(p.s_cpool.h, 16)
        p.s_cpool.add(16)
        emit(nc, p, IO(nc))
        p.finalize(block)
    return nc, list(p.cpool_vals)


def stage_barrier(p):
    s_b = p.sem()

    @p.on("gpsimd")
    def _(g, s_b=s_b):
        g.memset(p.dummy[:], 0.0).then_inc(s_b.h, 1)
    for eng in ("sync", "vector", "scalar", "tensor"):
        @p.on(eng)
        def _(e, s_b=s_b):
            e.wait_ge(s_b.h, 1)
    s_b.add(1)


def build_program_pair(emit_a, emit_b):
    nc = bass.Bass("TRN2", target_bir_lowering=False, debug=False)
    es = ExitStack()
    p = Prog(nc, es)
    cpool_dram = nc.dram_tensor("cpool", [128, 32], F32, kind="ExternalInput").ap()
    with es:
        block = es.enter_context(nc.Block())
        p.cpool_sb = p.sb([128, 32], F32, "cpool_sb")
        p.dummy = p.sb([1, 1], F32, "dummy_sb")
        p.s_cpool = p.sem("cpool")
        for eng in ("vector", "scalar", "gpsimd", "tensor"):
            p.flu[eng] = p.sem("flu_" + eng)

        @p.on("sync")
        def _(sy):
            sy.dma_start(out=p.cpool_sb[:], in_=cpool_dram).then_inc(p.s_cpool.h, 16)
        p.s_cpool.add(16)
        mid = nc.dram_tensor("mid_x", [T, D], F32).ap()
        sc_a = p.scope()
        p._dscope = sc_a
        emit_a(nc, p, IO(nc, prefix="a_", redirect={"y": mid}))
        p._dscope = None
        stage_barrier(p)
        sc_a.close()
        sc_b = p.scope()
        p._dscope = sc_b
        emit_b(nc, p, IO(nc, prefix="b_", redirect={"x": mid}))
        p._dscope = None
        sc_b.close()
        p.finalize(block)
    return nc, list(p.cpool_vals)


def cpool_array(vals):
    a = np.zeros((128, 32), np.float32)
    for i, v in enumerate(vals):
        a[:, i] = v
    return a



def emit_load_tiles(p, dram, tiles, rows=128, queue="sync", waits=()):
    s = p.sem()
    @p.on(queue)
    def _(e):
        for sm, val in waits:
            e.wait_ge(sm.h, val)
        for i in range(len(tiles)):
            if i >= 1:
                e.wait_ge(s.h, 16 * i)
            e.dma_start(out=tiles[i][:], in_=dram[i * rows:(i + 1) * rows, :]
                        ).then_inc(s.h, 16)
    s.add(16 * len(tiles))
    return s


def emit_ln(p, x_tiles, xn_tiles, wait_fn, rms=False, eps=1e-5, n_tiles=None,
            width=D, sc=None, after_tile=None, nb=2):
    NB = nb
    if n_tiles is None:
        n_tiles = NT
    mk = (lambda shape, dt: p.sb_in(sc, shape, dt)) if sc is not None else \
         (lambda shape, dt: p.sb(shape, dt))
    st = [{k: mk([128, 1], F32) for k in ("m", "s2", "mm", "sd", "rstd")}
          | {"x2": mk([128, width], F32)} for _ in range(NB)]
    s_sq = p.sem()
    s_var = p.sem()
    s_sd = p.sem()
    s_fin = p.sem()
    s_flush = p.sem()
    eps_ap = p.constf(eps)
    for i in range(n_tiles):
        b = st[i % NB]
        waits = wait_fn(i)

        @p.on("scalar")
        def _(s, i=i, b=b, waits=waits):
            for sm, val in waits:
                s.wait_ge(sm.h, val)
            if i >= NB:
                s.wait_ge(s_fin.h, i - NB + 1)
            s.activation(b["x2"][:], x_tiles[i][:], AF.Square).then_inc(s_sq.h, 1)
        v_sq = s_sq.add()

        fl = s_flush.v

        @p.on("vector")
        def _(v, i=i, b=b, waits=waits, v_sq=v_sq, fl=fl):
            for sm, val in waits:
                v.wait_ge(sm.h, val)
            if i >= NB:
                v.wait_ge(s_fin.h, i - NB + 1)
            k = fl
            if not rms:
                v.tensor_reduce(b["m"][:], x_tiles[i][:], AX.X, ALU.add
                                ).then_inc(s_flush.h, 1); k += 1
                v.wait_ge(s_flush.h, k)
                v.tensor_scalar(b["m"][:], b["m"][:], 1.0 / width, 0.0,
                                ALU.mult, ALU.add).then_inc(s_flush.h, 1); k += 1
                v.wait_ge(s_flush.h, k)
            v.wait_ge(s_sq.h, v_sq)
            v.tensor_reduce(b["s2"][:], b["x2"][:], AX.X, ALU.add
                            ).then_inc(s_flush.h, 1); k += 1
            v.wait_ge(s_flush.h, k)
            if not rms:
                v.tensor_mul(b["mm"][:], b["m"][:], b["m"][:]
                             ).then_inc(s_flush.h, 1); k += 1
                v.wait_ge(s_flush.h, k)
                v.scalar_tensor_tensor(b["s2"][:], b["s2"][:], 1.0 / width,
                                       b["mm"][:], ALU.mult, ALU.subtract
                                       ).then_inc(s_var.h, 1)
            else:
                v.tensor_scalar(b["s2"][:], b["s2"][:], 1.0 / width, 0.0,
                                ALU.mult, ALU.add).then_inc(s_var.h, 1)
        s_flush.add(4 if not rms else 1)
        v_var = s_var.add()

        @p.on("scalar")
        def _(s, b=b, v_var=v_var, eps_ap=eps_ap):
            s.wait_ge(s_var.h, v_var)
            s.wait_ge(p.s_cpool.h, 16)
            s.activation(b["sd"][:], b["s2"][:], AF.Sqrt, bias=eps_ap
                         ).then_inc(s_sd.h, 1)
        v_sd = s_sd.add()

        fl2 = s_flush.v

        @p.on("vector")
        def _(v, i=i, b=b, v_sd=v_sd, fl2=fl2):
            v.wait_ge(s_sd.h, v_sd)
            v.reciprocal(b["rstd"][:], b["sd"][:]).then_inc(s_flush.h, 1)
            v.wait_ge(s_flush.h, fl2 + 1)
            v.scalar_tensor_tensor(
                xn_tiles[i][:], x_tiles[i][:],
                (b["m"][:] if not rms else 0.0),
                b["rstd"][:].broadcast_to((128, width)),
                ALU.subtract if not rms else ALU.add,
                ALU.mult).then_inc(s_fin.h, 1)
        s_flush.add(1)
        s_fin.add()
        if after_tile is not None:
            after_tile(i, s_fin, s_fin.v)
    return s_fin


def emit_transpose_fm(p, src_tiles, dst_fm, ident_bf, wait_fn, n_tiles=None,
                      ncols=ND, col_off=0, pt=None, sems=None):
    if n_tiles is None:
        n_tiles = NT
    if pt is None:
        pt = [p.ps([128, 128], BF16) for _ in range(2)]
    s_mm, s_ev = sems if sems else (p.sem(), p.sem())
    for i in range(n_tiles):
        for c in range(ncols):
            k = i * ncols + c

            @p.on("tensor")
            def _(t, i=i, c=c, k=k):
                for sm, val in wait_fn(i):
                    t.wait_ge(sm.h, val)
                if k >= 2:
                    t.wait_ge(s_ev.h, k - 1)
                t.transpose(pt[k % 2][:], src_tiles[i][:, c * 128:(c + 1) * 128],
                            ident_bf[:]).then_inc(s_mm.h, 1)
            v_mm = s_mm.add()

            @p.on("vector")
            def _(v, i=i, c=c, v_mm=v_mm, k=k):
                v.wait_ge(s_mm.h, v_mm)
                v.tensor_copy(dst_fm[c][:, (col_off + i) * 128:(col_off + i + 1) * 128],
                              pt[k % 2][:]).then_inc(s_ev.h, 1)
            s_ev.add()
    return s_ev


def emit_weight_load_bf16(p, w_dram, w_tiles, wait=(), queue="gpsimd"):
    s = p.sem()
    @p.on(queue)
    def _(g):
        for sm, val in wait:
            g.wait_ge(sm.h, val)
        for i in range(len(w_tiles)):
            g.dma_start(out=w_tiles[i][:], in_=w_dram[i * 128:(i + 1) * 128, :]
                        ).then_inc(s.h, 16)
    s.add(16 * len(w_tiles))
    return s


def emit_bias_bcast(p, brow_sb, ncols, ones_f32, dst, s_ready, pb=None):
    nchunk = (ncols + 511) // 512
    if pb is None:
        pb = p.ps([128, 512], F32)
    s_mm = p.sem()
    s_ev = p.sem()
    for j in range(nchunk):
        w = min(512, ncols - j * 512)

        @p.on("tensor")
        def _(t, j=j, w=w):
            for sm, val in s_ready:
                t.wait_ge(sm.h, val)
            if j > 0:
                t.wait_ge(s_ev.h, j)
            t.matmul(pb[:, 0:w], ones_f32[:], brow_sb[:, j * 512:j * 512 + w],
                     start=True, stop=True).then_inc(s_mm.h, 1)
        v_mm = s_mm.add()

        @p.on("vector")
        def _(v, j=j, w=w, v_mm=v_mm):
            v.wait_ge(s_mm.h, v_mm)
            v.tensor_copy(dst[:, j * 512:j * 512 + w], pb[:, 0:w]).then_inc(s_ev.h, 1)
        s_ev.add()
    return s_ev



DEBUG_F0 = False

def emit_f0(nc, p, io):
    x_in = io.inp("x", [T, D])
    if DEBUG_F0:
        dbg_xn = io.out("dbg_xn", [128, D], BF16)
        dbg_wct = io.out("dbg_wct", [128, D], BF16)
        dbg_bias = io.out("dbg_bias", [128, D])
        dbg_xnfm = io.out("dbg_xnfm", [128, 128], BF16)
    wt = io.inp("WT", [D, D])
    b = io.inp("b", [1, D])
    ld_ = io.inp("lora_down", [4, 100])
    lu_ = io.inp("lora_up", [50, 4])
    ldT = io.inp("lora_downT", [100, 4])
    luT = io.inp("lora_upT", [4, 50])
    da = io.inp("down_aux", [100, D])
    uaT = io.inp("up_auxT", [50, D])
    ident = io.inp("ident_bf", [128, 128], BF16)
    ones1 = io.inp("ones1", [1, 128])
    x_out = io.out("y", [T, D])

    x_tiles = [p.sb([128, D], F32) for _ in range(NT)]
    xn_tiles = [p.sb([128, D], BF16) for _ in range(NT)]
    xn_fm = [p.sb([128, T], BF16) for _ in range(ND)]
    wct = [p.sb([128, D], BF16) for _ in range(ND)]
    ident_sb = p.sb([128, 128], BF16)
    ones_sb = p.sb([128, 128], F32)
    brow = p.sb([1, D], F32)
    bias_bc = p.sb([128, D], F32)

    s_ld = emit_load_tiles(p, x_in, x_tiles)
    s_misc = p.sem()

    @p.on("sync")
    def _(sy):
        sy.dma_start(out=ident_sb[:], in_=ident).then_inc(s_misc.h, 16)
        sy.dma_start(out=ones_sb[0:1, :], in_=ones1).then_inc(s_misc.h, 16)
        sy.dma_start(out=brow[:], in_=b).then_inc(s_misc.h, 16)
    s_misc.add(48)

    lsb = {
        "ldT": p.sb([128, 4], F32), "luT": p.sb([128, 50], F32),
        "da": p.sb([128, D], F32), "uaT": p.sb([128, D], F32),
        "t1": p.sb([128, D], F32), "t2": p.sb([128, D], F32),
    }
    s_lw = p.sem()

    @p.on("sync")
    def _(sy):
        sy.dma_start(out=lsb["ldT"][0:100, :], in_=ldT).then_inc(s_lw.h, 16)
        sy.dma_start(out=lsb["luT"][0:4, :], in_=luT).then_inc(s_lw.h, 16)
        sy.dma_start(out=lsb["da"][0:100, :], in_=da).then_inc(s_lw.h, 16)
        sy.dma_start(out=lsb["uaT"][0:50, :], in_=uaT).then_inc(s_lw.h, 16)
    s_lw.add(64)
    pl = p.ps([128, 512], F32)
    s_lmm = p.sem()
    s_lev = p.sem()
    for j in range(2):
        @p.on("tensor")
        def _(t, j=j):
            t.wait_ge(s_lw.h, 64)
            if j > 0:
                t.wait_ge(s_lev.h, j)
            t.matmul(pl[0:4, :], lsb["ldT"][0:100, :],
                     lsb["da"][0:100, j * 512:(j + 1) * 512],
                     start=True, stop=True).then_inc(s_lmm.h, 1)
        vm = s_lmm.add()

        @p.on("vector")
        def _(v, j=j, vm=vm):
            v.wait_ge(s_lmm.h, vm)
            v.tensor_copy(lsb["t1"][0:4, j * 512:(j + 1) * 512],
                          pl[0:4, :]).then_inc(s_lev.h, 1)
        s_lev.add()
    for j in range(2):
        @p.on("tensor")
        def _(t, j=j):
            t.wait_ge(s_lev.h, 2 + j)
            t.matmul(pl[0:50, :], lsb["luT"][0:4, :],
                     lsb["t1"][0:4, j * 512:(j + 1) * 512],
                     start=True, stop=True).then_inc(s_lmm.h, 1)
        vm = s_lmm.add()

        @p.on("vector")
        def _(v, j=j, vm=vm):
            v.wait_ge(s_lmm.h, vm)
            v.tensor_copy(lsb["t2"][0:50, j * 512:(j + 1) * 512],
                          pl[0:50, :]).then_inc(s_lev.h, 1)
        s_lev.add()
    wtf = p.sb([128, D], F32)
    s_wt = p.sem()
    plw = [p.ps([128, 512], F32) for _ in range(2)]
    s_wmm = p.sem()
    s_wev = p.sem()
    for c in range(ND):
        @p.on("sync")
        def _(sy, c=c):
            if c > 0:
                sy.wait_ge(s_wev.h, 2 * c)
            sy.dma_start(out=wtf[:], in_=wt[c * 128:(c + 1) * 128, :]
                         ).then_inc(s_wt.h, 16)
        s_wt.add(16)
        for j in range(2):
            @p.on("tensor")
            def _(t, c=c, j=j):
                t.wait_ge(s_lev.h, 4)
                if c * 2 + j >= 2:
                    t.wait_ge(s_wev.h, c * 2 + j - 1)
                t.matmul(plw[j % 2][:], lsb["t2"][0:50, c * 128:(c + 1) * 128],
                         lsb["uaT"][0:50, j * 512:(j + 1) * 512],
                         start=True, stop=True).then_inc(s_wmm.h, 1)
            vm = s_wmm.add()

            @p.on("vector")
            def _(v, c=c, j=j, vm=vm):
                v.wait_ge(s_wmm.h, vm)
                v.wait_ge(s_wt.h, 16 * (c + 1))
                v.tensor_copy(wct[c][:, j * 512:(j + 1) * 512], plw[j % 2][:])
                v.tensor_tensor(wct[c][:, j * 512:(j + 1) * 512],
                                wct[c][:, j * 512:(j + 1) * 512],
                                wtf[:, j * 512:(j + 1) * 512],
                                ALU.add).then_inc(s_wev.h, 1)
            s_wev.add()
    s_wdone_v = s_wev.v

    s_bias = emit_bias_bcast(p, brow, D, ones_sb[0:1, 0:128], bias_bc,
                             [(s_misc, 48)])

    s_ln = emit_ln(p, x_tiles, xn_tiles,
                   lambda i: [(s_ld, 16 * (i + 1))])
    s_tp = emit_transpose_fm(p, xn_tiles, xn_fm, ident_sb,
                             lambda i: [(s_ln, i + 1), (s_misc, 16)])

    pmm = [p.ps([128, 512], F32) for _ in range(2)]
    s_mm = p.sem()
    s_ev = p.sem()
    s_st = p.sem()
    yt = [p.sb([128, D], F32) for _ in range(2)]
    for i in range(NT):
        for j in range(2):
            @p.on("tensor")
            def _(t, i=i, j=j):
                t.wait_ge(s_tp.h, (i + 1) * ND)
                t.wait_ge(s_wev.h, s_wdone_v)
                k = i * 2 + j
                if k >= 2:
                    t.wait_ge(s_ev.h, k - 1)
                ins = None
                for c in range(ND):
                    ins = t.matmul(pmm[j][:], xn_fm[c][:, i * 128:(i + 1) * 128],
                                   wct[c][:, j * 512:(j + 1) * 512],
                                   start=(c == 0), stop=(c == ND - 1))
                ins.then_inc(s_mm.h, 1)
            vm = s_mm.add()

            @p.on("vector")
            def _(v, i=i, j=j, vm=vm):
                v.wait_ge(s_mm.h, vm)
                v.wait_ge(s_bias.h, 2)
                if i >= 1:
                    v.wait_ge(s_st.h, 32 * i)
                v.tensor_tensor(yt[j][:, j * 512:(j + 1) * 512], pmm[j][:],
                                bias_bc[:, j * 512:(j + 1) * 512], ALU.add
                                ).then_inc(s_ev.h, 1)
            s_ev.add()

        gfl = p.flu["gpsimd"].add(2)

        @p.on("gpsimd")
        def _(g, i=i, gfl=gfl):
            g.wait_ge(s_ev.h, (i + 1) * 2)
            for j in range(2):
                g.tensor_tensor(yt[j][:, j * 512:(j + 1) * 512],
                                yt[j][:, j * 512:(j + 1) * 512],
                                x_tiles[i][:, j * 512:(j + 1) * 512], ALU.add
                                ).then_inc(p.flu["gpsimd"].h, 1)
            g.wait_ge(p.flu["gpsimd"].h, gfl)
            g.dma_start(out=x_out[i * 128:(i + 1) * 128, 0:512],
                        in_=yt[0][:, 0:512]).then_inc(s_st.h, 16)
            g.dma_start(out=x_out[i * 128:(i + 1) * 128, 512:1024],
                        in_=yt[1][:, 512:1024]).then_inc(s_st.h, 16)
            g.wait_ge(s_st.h, 32 * (i + 1))
        s_st.add(32)

    if DEBUG_F0:
        @p.on("gpsimd")
        def _(g):
            g.wait_ge(s_ev.h, 2)
            g.dma_start(out=dbg_xn, in_=xn_tiles[0][:]).then_inc(s_st.h, 16)
            g.dma_start(out=dbg_wct, in_=wct[0][:]).then_inc(s_st.h, 16)
            g.dma_start(out=dbg_bias, in_=bias_bc[:]).then_inc(s_st.h, 16)
            g.dma_start(out=dbg_xnfm, in_=xn_fm[0][:, 0:128]).then_inc(s_st.h, 16)
        s_st.add(64)

    @p.on("gpsimd")
    def _(g):
        g.wait_ge(s_st.h, s_st.v)




def emit_attn_a(nc, p, io, variant):
    x_in = io.inp("x", [T, D])
    xp_in = io.inp("xp", [T, D])
    w_d = {k: io.inp("W" + k, [D, D]) for k in ("q", "k", "v", "o")}
    b_d = {k: io.inp("b" + k, [1, D]) for k in ("q", "k", "v", "o")}
    ident_bf_d = io.inp("ident_bf", [128, 128], BF16)
    ident_f_d = io.inp("ident_f", [128, 128])
    ones1_d = io.inp("ones1", [1, 128])
    y_out = io.out("y", [T, D])

    NH, DH = 16, 64
    NKT = 2 * NT
    L0 = (variant == 0)

    K_FM = [p.sb([128, 2 * T], BF16) for _ in range(ND)]
    V_AUG = [p.sb([128, NH * 65], BF16) for _ in range(NKT)]
    q_scr = io.internal("q_scr", [T, D], BF16)
    ident_bf = p.sb([128, 128], BF16)
    ident_f = p.sb([128, 128], F32) if L0 else None
    ones_f = p.sb([1, 128], F32)
    negB = p.sb([128, NH], F32)
    s_misc = p.sem()

    @p.on("sync")
    def _(sy):
        sy.dma_start(out=ident_bf[:], in_=ident_bf_d).then_inc(s_misc.h, 16)
        sy.wait_ge(s_misc.h, 16)
        if L0:
            sy.dma_start(out=ident_f[:], in_=ident_f_d).then_inc(s_misc.h, 16)
        else:
            sy.dma_start(out=ones_f[:], in_=ones1_d).then_inc(s_misc.h, 16)
        sy.wait_ge(s_misc.h, 32)
        sy.dma_start(out=ones_f[:], in_=ones1_d).then_inc(s_misc.h, 16)
    s_misc.add(48)

    c_vone = chain(p, "vector", [], [
        (lambda v, ti=ti: v.memset(V_AUG[ti][:, 64:NH * 65:65], 1.0))
        for ti in range(NKT)], flush=[False] * NKT)

    if L0:
        Q2M = p.sb([128, NH], F32)
        K2M = p.sb([128, NH], F32)
        c_m0 = chain(p, "vector", [], [
            lambda v: v.memset(Q2M[:], 0.0),
            lambda v: v.memset(K2M[:], 0.0)], flush=[False, False])
    else:
        c_m0 = None

    scAB = p.scope()
    w_sb = [p.sb_in(scAB, [128, D], BF16) for _ in range(ND)]
    bias_bc = {k: p.sb_in(scAB, [128, D], BF16) for k in ("k", "v", "q")}
    ktmp = p.sb_in(scAB, [128, D], F32)
    x2t = p.sb_in(scAB, [128, D], F32)
    kn2 = p.sb_in(scAB, [128, NH], F32)
    ksd = p.sb_in(scAB, [128, NH], F32)
    kbf = p.sb_in(scAB, [128, D], BF16)
    qst = [p.sb_in(scAB, [128, D], BF16) for _ in range(2)]
    s_qdma = p.sem()
    pps = [p.ps_in(scAB, [128, 512], F32) for _ in range(2)]
    ptp = [p.ps_in(scAB, [128, 128], BF16) for _ in range(2)]
    ptp2 = [p.ps_in(scAB, [128, 128], BF16) for _ in range(2)]
    s_bias2 = p.sem()

    @p.on("gpsimd")
    def _(g):
        for ki, key in enumerate(("k", "v", "q")):
            if ki >= 1:
                g.wait_ge(s_bias2.h, 16 * ki)
            g.dma_start(out=bias_bc[key][:],
                        in_=b_d[key].broadcast_to((128, D))
                        ).then_inc(s_bias2.h, 16)
    s_bias2.add(48)
    s_bias = {"k": (s_bias2, 16), "v": (s_bias2, 32), "q": (s_bias2, 48)}

    s_ktp = p.sem()
    n_ktp = [0]
    prev_w = [None]
    prev_fin = [[]]
    s_pmm = p.sem()
    s_pev = p.sem()
    s_pw = p.sem()

    def proj_pass(xnfm, wdram, bias_key, mode, dest, dst_off, s_tp, i0=0,
                  ntile=None, tp_base=0):
        s_w = s_pw
        w_base = s_pw.v
        s_mm = s_pmm
        s_ev = s_pev
        ev_base = s_pev.v
        pw_cap = prev_w[0]

        @p.on("gpsimd")
        def _(g, pw_cap=pw_cap, w_base=w_base):
            if pw_cap is not None:
                g.wait_ge(pw_cap[0].h, pw_cap[1])
            for c in range(ND):
                if w_base + c >= 1:
                    g.wait_ge(s_w.h, w_base + 16 * c)
                g.dma_start(out=w_sb[c][:], in_=wdram[c * 128:(c + 1) * 128, :]
                            ).then_inc(s_w.h, 16)
        s_w.add(16 * ND)
        if ntile is None:
            ntile = NT
        comp = []
        for i in range(i0, i0 + ntile):
            for jc in range(2):
                k = ev_base + 2 * (i - i0) + jc
                pf = list(prev_fin[0])

                @p.on("tensor")
                def _(t, i=i, jc=jc, k=k, w_base=w_base):
                    t.wait_ge(s_w.h, w_base + 16 * ND)
                    t.wait_ge(s_tp.h, ND * (i + 1))
                    if k >= 2:
                        t.wait_ge(s_ev.h, k - 1)
                    ins = None
                    for c in range(ND):
                        ins = t.matmul(pps[jc][:],
                                       xnfm[c][:, (i - i0) * 128:(i - i0 + 1) * 128],
                                       w_sb[c][:, jc * 512:(jc + 1) * 512],
                                       start=(c == 0), stop=(c == ND - 1))
                    ins.then_inc(s_mm.h, 1)
                v_mm = s_mm.add()

                @p.on("vector")
                def _(v, jc=jc, v_mm=v_mm, pf=pf):
                    v.wait_ge(s_mm.h, v_mm)
                    v.wait_ge(s_bias[bias_key][0].h, s_bias[bias_key][1])
                    for sm, val in pf:
                        v.wait_ge(sm.h, val)
                    v.tensor_tensor(ktmp[:, jc * 512:(jc + 1) * 512], pps[jc][:],
                                    bias_bc[bias_key][:, jc * 512:(jc + 1) * 512],
                                    ALU.add).then_inc(s_ev.h, 1)
                s_ev.add()
            v_ev = s_ev.v
            readers = []

            if mode == "maxtrack":
                c_sq = chain(p, "scalar", [(s_ev, v_ev)], [
                    lambda s: s.activation(x2t[:], ktmp[:], AF.Square)])
                readers.append(c_sq)
                acc = Q2M if dest == "Q" else K2M
                dst_t = kbf if dest == "K" else qst[i % 2]
                c_fin = chain(p, "vector", [c_sq, c_m0] +
                              ([(s_qdma, 16 * (i - 1))] if (dest == "Q" and i >= 2) else []), [
                    lambda v: v.tensor_reduce(
                        kn2[:].rearrange("p (g o) -> p g o", o=1),
                        x2t[:].rearrange("p (g x) -> p g x", g=NH),
                        AX.X, ALU.add),
                    lambda v, acc=acc: v.tensor_tensor(acc[:], acc[:], kn2[:],
                                                       ALU.max),
                    lambda v, dst_t=dst_t: v.tensor_copy(dst_t[:], ktmp[:]),
                ], flush=[True, False, True])
                readers.append(c_fin)
            elif mode == "normalize":
                c_sq = chain(p, "scalar", [(s_ev, v_ev)], [
                    lambda s: s.activation(x2t[:], ktmp[:], AF.Square)])
                c_red = chain(p, "vector", [c_sq], [
                    lambda v: v.tensor_reduce(
                        kn2[:].rearrange("p (g o) -> p g o", o=1),
                        x2t[:].rearrange("p (g x) -> p g x", g=NH),
                        AX.X, ALU.add)])
                c_sd = chain(p, "scalar", [c_red, (p.s_cpool, 16)], [
                    lambda s: s.activation(ksd[:], kn2[:], AF.Sqrt,
                                           bias=p.constf(0.0))])
                readers.append(c_sd)
                dst_t = kbf if dest == "K" else qst[i % 2]
                steps = [lambda v: v.tensor_scalar(ksd[:], ksd[:], 1.0, 1e-8,
                                                   ALU.mult, ALU.add),
                         lambda v: v.reciprocal(kn2[:], ksd[:])]
                steps += [
                    (lambda v, h=h, dst_t=dst_t: v.tensor_tensor(
                        dst_t[:, h * DH:(h + 1) * DH],
                        ktmp[:, h * DH:(h + 1) * DH],
                        kn2[:, h:h + 1].broadcast_to((128, DH)), ALU.mult))
                    for h in range(NH)]
                c_fin = chain(p, "vector", [c_sd] +
                              ([(s_qdma, 16 * (i - 1))] if (dest == "Q" and i >= 2) else []),
                              steps, flush=[True, True] + [False] * NH)
                readers.append(c_fin)
            else:
                if dest == "V":
                    steps = [
                        (lambda v, h=h, i=i: v.tensor_copy(
                            V_AUG[dst_off + i][:, h * 65:h * 65 + 64],
                            ktmp[:, h * DH:(h + 1) * DH]))
                        for h in range(NH)]
                else:
                    dst_t = kbf if dest == "K" else qst[i % 2]
                    steps = [lambda v, dst_t=dst_t: v.tensor_copy(dst_t[:],
                                                                  ktmp[:])]
                c_fin = chain(p, "vector", [(s_ev, v_ev)], steps,
                              flush=[False] * len(steps))
                readers.append(c_fin)

            if dest == "Q":
                qv = s_qdma.v

                @p.on("gpsimd")
                def _(g, i=i, c_fin=c_fin, qv=qv):
                    g.wait_ge(c_fin[0].h, c_fin[1])
                    if qv:
                        g.wait_ge(s_qdma.h, qv)
                    g.dma_start(out=q_scr[i * 128:(i + 1) * 128, :],
                                in_=qst[i % 2][:]).then_inc(s_qdma.h, 16)
                s_qdma.add(16)
                readers.append((s_qdma, s_qdma.v))
            if dest == "K":
                tb = n_ktp[0]
                for c in range(ND):
                    kk = tb + c

                    @p.on("tensor")
                    def _(t, c=c, kk=kk, c_fin=c_fin):
                        t.wait_ge(c_fin[0].h, c_fin[1])
                        t.wait_ge(s_misc.h, 16)
                        if kk >= 2:
                            t.wait_ge(s_ktp.h, kk - 1)
                        t.transpose(ptp2[kk % 2][:],
                                    kbf[:, c * 128:(c + 1) * 128],
                                    ident_bf[:]).then_inc(s_mm.h, 1)
                    v_t = s_mm.add()

                    @p.on("vector")
                    def _(v, c=c, kk=kk, v_t=v_t, i=i):
                        v.wait_ge(s_mm.h, v_t)
                        v.tensor_copy(
                            K_FM[c][:, (dst_off + i) * 128:(dst_off + i + 1) * 128],
                            ptp2[kk % 2][:]).then_inc(s_ktp.h, 1)
                    s_ktp.add()
                n_ktp[0] += ND
                readers.append((s_ktp, n_ktp[0]))
            prev_fin[0] = readers
            comp = readers
        prev_w[0] = (s_mm, s_mm.v)
        return comp

    deps_AB = []
    prev_half_ld = [None]

    def emit_half(xd, qpass, dst_off):
        sc = p.scope()
        NROT = 2
        NSUB = NT // 2
        xr = [p.sb_in(sc, [128, D], F32) for _ in range(NROT)]
        xnr = [p.sb_in(sc, [128, D], BF16) for _ in range(NROT)]
        xnfm = [p.sb_in(sc, [128, NSUB * 128], BF16) for _ in range(ND)]
        s_ld = p.sem()
        s_tmm = p.sem()
        s_tp = p.sem()
        s_ln_subs = []

        for sub in range(2):
            i0 = sub * NSUB
            pw = prev_w[0] if sub > 0 else None

            def wait_fn(j, i0=i0, sub=sub):
                i = i0 + j
                w = [(s_ld, 16 * (i + 1))]
                if i >= NROT:
                    w.append((s_tp, ND * (i - NROT + 1)))
                return w

            def after_tile(j, s_fin, fin_v, i0=i0, sub=sub, pw=pw):
                i = i0 + j
                for c in range(ND):
                    kk = i * ND + c

                    @p.on("tensor")
                    def _(t, i=i, c=c, kk=kk, j=j, fin_v=fin_v, pw=pw,
                          s_fin=s_fin):
                        t.wait_ge(s_fin.h, fin_v)
                        t.wait_ge(s_misc.h, 16)
                        if pw is not None and j == 0 and c == 0:
                            t.wait_ge(pw[0].h, pw[1])
                        if kk >= 2:
                            t.wait_ge(s_tp.h, kk - 1)
                        t.transpose(ptp[kk % 2][:],
                                    xnr[i % NROT][:, c * 128:(c + 1) * 128],
                                    ident_bf[:]).then_inc(s_tmm.h, 1)
                    v_t = s_tmm.add()

                    @p.on("vector")
                    def _(v, j=j, c=c, v_t=v_t, kk=kk):
                        v.wait_ge(s_tmm.h, v_t)
                        v.tensor_copy(xnfm[c][:, j * 128:(j + 1) * 128],
                                      ptp[kk % 2][:]).then_inc(s_tp.h, 1)
                    s_tp.add()

            s_ln = emit_ln(p, [xr[(i0 + j) % NROT] for j in range(NSUB)],
                           [xnr[(i0 + j) % NROT] for j in range(NSUB)],
                           wait_fn, sc=sc, n_tiles=NSUB, after_tile=after_tile,
                           nb=1)
            s_ln_subs.append(s_ln)
            tp_base = i0 * ND
            deps_AB.extend(proj_pass(xnfm, w_d["k"], "k",
                                     "maxtrack" if L0 else "normalize", "K",
                                     dst_off, s_tp, i0=i0, ntile=NSUB,
                                     tp_base=tp_base))
            deps_AB.extend(proj_pass(xnfm, w_d["v"], "v", "raw", "V", dst_off,
                                     s_tp, i0=i0, ntile=NSUB, tp_base=tp_base))
            if qpass:
                deps_AB.extend(proj_pass(xnfm, w_d["q"], "q",
                                         "maxtrack" if L0 else "normalize",
                                         "Q", 0, s_tp, i0=i0, ntile=NSUB,
                                         tp_base=tp_base))

        ph = prev_half_ld[0]

        @p.on("sync")
        def _(sy, ph=ph):
            for i in range(NT):
                if i == 0 and ph is not None:
                    for sm, val in ph:
                        sy.wait_ge(sm.h, val)
                if i >= 1:
                    sy.wait_ge(s_ld.h, 16 * i)
                if i >= NROT:
                    k = i - NROT
                    sy.wait_ge(s_ln_subs[k // NSUB].h, k % NSUB + 1)
                sy.dma_start(out=xr[i % NROT][:],
                             in_=xd[i * 128:(i + 1) * 128, :]).then_inc(s_ld.h, 16)
        s_ld.add(16 * NT)
        prev_half_ld[0] = ([(s_ld, 16 * NT), (s_tp, s_tp.v)] +
                           list(prev_fin[0]) + [prev_w[0]])
        sc.close()

    emit_half(xp_in, False, NT)
    emit_half(x_in, True, 0)
    deps_AB.append(c_vone)
    scAB.close()

    if L0:
        nb_scr = io.internal("nb_scr", [1, 16])
        scC = p.scope()
        mrow = p.sb_in(scC, [16, 256], F32)
        mcol = p.sb_in(scC, [16, 2], F32)
        mcsd = p.sb_in(scC, [16, 2], F32)
        nbcol = p.sb_in(scC, [16, 1], F32)
        pC = p.ps_in(scC, [128, 128], F32)
        s_cmm = p.sem()
        cw = [None, None]
        for j in range(2):
            acc = (Q2M, K2M)[j]

            @p.on("tensor")
            def _(t, j=j, acc=acc, cwp=cw[0]):
                if j == 0:
                    for sm, val in deps_AB:
                        t.wait_ge(sm.h, val)
                    t.wait_ge(s_misc.h, 32)
                else:
                    t.wait_ge(p.flu["vector"].h, cwp[1])
                t.transpose(pC[0:16, :], acc[:], ident_f[:]).then_inc(s_cmm.h, 1)
            v_c = s_cmm.add()
            cw[0] = chain(p, "vector", [(s_cmm, v_c)], [
                lambda v, j=j: v.tensor_copy(mrow[:, j * 128:(j + 1) * 128],
                                             pC[0:16, :]),
                lambda v, j=j: v.reduce_max(mcol[:, j:j + 1],
                                            mrow[:, j * 128:(j + 1) * 128],
                                            AX.X),
            ], flush=[True, True])
        c_msd = chain(p, "scalar", [cw[0], (p.s_cpool, 16)], [
            lambda s: s.activation(mcsd[:], mcol[:], AF.Sqrt,
                                   bias=p.constf(0.0)[0:16, :])])
        c_nb = chain(p, "vector", [c_msd], [
            lambda v: v.tensor_mul(nbcol[:], mcsd[:, 0:1], mcsd[:, 1:2]),
            lambda v: v.tensor_scalar(nbcol[:], nbcol[:], -0.125, 0.0,
                                      ALU.mult, ALU.add),
        ], flush=[True, True])

        s_nbd = p.sem()

        @p.on("gpsimd")
        def _(g, c_nb=c_nb):
            g.wait_ge(c_nb[0].h, c_nb[1])
            g.dma_start(out=nb_scr, in_=nbcol[:]).then_inc(s_nbd.h, 16)
            g.wait_ge(s_nbd.h, 16)
            g.dma_start(out=negB[:], in_=nb_scr.broadcast_to((128, 16))
                        ).then_inc(s_nbd.h, 16)
        s_nbd.add(32)
        c_negB = (s_nbd, 32)
        scC.close()
    else:
        c_negB = deps_AB[0]

    QCW = T // NQC
    scD = p.scope()
    Qfm = [p.sb_in(scD, [128, QCW], BF16) for _ in range(ND)]
    qre = [p.sb_in(scD, [128, D], BF16) for _ in range(2)]
    s_qre = p.sem()
    attno = [p.sb_in(scD, [64, QCW], BF16) for _ in range(NH)]
    den64 = [p.sb_in(scD, [65, QCW], F32)]
    den_all = p.sb_in(scD, [16, QCW], F32)
    rden_one = [p.sb_in(scD, [1, QCW], F32)]
    wo_h = [p.sb_in(scD, [64, 512], BF16) for _ in range(NH)]
    bias_o = p.sb_in(scD, [128, D], BF16)
    xre = p.sb_in(scD, [128, D], F32)
    ytp = [p.sb_in(scD, [128, 512], F32) for _ in range(2)]
    phi = [p.sb_in(scD, [128, 2 * QCW], BF16) for _ in range(2)]
    if not L0:
        lnd = p.sb_in(scD, [128, 2 * QCW], F32)
        u2b = [p.sb_in(scD, [128, 2 * QCW], BF16) for _ in range(2)]
    ps_big = [p.ps_in(scD, [128, 2 * QCW], F32) for _ in range(2)]
    ps_o = p.ps_in(scD, [65, QCW], F32)
    ps_y = [p.ps_in(scD, [128, 512], F32) for _ in range(2)]
    ptq = p.ps_in(scD, [128, 128], BF16)

    s_wo = p.sem()

    @p.on("gpsimd")
    def _(g):
        for sm, val in deps_AB:
            g.wait_ge(sm.h, val)
        g.dma_start(out=bias_o[:], in_=b_d["o"].broadcast_to((128, D))
                    ).then_inc(s_wo.h, 16)
    s_wo.add(16)
    s_wol = p.sem()

    s_qt = p.sem()
    s_qev = p.sem()
    s_smm = p.sem()
    s_pv = p.sem()
    s_oev = p.sem()
    s_den = p.sem()
    s_rfl = p.sem()
    s_xre = p.sem()
    s_ymm = p.sem()
    s_yev = p.sem()
    s_yst = p.sem()
    s_nb = p.sem()
    act_done = {}
    phi_done = {}

    NTT = (T // NQC) // 128
    for qc in range(NQC):
        qb = qc * NTT * ND
        for tt in range(NTT):
            ti = qc * 4 + tt

            @p.on("sync")
            def _(sy, ti=ti, tt=tt, qc=qc):
                idx = qc * NTT + tt
                if idx == 0:
                    for sm, val in (prev_half_ld[0] or []):
                        sy.wait_ge(sm.h, val)
                    for sm, val in deps_AB:
                        sy.wait_ge(sm.h, val)
                if idx >= 1:
                    sy.wait_ge(s_qre.h, 16 * idx)
                if idx >= 2:
                    sy.wait_ge(s_qt.h, (idx - 1) * ND)
                sy.dma_start(out=qre[ti % 2][:],
                             in_=q_scr[ti * 128:(ti + 1) * 128, :]
                             ).then_inc(s_qre.h, 16)
            s_qre.add(16)
            v_qre = s_qre.v
            for c in range(ND):
                kq = qb + tt * ND + c

                @p.on("tensor")
                def _(t, tt=tt, c=c, kq=kq, qc=qc, ti=ti, v_qre=v_qre):
                    if kq == 0:
                        for sm, val in deps_AB:
                            t.wait_ge(sm.h, val)
                        t.wait_ge(c_negB[0].h, c_negB[1])
                    t.wait_ge(s_qre.h, v_qre)
                    if kq >= 1:
                        t.wait_ge(s_qev.h, kq)
                    if qc > 0:
                        t.wait_ge(s_smm.h, qc * NH * NT)
                    t.transpose(ptq[:], qre[ti % 2][:, c * 128:(c + 1) * 128],
                                ident_bf[:]).then_inc(s_qt.h, 1)
                v_q = s_qt.add()

                @p.on("vector")
                def _(v, tt=tt, c=c, v_q=v_q):
                    v.wait_ge(s_qt.h, v_q)
                    v.tensor_copy(Qfm[c][:, tt * 128:(tt + 1) * 128], ptq[:]
                                  ).then_inc(s_qev.h, 1)
                s_qev.add()

        for h in range(NH):
            hb = (h % 2) * 64
            ch = h // 2
            NPR = NT
            for pr in range(NPR):
                gidx = ((qc * NH) + h) * NPR + pr

                @p.on("tensor")
                def _(t, h=h, pr=pr, qc=qc, hb=hb, ch=ch, gidx=gidx):
                    t.wait_ge(s_qev.h, (qc + 1) * NTT * ND)
                    if gidx >= 2:
                        w = act_done[gidx - 2]
                        t.wait_ge(w[0].h, w[1])
                    ins = None
                    for sub in range(2):
                        kt = 2 * pr + sub
                        ins = t.matmul(
                            ps_big[pr % 2][:, sub * QCW:(sub + 1) * QCW],
                            K_FM[ch][hb:hb + 64, kt * 128:(kt + 1) * 128],
                            Qfm[ch][hb:hb + 64, :],
                            start=True, stop=True,
                            tile_position=(hb, 0))
                    ins.then_inc(s_smm.h, 1)
                v_s = s_smm.add()

                if L0:
                    c_act = chain(p, "scalar", [(s_smm, v_s), c_negB] +
                                  ([(s_pv, gidx - 1)] if gidx >= 2 else []), [
                        lambda s, h=h, pr=pr: s.activation(
                            phi[pr % 2][:], ps_big[pr % 2][:], AF.Exp,
                            bias=negB[:, h:h + 1], scale=0.125)])
                    c_phi = c_act
                else:
                    c_act = chain(p, "scalar", [(s_smm, v_s), (p.s_cpool, 16)] +
                                  ([phi_done[gidx - 2], (s_pv, gidx - 1)]
                                   if gidx >= 2 else []), [
                        lambda s, pr=pr: s.activation(
                            lnd[:], ps_big[pr % 2][:], AF.Ln,
                            bias=p.constf(1.1), scale=-1.0),
                        lambda s, pr=pr: s.activation(
                            phi[pr % 2][:], lnd[:], AF.Exp,
                            bias=p.constf(0.0), scale=-1.0),
                        lambda s, pr=pr: s.activation(
                            u2b[pr % 2][:], ps_big[pr % 2][:], AF.Square,
                            bias=p.constf(1.0), scale=1.0),
                    ], flush=[True, False, True])
                    c_phi = chain(p, "vector", [c_act] +
                                  ([(s_pv, gidx - 1)] if gidx >= 2 else []), [
                        lambda v, pr=pr: v.tensor_tensor(
                            phi[pr % 2][:], phi[pr % 2][:], u2b[pr % 2][:],
                            ALU.mult)])
                act_done[gidx] = c_act
                phi_done[gidx] = c_phi

                @p.on("tensor")
                def _(t, h=h, pr=pr, c_phi=c_phi, qc=qc):
                    t.wait_ge(c_phi[0].h, c_phi[1])
                    if pr == 0:
                        t.wait_ge(s_oev.h, qc * NH + h)
                    ins = None
                    for sub in range(2):
                        kt = 2 * pr + sub
                        ins = t.matmul(ps_o[:],
                                       V_AUG[kt][:, h * 65:(h + 1) * 65],
                                       phi[pr % 2][:, sub * QCW:(sub + 1) * QCW],
                                       start=(kt == 0), stop=(kt == 2 * NT - 1))
                    ins.then_inc(s_pv.h, 1)
                s_pv.add()

            v_pv = s_pv.v

            @p.on("vector")
            def _(v, h=h, qc=qc, v_pv=v_pv, dguard=qc * NH + h - 1):
                v.wait_ge(s_pv.h, v_pv)
                if qc > 0:
                    v.wait_ge(s_ymm.h, qc * 2 * NTT)
                if qc * NH + h >= 1:
                    v.wait_ge(s_den.h, 16 * (dguard + 1))
                v.tensor_copy(attno[h][:], ps_o[0:64, :])
                v.tensor_copy(den64[0][64:65, :], ps_o[64:65, :]
                              ).then_inc(s_oev.h, 1)
            s_oev.add()
            v_oe = s_oev.v

            dv = s_den.v

            @p.on("gpsimd")
            def _(g, h=h, v_oe=v_oe, qc=qc, dv=dv):
                g.wait_ge(s_oev.h, v_oe)
                if dv:
                    g.wait_ge(s_den.h, dv)
                if h == 0 and qc > 0:
                    g.wait_ge(s_rfl.h, qc * 16)
                g.dma_start(out=den_all[h:h + 1, :], in_=den64[0][64:65, :]
                            ).then_inc(s_den.h, 16)
            s_den.add(16)

        rd_steps = []
        if not L0:
            rd_steps.append(lambda v: v.tensor_scalar(
                den_all[:], den_all[:], 1.0, 1e-6, ALU.mult, ALU.add))
        rd_steps.append(lambda v: v.reciprocal(den_all[:], den_all[:]))
        c_rd = chain(p, "vector", [(s_den, s_den.v)], rd_steps,
                     flush=[True] * len(rd_steps))

        nb_base = s_nb.v
        for h in range(NH):
            rv = s_rfl.v

            @p.on("gpsimd")
            def _(g, h=h, c_rd=c_rd, qc=qc, rv=rv, nb_base=nb_base):
                if h == 0:
                    g.wait_ge(c_rd[0].h, c_rd[1])
                if rv:
                    g.wait_ge(s_rfl.h, rv)
                if h >= 1:
                    g.wait_ge(s_nb.h, nb_base + 2 * h - 1)
                elif nb_base:
                    g.wait_ge(s_nb.h, nb_base - 1)
                g.dma_start(out=rden_one[0][:],
                            in_=den_all[h:h + 1, :]).then_inc(s_rfl.h, 16)
            s_rfl.add(16)
            v_rdfl = s_rfl.v

            @p.on("tensor")
            def _(t, h=h, v_rdfl=v_rdfl, qc=qc, nb_base=nb_base):
                t.wait_ge(s_rfl.h, v_rdfl)
                if h == 0 and qc == 0:
                    t.wait_ge(s_misc.h, 48)
                if h == 0 and qc > 0:
                    t.wait_ge(s_yev.h, qc * 2 * NTT)
                if h > 0:
                    t.wait_ge(s_nb.h, nb_base + 2 * h)
                t.matmul(ps_y[0][0:64, :], ones_f[:, 0:64],
                         rden_one[0][:],
                         start=True, stop=True).then_inc(s_nb.h, 1)
            v_n = s_nb.add()

            @p.on("vector")
            def _(v, h=h, v_n=v_n):
                v.wait_ge(s_nb.h, v_n)
                v.tensor_tensor(attno[h][:], attno[h][:], ps_y[0][0:64, :],
                                ALU.mult).then_inc(s_nb.h, 1)
            s_nb.add()
        v_nrm = s_nb.v

        NOG = 2 * NTT
        for oc in range(2):
            v_prev_mm = s_ymm.v

            v_prev_wol = s_wol.v

            @p.on("gpsimd")
            def _(g, qc=qc, oc=oc, v_prev_mm=v_prev_mm, v_prev_wol=v_prev_wol):
                g.wait_ge(s_wo.h, 16)
                g.wait_ge(s_ymm.h, v_prev_mm)
                for h in range(NH):
                    if v_prev_wol + 16 * h > 0:
                        g.wait_ge(s_wol.h, v_prev_wol + 16 * h)
                    g.dma_start(out=wo_h[h][:],
                                in_=w_d["o"][h * DH:(h + 1) * DH,
                                             oc * 512:(oc + 1) * 512]
                                ).then_inc(s_wol.h, 16)
            s_wol.add(16 * NH)
            v_wol = s_wol.v
            for tt in range(NTT):
                ti = qc * NTT + tt
                g_idx = qc * NOG + oc * NTT + tt

                @p.on("sync")
                def _(sy, ti=ti, g_idx=g_idx):
                    if g_idx == 0:
                        for sm, val in deps_AB:
                            sy.wait_ge(sm.h, val)
                    if g_idx >= 1:
                        sy.wait_ge(s_xre.h, 16 * g_idx)
                        sy.wait_ge(s_yst.h, 16 * g_idx)
                    sy.dma_start(out=xre[:],
                                 in_=x_in[ti * 128:(ti + 1) * 128, :]
                                 ).then_inc(s_xre.h, 16)
                s_xre.add(16)
                v_xre = s_xre.v

                @p.on("tensor")
                def _(t, tt=tt, oc=oc, v_nrm=v_nrm, g_idx=g_idx, v_wol=v_wol):
                    t.wait_ge(s_nb.h, v_nrm)
                    t.wait_ge(s_wol.h, v_wol)
                    if g_idx >= 2:
                        t.wait_ge(s_yev.h, g_idx - 1)
                    ins = None
                    for h in range(NH):
                        ins = t.matmul(ps_y[g_idx % 2][:],
                                       attno[h][:, tt * 128:(tt + 1) * 128],
                                       wo_h[h][:],
                                       start=(h == 0), stop=(h == NH - 1))
                    ins.then_inc(s_ymm.h, 1)
                v_y = s_ymm.add()

                @p.on("vector")
                def _(v, oc=oc, v_y=v_y, g_idx=g_idx):
                    v.wait_ge(s_ymm.h, v_y)
                    v.wait_ge(s_wo.h, 16)
                    if g_idx >= 2:
                        v.wait_ge(s_yst.h, 16 * (g_idx - 1))
                    v.tensor_tensor(ytp[g_idx % 2][:], ps_y[g_idx % 2][:],
                                    bias_o[:, oc * 512:(oc + 1) * 512],
                                    ALU.add).then_inc(s_yev.h, 1)
                s_yev.add()
                v_ev2 = s_yev.v

                @p.on("gpsimd")
                def _(g, ti=ti, oc=oc, v_ev2=v_ev2, v_xre=v_xre, g_idx=g_idx):
                    g.wait_ge(s_yev.h, v_ev2)
                    g.wait_ge(s_xre.h, v_xre)
                    if g_idx >= 1:
                        g.wait_ge(s_yst.h, 16 * g_idx)
                    g.tensor_tensor(ytp[g_idx % 2][:], ytp[g_idx % 2][:],
                                    xre[:, oc * 512:(oc + 1) * 512],
                                    ALU.add).then_inc(p.flu["gpsimd"].h, 1)
                vf = p.flu["gpsimd"].add()

                @p.on("gpsimd")
                def _(g, ti=ti, oc=oc, g_idx=g_idx, vf=vf):
                    g.wait_ge(p.flu["gpsimd"].h, vf)
                    g.dma_start(out=y_out[ti * 128:(ti + 1) * 128,
                                          oc * 512:(oc + 1) * 512],
                                in_=ytp[g_idx % 2][:]).then_inc(s_yst.h, 16)
                s_yst.add(16)
    scD.close()

    @p.on("gpsimd")
    def _(g):
        g.wait_ge(s_yst.h, 16 * 2 * NT)



F6 = 3072
QMAX6 = 32767.0
DEBUG_F1 = False


def emit_wdq(nc, p, io, w_dram, wT_dram, O, I, ident_sb, s_misc_v, s_misc,
             inv7_pp, c_inv, prev=None, sems=None):
    nb = I // 32
    sc = p.scope()
    wt_ = [p.sb_in(sc, [128, I], F32) for _ in range(2)]
    wab = p.sb_in(sc, [128, I], F32)
    qi = p.sb_in(sc, [128, I], I32)
    qf = p.sb_in(sc, [128, I], F32)
    wdq = p.sb_in(sc, [128, I], BF16)
    bm = p.sb_in(sc, [128, nb], F32)
    binv = p.sb_in(sc, [128, nb], F32)
    ptp = [p.ps_in(sc, [128, 128], BF16) for _ in range(2)]
    etp = [p.sb_in(sc, [128, 128], BF16) for _ in range(2)]
    if sems is None:
        sems = tuple(p.sem() for _ in range(6))
    s_ld, s_ab, s_dq, s_tm, s_te, s_st = sems
    b_ld, b_dq, b_te, b_st = s_ld.v, s_dq.v, s_te.v, s_st.v
    not_ = O // 128
    nc_ = I // 128
    for ot in range(not_):
        @p.on("sync")
        def _(sy, ot=ot):
            if ot == 0 and prev is not None:
                sy.wait_ge(prev[0].h, prev[1])
            if ot >= 1:
                sy.wait_ge(s_ld.h, b_ld + 16 * ot)
            if ot >= 2:
                sy.wait_ge(s_dq.h, b_dq + ot - 1)
            sy.dma_start(out=wt_[ot % 2][:], in_=w_dram[ot * 128:(ot + 1) * 128, :]
                         ).then_inc(s_ld.h, 16)
        s_ld.add(16)
        v_ld = s_ld.v

        @p.on("scalar")
        def _(s, ot=ot, v_ld=v_ld):
            if ot == 0 and prev is not None:
                s.wait_ge(prev[0].h, prev[1])
            s.wait_ge(s_ld.h, v_ld)
            if ot >= 1:
                s.wait_ge(s_dq.h, b_dq + ot)
            s.activation(wab[:], wt_[ot % 2][:], AF.Abs).then_inc(s_ab.h, 1)
        v_ab = s_ab.add()

        fl = p.flu["vector"]
        b0 = fl.v

        @p.on("vector")
        def _(v, ot=ot, v_ld=v_ld, b0=b0, v_ab=v_ab):
            v.wait_ge(s_ld.h, v_ld)
            v.wait_ge(s_ab.h, v_ab)
            v.wait_ge(c_inv[0].h, c_inv[1])
            if ot >= 1:
                v.wait_ge(s_te.h, b_te + ot * nc_)
            w = wt_[ot % 2]
            v.tensor_reduce(bm[:].rearrange("p (g o) -> p g o", o=1),
                            wab[:].rearrange("p (g k) -> p g k", g=nb),
                            AX.X, ALU.max).then_inc(fl.h, 1)
            v.wait_ge(fl.h, b0 + 1)
            v.tensor_tensor(bm[:], bm[:], inv7_pp[:].broadcast_to((128, nb)),
                            ALU.mult).then_inc(fl.h, 1)
            v.wait_ge(fl.h, b0 + 2)
            v.tensor_scalar(bm[:], bm[:], 1e-12, 0.0, ALU.max, ALU.add
                            ).then_inc(fl.h, 1)
            v.wait_ge(fl.h, b0 + 3)
            v.reciprocal(binv[:], bm[:]).then_inc(fl.h, 1)
            v.wait_ge(fl.h, b0 + 4)
            for b in range(nb):
                ins = v.tensor_tensor(qi[:, b * 32:(b + 1) * 32],
                                      w[:, b * 32:(b + 1) * 32],
                                      binv[:, b:b + 1].broadcast_to((128, 32)),
                                      ALU.mult)
            ins.then_inc(fl.h, 1)
            v.wait_ge(fl.h, b0 + 5)
            v.tensor_copy(qf[:], qi[:]).then_inc(fl.h, 1)
            v.wait_ge(fl.h, b0 + 6)
            for b in range(nb):
                ins = v.tensor_tensor(wdq[:, b * 32:(b + 1) * 32],
                                      qf[:, b * 32:(b + 1) * 32],
                                      bm[:, b:b + 1].broadcast_to((128, 32)),
                                      ALU.mult)
            ins.then_inc(s_dq.h, 1)
        fl.add(6)
        v_dq = s_dq.add()

        for c in range(nc_):
            k = ot * nc_ + c

            @p.on("tensor")
            def _(t, ot=ot, c=c, k=k, v_dq=v_dq):
                if k == 0 and prev is not None:
                    t.wait_ge(prev[0].h, prev[1])
                t.wait_ge(s_dq.h, v_dq)
                t.wait_ge(s_misc.h, s_misc_v)
                if k >= 2:
                    t.wait_ge(s_te.h, b_te + k - 1)
                t.transpose(ptp[k % 2][:], wdq[:, c * 128:(c + 1) * 128],
                            ident_sb[:]).then_inc(s_tm.h, 1)
            v_tm = s_tm.add()

            @p.on("vector")
            def _(v, k=k, v_tm=v_tm):
                v.wait_ge(s_tm.h, v_tm)
                if k >= 2:
                    v.wait_ge(s_st.h, b_st + 16 * (k - 1))
                v.tensor_copy(etp[k % 2][:], ptp[k % 2][:]).then_inc(s_te.h, 1)
            v_te = s_te.add()

            @p.on("gpsimd")
            def _(g, ot=ot, c=c, k=k, v_te=v_te):
                g.wait_ge(s_te.h, v_te)
                if k >= 1:
                    g.wait_ge(s_st.h, b_st + 16 * k)
                g.dma_start(out=wT_dram[c * 128:(c + 1) * 128,
                                        ot * 128:(ot + 1) * 128],
                            in_=etp[k % 2][:]).then_inc(s_st.h, 16)
            s_st.add(16)
    sc.close()
    return (s_st, s_st.v)


def emit_armax(nc, p, io, name, groups, acc_list, sq_pp_list, inv_pp_list,
               waits, invqm_pp, c_inv, arb, sems=None):
    nr = len(acc_list)
    ar_in = io.internal(name + "_in", [nr, 128])
    ar_out = io.internal(name + "_out", [nr, 128])
    if sems is None:
        sems = (p.sem(), p.sem())
    s_d, s_c = sems
    b_d = s_d.v

    @p.on("gpsimd")
    def _(g, b_d=b_d):
        for sm, val in waits:
            g.wait_ge(sm.h, val)
        for r in range(nr):
            if b_d + r >= 1:
                g.wait_ge(s_d.h, b_d + 16 * r)
            g.dma_start(out=ar_in[r:r + 1, :], in_=acc_list[r][:, 0:1]
                        ).then_inc(s_d.h, 16)
        g.wait_ge(s_d.h, b_d + 16 * nr)
        g.collective_compute(
            "AllReduce", ALU.max, replica_groups=groups,
            ins=[ar_in.opt()], outs=[ar_out.opt()],
        ).then_inc(s_c.h, 1)
    s_d.add(16 * nr)
    s_c.add(1)
    last = None
    for r in range(nr):
        dv = s_d.v
        cv = s_c.v

        @p.on("gpsimd")
        def _(g, r=r, dv=dv, cv=cv, last=last):
            g.wait_ge(s_c.h, cv)
            if dv:
                g.wait_ge(s_d.h, dv)
            if last is not None:
                g.wait_ge(last[0].h, last[1])
            g.dma_start(out=arb[:], in_=ar_out[r:r + 1, :].broadcast_to((128, 128))
                        ).then_inc(s_d.h, 16)
        v_b = s_d.add(16)
        sq = sq_pp_list[r]
        iq = inv_pp_list[r]
        last = chain(p, "vector", [(s_d, v_b), c_inv], [
            lambda v, sq=sq: v.tensor_reduce(sq[:], arb[:], AX.X, ALU.max),
            lambda v, sq=sq: v.tensor_tensor(sq[:], sq[:], invqm_pp[:],
                                             ALU.mult),
            lambda v, sq=sq: v.tensor_scalar(sq[:], sq[:], 1e-12, 0.0,
                                             ALU.max, ALU.add),
            lambda v, sq=sq, iq=iq: v.reciprocal(iq[:], sq[:]),
        ], flush=[True, True, True, True])
    return last


def emit_f1(nc, p, io, ncores=None):
    if ncores is None:
        ncores = NCORE
    groups = [list(range(ncores))]
    x_in = io.inp("x", [T, D])
    wg_d = io.inp("w_gate", [F6, D])
    wu_d = io.inp("w_up", [F6, D])
    wd_d = io.inp("w_down", [D, F6])
    ident_bf_d = io.inp("ident_bf", [128, 128], BF16)
    y_out = io.out("y", [T, D])

    if DEBUG_F1:
        dbg_sc = io.out("dbg_sc", [128, 16])
        dbg_xq = io.out("dbg_xq", [128, 128], BF16)
        dbg_up0 = io.out("dbg_up0", [128, 512])
        dbg_upq0 = io.out("dbg_upq0", [128, 512])
        dbg_g20 = io.out("dbg_g20", [128, 512])
        dbg_o0 = io.out("dbg_o0", [128, 512])
        dbg_wuT = io.out("dbg_wuT", [128, 512], BF16)
    wgT = io.internal("wgT", [D, F6], BF16)
    wuT = io.internal("wuT", [D, F6], BF16)
    wdT = io.internal("wdT", [F6, D], BF16)
    m_up = io.internal("m_up", [T, F6])
    m_gate = io.internal("m_gate", [T, F6])

    ident_sb = p.sb([128, 128], BF16)
    s_misc = p.sem()

    @p.on("sync")
    def _(sy):
        sy.dma_start(out=ident_sb[:], in_=ident_bf_d).then_inc(s_misc.h, 16)
    s_misc.add(16)

    invqm_pp = p.sb([128, 1], F32)
    inv7_pp = p.sb([128, 1], F32)
    c_inv = chain(p, "vector", [(p.s_cpool, 16)], [
        lambda v: v.reciprocal(invqm_pp[:], p.constf(QMAX6)),
        lambda v: v.reciprocal(inv7_pp[:], p.constf(7.0)),
    ], flush=[True, True])

    sq_x, inv_x = p.sb([128, 1], F32), p.sb([128, 1], F32)
    sq_u, inv_u = p.sb([128, 1], F32), p.sb([128, 1], F32)
    sq_g, inv_g = p.sb([128, 1], F32), p.sb([128, 1], F32)
    sq_g2, inv_g2 = p.sb([128, 1], F32), p.sb([128, 1], F32)
    sq_o, inv_o = p.sb([128, 1], F32), p.sb([128, 1], F32)
    acc_x = p.sb([128, 1], F32)
    rx = p.sb([128, 1], F32)
    acc_u = p.sb([128, 1], F32)
    acc_g = p.sb([128, 1], F32)
    acc_g2 = p.sb([128, 1], F32)
    acc_o = p.sb([128, 1], F32)
    arb6 = p.sb([128, 128], F32)
    XQ = [p.sb([128, T], BF16) for _ in range(ND)]
    dbsc = p.sb([128, 16], F32) if DEBUG_F1 else None

    wdq_sems = tuple(p.sem() for _ in range(6))
    c_wu = emit_wdq(nc, p, io, wu_d, wuT, F6, D, ident_sb, 16, s_misc,
                    inv7_pp, c_inv, sems=wdq_sems)
    c_wg = emit_wdq(nc, p, io, wg_d, wgT, F6, D, ident_sb, 16, s_misc,
                    inv7_pp, c_inv, prev=c_wu, sems=wdq_sems)
    c_wd = emit_wdq(nc, p, io, wd_d, wdT, D, F6, ident_sb, 16, s_misc,
                    inv7_pp, c_inv, prev=c_wg, sems=wdq_sems)

    scX = p.scope()
    x_t = [p.sb_in(scX, [128, D], F32) for _ in range(NT)]
    xab = p.sb_in(scX, [128, D], F32)
    s_xab = p.sem()
    s_ld = emit_load_tiles(p, x_in, x_t, waits=[c_wd])
    s_ln = emit_ln(p, x_t, x_t, lambda i: [(s_ld, 16 * (i + 1))], nb=2,
                   sc=scX)
    fl = p.flu["vector"]

    @p.on("vector")
    def _(v):
        v.memset(acc_x[:], 0.0)
    mx_last = None
    for i in range(NT):
        ml = mx_last

        @p.on("scalar")
        def _(s, i=i, ml=ml):
            s.wait_ge(s_ln.h, i + 1)
            if ml is not None:
                s.wait_ge(ml[0].h, ml[1])
            s.activation(xab[:], x_t[i][:], AF.Abs).then_inc(s_xab.h, 1)
        v_xab = s_xab.add()
        mx_last = chain(p, "vector",
                        [(s_xab, v_xab)] + ([mx_last] if mx_last else []), [
            lambda v, i=i: v.tensor_reduce(rx[:], xab[:], AX.X, ALU.max),
            lambda v: v.tensor_tensor(acc_x[:], acc_x[:], rx[:], ALU.max),
        ], flush=[True, True])
    armax_sems = (p.sem(), p.sem())
    c_sx = emit_armax(nc, p, io, "ar_x", groups, [acc_x], [sq_x], [inv_x],
                      [mx_last], invqm_pp, c_inv, arb6, sems=armax_sems)

    qi_x = p.sb_in(scX, [128, D], I32)
    qf_x = p.sb_in(scX, [128, D], F32)
    xqb = p.sb_in(scX, [128, D], BF16)
    ptq = [p.ps_in(scX, [128, 128], BF16) for _ in range(2)]
    s_xq = p.sem()
    s_xtm = p.sem()
    s_xte = p.sem()
    for i in range(NT):
        fl0 = fl.v

        @p.on("vector")
        def _(v, i=i, c_sx=c_sx, fl0=fl0):
            v.wait_ge(c_sx[0].h, c_sx[1])
            if i >= 1:
                v.wait_ge(s_xte.h, i * ND)
            v.scalar_tensor_tensor(qi_x[:], x_t[i][:], inv_x[:], x_t[i][:],
                                   ALU.mult, ALU.bypass).then_inc(fl.h, 1)
            v.wait_ge(fl.h, fl0 + 1)
            v.tensor_copy(qf_x[:], qi_x[:]).then_inc(fl.h, 1)
            v.wait_ge(fl.h, fl0 + 2)
            v.scalar_tensor_tensor(xqb[:], qf_x[:], sq_x[:], qf_x[:],
                                   ALU.mult, ALU.bypass).then_inc(s_xq.h, 1)
        fl.add(2)
        v_xq = s_xq.add()
        for c in range(ND):
            k = i * ND + c

            @p.on("tensor")
            def _(t, i=i, c=c, k=k, v_xq=v_xq):
                t.wait_ge(s_xq.h, v_xq)
                t.wait_ge(s_misc.h, 16)
                if k >= 2:
                    t.wait_ge(s_xte.h, k - 1)
                t.transpose(ptq[k % 2][:], xqb[:, c * 128:(c + 1) * 128],
                            ident_sb[:]).then_inc(s_xtm.h, 1)
            v_tm = s_xtm.add()

            @p.on("vector")
            def _(v, i=i, c=c, k=k, v_tm=v_tm):
                v.wait_ge(s_xtm.h, v_tm)
                v.tensor_copy(XQ[c][:, i * 128:(i + 1) * 128],
                              ptq[k % 2][:]).then_inc(s_xte.h, 1)
            s_xte.add()
    v_xdone = s_xte.v
    scX.close()

    scM = p.scope()
    wus = [p.sb_in(scM, [128, 512], BF16) for _ in range(ND)]
    wgs = [p.sb_in(scM, [128, 512], BF16) for _ in range(ND)]
    ut = [p.sb_in(scM, [128, 512], F32) for _ in range(2)]
    gt = [p.sb_in(scM, [128, 512], F32) for _ in range(2)]
    pu = [p.ps_in(scM, [128, 512], F32) for _ in range(2)]
    pg = [p.ps_in(scM, [128, 512], F32) for _ in range(2)]
    r1 = p.sb_in(scM, [128, 1], F32)
    uab = p.sb_in(scM, [128, 512], F32)
    gab = p.sb_in(scM, [128, 512], F32)
    s_wl = p.sem()
    s_mm = p.sem()
    s_mab = p.sem()
    s_ev = p.sem()
    s_st = p.sem()

    @p.on("vector")
    def _(v):
        v.memset(acc_u[:], 0.0)
        v.memset(acc_g[:], 0.0)

    NOC = F6 // 512
    for oc in range(NOC):
        @p.on("sync")
        def _(sy, oc=oc, c_wu=c_wu, c_wg=c_wg):
            if oc == 0:
                sy.wait_ge(c_wu[0].h, c_wu[1])
                sy.wait_ge(c_wg[0].h, c_wg[1])
                sy.wait_ge(s_xte.h, v_xdone)
            else:
                sy.wait_ge(s_wl.h, 16 * 2 * ND * oc)
                sy.wait_ge(s_mm.h, 2 * NT * oc)
            for c in range(ND):
                sy.dma_start(out=wus[c][:],
                             in_=wuT[c * 128:(c + 1) * 128,
                                     oc * 512:(oc + 1) * 512]
                             ).then_inc(s_wl.h, 16)
            for c in range(ND):
                sy.dma_start(out=wgs[c][:],
                             in_=wgT[c * 128:(c + 1) * 128,
                                     oc * 512:(oc + 1) * 512]
                             ).then_inc(s_wl.h, 16)
        s_wl.add(16 * 2 * ND)
        v_wl = s_wl.v

        for tt in range(NT):
            g_idx = oc * NT + tt

            @p.on("tensor")
            def _(t, oc=oc, tt=tt, g_idx=g_idx, v_wl=v_wl):
                if g_idx == 0:
                    t.wait_ge(s_xte.h, v_xdone)
                t.wait_ge(s_wl.h, v_wl)
                if g_idx >= 2:
                    t.wait_ge(s_ev.h, g_idx - 1)
                ins = None
                for c in range(ND):
                    ins = t.matmul(pu[tt % 2][:],
                                   XQ[c][:, tt * 128:(tt + 1) * 128],
                                   wus[c][:], start=(c == 0), stop=(c == ND - 1))
                ins.then_inc(s_mm.h, 1)
                ins = None
                for c in range(ND):
                    ins = t.matmul(pg[tt % 2][:],
                                   XQ[c][:, tt * 128:(tt + 1) * 128],
                                   wgs[c][:], start=(c == 0), stop=(c == ND - 1))
                ins.then_inc(s_mm.h, 1)
            v_mm = s_mm.add(2)

            @p.on("scalar")
            def _(s, tt=tt, g_idx=g_idx, v_mm=v_mm):
                s.wait_ge(s_mm.h, v_mm)
                if g_idx >= 1:
                    s.wait_ge(s_ev.h, g_idx)
                s.activation(uab[:], pu[tt % 2][:], AF.Abs)
                s.activation(gab[:], pg[tt % 2][:], AF.Abs
                             ).then_inc(s_mab.h, 1)
            v_mab = s_mab.add()

            fl0 = fl.v

            @p.on("vector")
            def _(v, oc=oc, tt=tt, g_idx=g_idx, v_mm=v_mm, fl0=fl0,
                  v_mab=v_mab):
                v.wait_ge(s_mm.h, v_mm)
                v.wait_ge(s_mab.h, v_mab)
                if g_idx >= 2:
                    v.wait_ge(s_st.h, 32 * (g_idx - 1))
                v.tensor_reduce(r1[:], uab[:], AX.X, ALU.max
                                ).then_inc(fl.h, 1)
                v.wait_ge(fl.h, fl0 + 1)
                v.tensor_tensor(acc_u[:], acc_u[:], r1[:], ALU.max
                                ).then_inc(fl.h, 1)
                v.wait_ge(fl.h, fl0 + 2)
                v.tensor_reduce(r1[:], gab[:], AX.X, ALU.max
                                ).then_inc(fl.h, 1)
                v.wait_ge(fl.h, fl0 + 3)
                v.tensor_tensor(acc_g[:], acc_g[:], r1[:], ALU.max
                                ).then_inc(fl.h, 1)
                v.wait_ge(fl.h, fl0 + 4)
                v.tensor_copy(ut[tt % 2][:], pu[tt % 2][:])
                v.tensor_copy(gt[tt % 2][:], pg[tt % 2][:]
                              ).then_inc(s_ev.h, 1)
            fl.add(4)
            v_ev = s_ev.add(1)

            @p.on("gpsimd")
            def _(g, oc=oc, tt=tt, g_idx=g_idx, v_ev=v_ev):
                g.wait_ge(s_ev.h, v_ev)
                if g_idx >= 1:
                    g.wait_ge(s_st.h, 32 * g_idx)
                g.dma_start(out=m_up[tt * 128:(tt + 1) * 128,
                                     oc * 512:(oc + 1) * 512],
                            in_=ut[tt % 2][:]).then_inc(s_st.h, 16)
                g.dma_start(out=m_gate[tt * 128:(tt + 1) * 128,
                                       oc * 512:(oc + 1) * 512],
                            in_=gt[tt % 2][:]).then_inc(s_st.h, 16)
            s_st.add(32)
    v_m1st = s_st.v
    if DEBUG_F1:
        @p.on("gpsimd")
        def _(g, vm=v_m1st):
            g.wait_ge(s_st.h, vm)
            g.dma_start(out=dbg_up0, in_=m_up[0:128, 0:512]).then_inc(s_st.h, 16)
        s_st.add(16)
        v_m1st = s_st.v
    c_sug = emit_armax(nc, p, io, "ar_ug", groups, [acc_u, acc_g],
                       [sq_u, sq_g], [inv_u, inv_g],
                       [(s_ev, s_ev.v), (s_st, v_m1st)], invqm_pp, c_inv,
                       arb6, sems=armax_sems)
    scM.close()

    ECH = 1024
    NEC = F6 // ECH
    scE = p.scope()
    eu = [p.sb_in(scE, [128, ECH], F32) for _ in range(2)]
    eg = [p.sb_in(scE, [128, ECH], F32) for _ in range(2)]
    eqi = p.sb_in(scE, [128, ECH], I32)
    eqf = p.sb_in(scE, [128, ECH], F32)
    esig = p.sb_in(scE, [128, ECH], F32)
    eab = p.sb_in(scE, [128, ECH], F32)
    r2 = p.sb_in(scE, [128, 1], F32)
    s_el = p.sem()
    s_ec = p.sem()
    s_es = p.sem()
    s_esg = p.sem()
    s_eab = p.sem()

    @p.on("vector")
    def _(v):
        v.memset(acc_g2[:], 0.0)

    NCHU = NT * NEC
    for i in range(NCHU):
        tt, ec = divmod(i, NEC)
        ro, co = tt * 128, ec * ECH

        @p.on("sync")
        def _(sy, i=i, ro=ro, co=co, c_sug=c_sug):
            if i == 0:
                sy.wait_ge(c_sug[0].h, c_sug[1])
            else:
                sy.wait_ge(s_el.h, 32 * i)
            if i >= 2:
                sy.wait_ge(s_es.h, 32 * (i - 1))
            sy.dma_start(out=eu[i % 2][:], in_=m_up[ro:ro + 128, co:co + ECH]
                         ).then_inc(s_el.h, 16)
            sy.dma_start(out=eg[i % 2][:], in_=m_gate[ro:ro + 128, co:co + ECH]
                         ).then_inc(s_el.h, 16)
        s_el.add(32)
        v_el = s_el.v

        fl0 = fl.v

        @p.on("vector")
        def _(v, i=i, v_el=v_el, fl0=fl0):
            v.wait_ge(s_el.h, v_el)
            u, g = eu[i % 2], eg[i % 2]
            v.scalar_tensor_tensor(eqi[:], u[:], inv_u[:], u[:],
                                   ALU.mult, ALU.bypass).then_inc(fl.h, 1)
            v.wait_ge(fl.h, fl0 + 1)
            v.tensor_copy(eqf[:], eqi[:]).then_inc(fl.h, 1)
            v.wait_ge(fl.h, fl0 + 2)
            v.scalar_tensor_tensor(u[:], eqf[:], sq_u[:], eqf[:],
                                   ALU.mult, ALU.bypass).then_inc(fl.h, 1)
            v.wait_ge(fl.h, fl0 + 3)
            v.scalar_tensor_tensor(eqi[:], g[:], inv_g[:], g[:],
                                   ALU.mult, ALU.bypass).then_inc(fl.h, 1)
            v.wait_ge(fl.h, fl0 + 4)
            v.tensor_copy(eqf[:], eqi[:]).then_inc(fl.h, 1)
            v.wait_ge(fl.h, fl0 + 5)
            v.scalar_tensor_tensor(g[:], eqf[:], sq_g[:], eqf[:],
                                   ALU.mult, ALU.bypass).then_inc(s_ec.h, 1)
        fl.add(5)
        v_ec = s_ec.add()

        @p.on("scalar")
        def _(s, i=i, v_ec=v_ec):
            s.wait_ge(s_ec.h, v_ec)
            s.activation(esig[:], eg[i % 2][:], AF.Sigmoid).then_inc(s_esg.h, 1)
        v_sg = s_esg.add()

        fl0 = fl.v

        @p.on("vector")
        def _(v, i=i, v_sg=v_sg, fl0=fl0):
            v.wait_ge(s_esg.h, v_sg)
            g = eg[i % 2]
            v.tensor_scalar(eqi[:], esig[:], 65536.0, 0.0, ALU.mult, ALU.add
                            ).then_inc(fl.h, 1)
            v.wait_ge(fl.h, fl0 + 1)
            v.tensor_copy(eqf[:], eqi[:]).then_inc(fl.h, 1)
            v.wait_ge(fl.h, fl0 + 2)
            v.tensor_scalar(esig[:], eqf[:], 65535.0, 1.0 / 65536.0,
                            ALU.min, ALU.mult).then_inc(fl.h, 1)
            v.wait_ge(fl.h, fl0 + 3)
            v.tensor_tensor(g[:], g[:], esig[:], ALU.mult).then_inc(fl.h, 1)
        fl.add(4)
        v_fl4 = fl.v

        @p.on("scalar")
        def _(s, i=i, v_fl4=v_fl4):
            s.wait_ge(fl.h, v_fl4)
            s.activation(eab[:], eg[i % 2][:], AF.Abs).then_inc(s_eab.h, 1)
        v_eab = s_eab.add()

        fl0b = fl.v

        @p.on("vector")
        def _(v, i=i, v_eab=v_eab, fl0b=fl0b):
            v.wait_ge(s_eab.h, v_eab)
            v.tensor_reduce(r2[:], eab[:], AX.X, ALU.max).then_inc(fl.h, 1)
            v.wait_ge(fl.h, fl0b + 1)
            v.tensor_tensor(acc_g2[:], acc_g2[:], r2[:], ALU.max
                            ).then_inc(s_ec.h, 1)
        fl.add(1)
        v_ec2 = s_ec.add()

        @p.on("gpsimd")
        def _(g_, i=i, ro=ro, co=co, v_ec2=v_ec2):
            g_.wait_ge(s_ec.h, v_ec2)
            if i >= 1:
                g_.wait_ge(s_es.h, 32 * i)
            g_.dma_start(out=m_up[ro:ro + 128, co:co + ECH], in_=eu[i % 2][:]
                         ).then_inc(s_es.h, 16)
            g_.dma_start(out=m_gate[ro:ro + 128, co:co + ECH], in_=eg[i % 2][:]
                         ).then_inc(s_es.h, 16)
        s_es.add(32)
    if DEBUG_F1:
        @p.on("gpsimd")
        def _(g, vv=s_es.v):
            g.wait_ge(s_es.h, vv)
            g.dma_start(out=dbg_g20, in_=m_gate[0:128, 0:512]).then_inc(s_es.h, 16)
            g.wait_ge(s_es.h, vv + 16)
            g.dma_start(out=dbg_upq0, in_=m_up[0:128, 0:512]).then_inc(s_es.h, 16)
        s_es.add(32)
    c_sg2 = emit_armax(nc, p, io, "ar_g2", groups, [acc_g2], [sq_g2], [inv_g2],
                       [(s_ec, s_ec.v), (s_es, s_es.v)], invqm_pp, c_inv, arb6,
                       sems=armax_sems)


    @p.on("vector")
    def _(v):
        v.memset(acc_o[:], 0.0)
    s_el2 = p.sem()
    s_ec2 = p.sem()
    s_es2 = p.sem()
    for i in range(NCHU):
        tt, ec = divmod(i, NEC)
        ro, co = tt * 128, ec * ECH

        @p.on("sync")
        def _(sy, i=i, ro=ro, co=co, c_sg2=c_sg2):
            if i == 0:
                sy.wait_ge(c_sg2[0].h, c_sg2[1])
            else:
                sy.wait_ge(s_el2.h, 32 * i)
            if i >= 2:
                sy.wait_ge(s_es2.h, 16 * (i - 1))
                sy.wait_ge(s_ec2.h, i - 1)
            sy.dma_start(out=eu[i % 2][:], in_=m_up[ro:ro + 128, co:co + ECH]
                         ).then_inc(s_el2.h, 16)
            sy.dma_start(out=eg[i % 2][:], in_=m_gate[ro:ro + 128, co:co + ECH]
                         ).then_inc(s_el2.h, 16)
        s_el2.add(32)
        v_el = s_el2.v

        fl0 = fl.v

        @p.on("vector")
        def _(v, i=i, v_el=v_el, fl0=fl0):
            v.wait_ge(s_el2.h, v_el)
            u, g = eu[i % 2], eg[i % 2]
            v.scalar_tensor_tensor(eqi[:], g[:], inv_g2[:], g[:],
                                   ALU.mult, ALU.bypass).then_inc(fl.h, 1)
            v.wait_ge(fl.h, fl0 + 1)
            v.tensor_copy(eqf[:], eqi[:]).then_inc(fl.h, 1)
            v.wait_ge(fl.h, fl0 + 2)
            v.scalar_tensor_tensor(g[:], eqf[:], sq_g2[:], eqf[:],
                                   ALU.mult, ALU.bypass).then_inc(fl.h, 1)
            v.wait_ge(fl.h, fl0 + 3)
            v.tensor_tensor(g[:], g[:], u[:], ALU.mult).then_inc(fl.h, 1)
        fl.add(4)
        v_fl4 = fl.v

        @p.on("scalar")
        def _(s, i=i, v_fl4=v_fl4):
            s.wait_ge(fl.h, v_fl4)
            s.activation(eab[:], eg[i % 2][:], AF.Abs).then_inc(s_eab.h, 1)
        v_eab = s_eab.add()

        fl0b = fl.v

        @p.on("vector")
        def _(v, i=i, v_eab=v_eab, fl0b=fl0b):
            v.wait_ge(s_eab.h, v_eab)
            v.tensor_reduce(r2[:], eab[:], AX.X, ALU.max).then_inc(fl.h, 1)
            v.wait_ge(fl.h, fl0b + 1)
            v.tensor_tensor(acc_o[:], acc_o[:], r2[:], ALU.max
                            ).then_inc(s_ec2.h, 1)
        fl.add(1)
        v_ec = s_ec2.add()

        @p.on("gpsimd")
        def _(g_, i=i, ro=ro, co=co, v_ec=v_ec):
            g_.wait_ge(s_ec2.h, v_ec)
            if i >= 1:
                g_.wait_ge(s_es2.h, 16 * i)
            g_.dma_start(out=m_gate[ro:ro + 128, co:co + ECH], in_=eg[i % 2][:]
                         ).then_inc(s_es2.h, 16)
        s_es2.add(16)
    if DEBUG_F1:
        @p.on("gpsimd")
        def _(g, vv=s_es2.v):
            g.wait_ge(s_es2.h, vv)
            g.dma_start(out=dbg_o0, in_=m_gate[0:128, 0:512]).then_inc(s_es2.h, 16)
        s_es2.add(16)
    c_so = emit_armax(nc, p, io, "ar_o", groups, [acc_o], [sq_o], [inv_o],
                      [(s_ec2, s_ec2.v), (s_es2, s_es2.v)], invqm_pp, c_inv,
                      arb6, sems=armax_sems)
    scE.close()

    scD = p.scope()
    wds = [p.sb_in(scD, [128, D], BF16) for _ in range(F6 // 128)]
    ot_ = [p.sb_in(scD, [128, F6], F32) for _ in range(2)]
    oqi = p.sb_in(scD, [128, F6], I32)
    oqf = p.sb_in(scD, [128, F6], F32)
    oqb = p.sb_in(scD, [128, F6], BF16)
    OQF = [p.sb_in(scD, [128, 128], BF16) for _ in range(F6 // 128)]
    xr = [p.sb_in(scD, [128, D], F32) for _ in range(2)]
    yt = [p.sb_in(scD, [128, 512], F32) for _ in range(2)]
    pd = [p.ps_in(scD, [128, 512], F32) for _ in range(2)]
    ptd = [p.ps_in(scD, [128, 128], BF16) for _ in range(2)]
    s_wdl = p.sem()
    s_ol = p.sem()
    s_oq = p.sem()
    s_otm = p.sem()
    s_ote = p.sem()
    s_dmm = p.sem()
    s_dev = p.sem()
    s_xr = p.sem()
    s_yst = p.sem()
    NF = F6 // 128

    @p.on("sync")
    def _(sy, c_wd=c_wd, c_so=c_so):
        sy.wait_ge(c_wd[0].h, c_wd[1])
        sy.wait_ge(c_so[0].h, c_so[1])
        for c in range(NF):
            if c >= 1:
                sy.wait_ge(s_wdl.h, 16 * c)
            sy.dma_start(out=wds[c][:], in_=wdT[c * 128:(c + 1) * 128, :]
                         ).then_inc(s_wdl.h, 16)
    s_wdl.add(16 * NF)

    for tt in range(NT):
        @p.on("sync")
        def _(sy, tt=tt, c_so=c_so):
            if tt == 0:
                sy.wait_ge(c_so[0].h, c_so[1])
            else:
                sy.wait_ge(s_ol.h, 16 * tt)
            if tt >= 2:
                sy.wait_ge(s_oq.h, tt - 1)
            sy.dma_start(out=ot_[tt % 2][:],
                         in_=m_gate[tt * 128:(tt + 1) * 128, :]
                         ).then_inc(s_ol.h, 16)
        s_ol.add(16)
        v_ol = s_ol.v

        fl0 = fl.v

        @p.on("vector")
        def _(v, tt=tt, v_ol=v_ol, fl0=fl0):
            v.wait_ge(s_ol.h, v_ol)
            if tt >= 1:
                v.wait_ge(s_ote.h, tt * NF)
            o = ot_[tt % 2]
            v.scalar_tensor_tensor(oqi[:], o[:], inv_o[:], o[:],
                                   ALU.mult, ALU.bypass).then_inc(fl.h, 1)
            v.wait_ge(fl.h, fl0 + 1)
            v.tensor_copy(oqf[:], oqi[:]).then_inc(fl.h, 1)
            v.wait_ge(fl.h, fl0 + 2)
            v.scalar_tensor_tensor(oqb[:], oqf[:], sq_o[:], oqf[:],
                                   ALU.mult, ALU.bypass).then_inc(s_oq.h, 1)
        fl.add(2)
        v_oq = s_oq.add()

        for c in range(NF):
            k = tt * NF + c

            @p.on("tensor")
            def _(t, c=c, k=k, v_oq=v_oq):
                t.wait_ge(s_oq.h, v_oq)
                if k >= 2:
                    t.wait_ge(s_ote.h, k - 1)
                t.transpose(ptd[k % 2][:], oqb[:, c * 128:(c + 1) * 128],
                            ident_sb[:]).then_inc(s_otm.h, 1)
            v_tm = s_otm.add()

            @p.on("vector")
            def _(v, c=c, k=k, v_tm=v_tm, tt=tt):
                v.wait_ge(s_otm.h, v_tm)
                if tt >= 1 and c == 0:
                    v.wait_ge(s_dmm.h, 2 * tt)
                v.tensor_copy(OQF[c][:], ptd[k % 2][:]).then_inc(s_ote.h, 1)
            s_ote.add()
        v_ote = s_ote.v

        @p.on("sync")
        def _(sy, tt=tt):
            if tt >= 1:
                sy.wait_ge(s_xr.h, 16 * tt)
            if tt >= 2:
                sy.wait_ge(s_yst.h, 32 * (tt - 1))
            sy.dma_start(out=xr[tt % 2][:], in_=x_in[tt * 128:(tt + 1) * 128, :]
                         ).then_inc(s_xr.h, 16)
        s_xr.add(16)
        v_xr = s_xr.v

        for dc in range(2):
            g_idx = tt * 2 + dc

            @p.on("tensor")
            def _(t, tt=tt, dc=dc, g_idx=g_idx, v_ote=v_ote):
                t.wait_ge(s_ote.h, v_ote)
                if g_idx == 0:
                    t.wait_ge(s_wdl.h, 16 * NF)
                if g_idx >= 2:
                    t.wait_ge(s_dev.h, g_idx - 1)
                ins = None
                for c in range(NF):
                    ins = t.matmul(pd[dc][:], OQF[c][:],
                                   wds[c][:, dc * 512:(dc + 1) * 512],
                                   start=(c == 0), stop=(c == NF - 1))
                ins.then_inc(s_dmm.h, 1)
            v_dm = s_dmm.add()

            @p.on("vector")
            def _(v, tt=tt, dc=dc, g_idx=g_idx, v_dm=v_dm):
                v.wait_ge(s_dmm.h, v_dm)
                if g_idx >= 2:
                    v.wait_ge(s_yst.h, 16 * (g_idx - 1))
                v.tensor_copy(yt[dc][:], pd[dc][:]).then_inc(s_dev.h, 1)
            v_de = s_dev.add()

            gfl = p.flu["gpsimd"].add()

            @p.on("gpsimd")
            def _(g_, tt=tt, dc=dc, g_idx=g_idx, v_de=v_de, v_xr=v_xr,
                  gfl=gfl):
                g_.wait_ge(s_dev.h, v_de)
                g_.wait_ge(s_xr.h, v_xr)
                g_.tensor_tensor(yt[dc][:], yt[dc][:],
                                 xr[tt % 2][:, dc * 512:(dc + 1) * 512],
                                 ALU.add).then_inc(p.flu["gpsimd"].h, 1)
                g_.wait_ge(p.flu["gpsimd"].h, gfl)
                if g_idx >= 1:
                    g_.wait_ge(s_yst.h, 16 * g_idx)
                g_.dma_start(out=y_out[tt * 128:(tt + 1) * 128,
                                       dc * 512:(dc + 1) * 512],
                             in_=yt[dc][:]).then_inc(s_yst.h, 16)
            s_yst.add(16)
    scD.close()

    if DEBUG_F1:
        c_db = chain(p, "vector", [c_so, (s_yst, 32 * NT)], [
            lambda v: v.memset(dbsc[:], 0.0),
            lambda v: v.tensor_copy(dbsc[:, 0:1], sq_x[:]),
            lambda v: v.tensor_copy(dbsc[:, 1:2], inv_x[:]),
            lambda v: v.tensor_copy(dbsc[:, 2:3], sq_u[:]),
            lambda v: v.tensor_copy(dbsc[:, 3:4], inv_u[:]),
            lambda v: v.tensor_copy(dbsc[:, 4:5], sq_g[:]),
            lambda v: v.tensor_copy(dbsc[:, 5:6], inv_g[:]),
            lambda v: v.tensor_copy(dbsc[:, 6:7], sq_g2[:]),
            lambda v: v.tensor_copy(dbsc[:, 7:8], inv_g2[:]),
            lambda v: v.tensor_copy(dbsc[:, 8:9], sq_o[:]),
            lambda v: v.tensor_copy(dbsc[:, 9:10], inv_o[:]),
        ], flush="all")
        s_db = p.sem()

        @p.on("gpsimd")
        def _(g, c_db=c_db):
            g.wait_ge(c_db[0].h, c_db[1])
            g.dma_start(out=dbg_sc, in_=dbsc[:]).then_inc(s_db.h, 16)
            g.dma_start(out=dbg_xq, in_=XQ[0][:, 0:128]).then_inc(s_db.h, 16)
            g.dma_start(out=dbg_wuT, in_=wuT[0:128, 0:512]).then_inc(s_db.h, 16)
            g.wait_ge(s_db.h, 48)
        s_db.add(48)

    @p.on("gpsimd")
    def _(g_):
        g_.wait_ge(s_yst.h, 32 * NT)



FFNB = 4096
CHUNK = 64
DEBUG_B1 = False


def emit_b1(nc, p, io, ncores=None):
    if ncores is None:
        ncores = NCORE
    groups = [[i, i + 1] for i in range(0, ncores, 2)]
    x_in = io.inp("x", [T, D])
    xl3_in = io.inp("xl3", [3, D])
    iso_in = io.inp("is_odd", [1, 1])
    pnw_in = io.inp("pnw", [1, D])
    cw4_in = io.inp("cw4", [4, D])
    cb_in = io.inp("cb", [1, D])
    waT_in = io.inp("waT", [D, D])
    wiT_in = io.inp("wiT", [D, D])
    wvT_in = io.inp("wvT", [D, D])
    db_in = io.inp("db", [1, D])
    woT_in = io.inp("woT", [D, D])
    fnw_in = io.inp("fnw", [1, D])
    wguT_in = io.inp("wguT", [D, 2 * FFNB])
    wdnT_in = io.inp("wdnT", [FFNB, D])
    identb_in = io.inp("ident_bf", [128, 128], BF16)
    identf_in = io.inp("ident_f", [128, 128])
    y_out = io.out("y", [T, D])

    if DEBUG_B1:
        dbg_afm = io.out("dbg_afm", [128, T])
        dbg_ufm = io.out("dbg_ufm", [128, T])
        dbg_sr = io.out("dbg_sr", [128, T])
        dbg_x2 = io.out("dbg_x2", [128, D])
        dbg_hbfm = io.out("dbg_hbfm", [128, T], BF16)
    a_fm = io.internal("a_fm", [D, T])
    u_fm = io.internal("u_fm", [D, T])
    hb_fm = io.internal("hb_fm", [FFNB, T], BF16)
    ag_in = io.internal("ag_in", [1, D])
    ag_out = io.internal("ag_out", [2, D])

    NCH = T // CHUNK
    fl = p.flu["vector"]

    identb = p.sb([128, 128], BF16)
    identf = p.sb([128, 128], F32)
    pnw_bc = p.sb([128, D], F32)
    fnw_bc = p.sb([128, D], F32)
    db_bc = p.sb([128, D], F32)
    cwt = [p.sb([128, 4], F32) for _ in range(ND)]
    cbt = [p.sb([128, 1], F32) for _ in range(ND)]
    iso_pp = p.sb([128, 1], F32)
    s_misc = p.sem()

    @p.on("gpsimd")
    def _(g):
        n = [0]
        def nx(ins):
            n[0] += 16
            ins.then_inc(s_misc.h, 16)
            g.wait_ge(s_misc.h, n[0])
        nx(g.dma_start(out=identb[:], in_=identb_in))
        nx(g.dma_start(out=identf[:], in_=identf_in))
        nx(g.dma_start(out=pnw_bc[:], in_=pnw_in.broadcast_to((128, D))))
        nx(g.dma_start(out=fnw_bc[:], in_=fnw_in.broadcast_to((128, D))))
        nx(g.dma_start(out=db_bc[:], in_=db_in.broadcast_to((128, D))))
        nx(g.dma_start(out=iso_pp[:], in_=iso_in.broadcast_to((128, 1))))
        for c in range(ND):
            for k in range(4):
                nx(g.dma_start(out=cwt[c][:, k:k + 1],
                               in_=cw4_in[k:k + 1, c * 128:(c + 1) * 128]))
            nx(g.dma_start(out=cbt[c][:], in_=cb_in[0:1, c * 128:(c + 1) * 128]))
    s_misc.add(16 * (6 + 5 * ND))
    v_misc = s_misc.v

    x_t = [p.sb([128, D], F32) for _ in range(NT)]
    scC = p.scope()
    cfm = [p.sb_in(scC, [128, T], BF16) for _ in range(ND)]
    sc1 = p.scope()
    ext = [p.sb_in(sc1, [128, T + 3], F32) for _ in range(ND)]
    xl3t = p.sb_in(sc1, [128, D], F32)
    xnr = [p.sb_in(sc1, [128, D], F32) for _ in range(2)]
    ptf = [p.ps_in(sc1, [128, 128], F32) for _ in range(2)]
    s_ld = emit_load_tiles(p, x_in, x_t)
    s_oln = emit_ln(p, x_t, x_t, lambda i: [(s_ld, 16 * (i + 1))],
                    rms=False, eps=1e-5, sc=sc1, nb=1)
    s_l3 = p.sem()
    c_z3 = chain(p, "vector", [], [lambda v: v.memset(xl3t[:], 0.0)])

    @p.on("sync")
    def _(sy, c_z3=c_z3):
        sy.wait_ge(c_z3[0].h, c_z3[1])
        sy.dma_start(out=xl3t[0:3, :], in_=xl3_in).then_inc(s_l3.h, 16)
    s_l3.add(16)

    s_tp1 = p.sem()
    s_tm1 = p.sem()

    s_oln3 = emit_ln(p, [xl3t], [xl3t], lambda i: [(s_l3, 16)], rms=False,
                     eps=1e-5, n_tiles=1, sc=sc1, nb=1)
    s_ln3 = emit_ln(p, [xl3t], [xl3t], lambda i: [(s_oln3, 1)], rms=True,
                    eps=1e-6, n_tiles=1, sc=sc1, nb=1)
    c_w3 = chain(p, "vector", [(s_ln3, 1), (s_misc, v_misc)], [
        lambda v: v.tensor_tensor(xl3t[:], xl3t[:], pnw_bc[:], ALU.mult)])
    for c in range(ND):
        @p.on("tensor")
        def _(t, c=c, c_w3=c_w3):
            if c == 0:
                t.wait_ge(c_w3[0].h, c_w3[1])
                t.wait_ge(s_misc.h, 32)
            if c >= 2:
                t.wait_ge(s_tp1.h, c - 1)
            t.transpose(ptf[c % 2][:], xl3t[:, c * 128:(c + 1) * 128],
                        identf[:]).then_inc(s_tm1.h, 1)
        v_tm = s_tm1.add()

        @p.on("vector")
        def _(v, c=c, v_tm=v_tm):
            v.wait_ge(s_tm1.h, v_tm)
            v.tensor_copy(ext[c][:, 0:3], ptf[c % 2][:, 0:3]).then_inc(s_tp1.h, 1)
        s_tp1.add()
    base_tp1 = s_tp1.v

    def at1(i, s_fin, fin_v):
        c_wm = chain(p, "vector", [(s_fin, fin_v), (s_misc, v_misc)], [
            lambda v, i=i: v.tensor_tensor(xnr[i % 2][:], xnr[i % 2][:],
                                           pnw_bc[:], ALU.mult)])
        for c in range(ND):
            k = i * ND + c

            @p.on("tensor")
            def _(t, i=i, c=c, k=k, c_wm=c_wm):
                t.wait_ge(c_wm[0].h, c_wm[1])
                if base_tp1 + k >= 2:
                    t.wait_ge(s_tp1.h, base_tp1 + k - 1)
                t.transpose(ptf[k % 2][:], xnr[i % 2][:, c * 128:(c + 1) * 128],
                            identf[:]).then_inc(s_tm1.h, 1)
            v_tm = s_tm1.add()

            @p.on("vector")
            def _(v, i=i, c=c, k=k, v_tm=v_tm):
                v.wait_ge(s_tm1.h, v_tm)
                v.tensor_copy(ext[c][:, 3 + i * 128:3 + (i + 1) * 128],
                              ptf[k % 2][:]).then_inc(s_tp1.h, 1)
            s_tp1.add()

    s_ln1 = emit_ln(p, x_t, [xnr[i % 2] for i in range(NT)],
                    lambda i: [(s_oln, i + 1)] +
                              ([(s_tp1, base_tp1 + (i - 1) * ND)]
                               if i >= 2 else []),
                    rms=True, eps=1e-6, sc=sc1, nb=1, after_tile=at1)
    v_tp1 = base_tp1 + NT * ND

    HW_ = T // 2
    ctmp = p.sb_in(sc1, [128, HW_], F32)
    s_cv = p.sem()
    for c in range(ND):
        for hh in range(2):
            ic = c * 2 + hh
            fl0 = fl.v

            @p.on("vector")
            def _(v, c=c, hh=hh, ic=ic, fl0=fl0):
                if ic == 0:
                    v.wait_ge(s_tp1.h, v_tp1)
                else:
                    v.wait_ge(s_cv.h, ic)
                o0 = hh * HW_
                v.scalar_tensor_tensor(ctmp[:], ext[c][:, o0:o0 + HW_],
                                       cwt[c][:, 0:1],
                                       ext[c][:, o0:o0 + HW_],
                                       ALU.mult, ALU.bypass).then_inc(fl.h, 1)
                v.wait_ge(fl.h, fl0 + 1)
                for k in (1, 2, 3):
                    v.scalar_tensor_tensor(ctmp[:], ext[c][:, o0 + k:o0 + k + HW_],
                                           cwt[c][:, k:k + 1], ctmp[:],
                                           ALU.mult, ALU.add).then_inc(fl.h, 1)
                    v.wait_ge(fl.h, fl0 + k + 1)
                v.scalar_tensor_tensor(cfm[c][:, o0:o0 + HW_], ctmp[:],
                                       cbt[c][:], ctmp[:],
                                       ALU.add, ALU.bypass).then_inc(s_cv.h, 1)
            fl.add(4)
            s_cv.add()
    v_cv = s_cv.v
    sc1.close()

    sc3 = p.scope()
    wa_sb = [p.sb_in(sc3, [128, D], BF16) for _ in range(ND)]
    wi_sb = [p.sb_in(sc3, [128, D], BF16) for _ in range(ND)]
    wv_sb = [p.sb_in(sc3, [128, D], BF16) for _ in range(ND)]
    s_w3 = emit_weight_load_bf16(p, waT_in, wa_sb, wait=[(s_cv, v_cv)])
    s_w3b = emit_weight_load_bf16(p, wiT_in, wi_sb, wait=[(s_w3, 16 * ND)])
    s_w3c = emit_weight_load_bf16(p, wvT_in, wv_sb, wait=[(s_w3b, 16 * ND)])
    pa = [p.ps_in(sc3, [128, 512], F32) for _ in range(2)]
    pi = [p.ps_in(sc3, [128, 512], F32) for _ in range(2)]
    pv = [p.ps_in(sc3, [128, 512], F32) for _ in range(2)]
    ptf3 = [p.ps_in(sc3, [128, 128], F32) for _ in range(2)]
    t1 = p.sb_in(sc3, [128, 512], F32)
    av = [p.sb_in(sc3, [128, 512], F32) for _ in range(2)]
    ig = p.sb_in(sc3, [128, 512], F32)
    vv = [p.sb_in(sc3, [128, 512], F32) for _ in range(2)]
    etf = [p.sb_in(sc3, [128, 128], F32) for _ in range(2)]
    s_mm3 = p.sem()
    s_sg3 = p.sem()
    s_ev3 = p.sem()
    s_tm3 = p.sem()
    s_te3 = p.sem()
    s_st3 = p.sem()
    for tt in range(NT):
        for oc in range(2):
            g_idx = tt * 2 + oc

            @p.on("tensor")
            def _(t, tt=tt, oc=oc, g_idx=g_idx):
                if g_idx == 0:
                    t.wait_ge(s_cv.h, v_cv)
                    t.wait_ge(s_w3c.h, 16 * ND)
                if g_idx >= 2:
                    t.wait_ge(s_ev3.h, 3 * (g_idx - 1))
                for wsb, ps in ((wa_sb, pa), (wi_sb, pi), (wv_sb, pv)):
                    ins = None
                    for c in range(ND):
                        ins = t.matmul(ps[g_idx % 2][:],
                                       cfm[c][:, tt * 128:(tt + 1) * 128],
                                       wsb[c][:, oc * 512:(oc + 1) * 512],
                                       start=(c == 0), stop=(c == ND - 1))
                    ins.then_inc(s_mm3.h, 1)
            v_mm = s_mm3.add(3)

            @p.on("vector")
            def _(v, tt=tt, oc=oc, g_idx=g_idx, v_mm=v_mm):
                v.wait_ge(s_mm3.h, v_mm)
                if g_idx >= 2:
                    v.wait_ge(s_te3.h, 8 * (g_idx - 1))
                v.tensor_tensor(t1[:], pa[g_idx % 2][:],
                                db_bc[:, oc * 512:(oc + 1) * 512], ALU.add
                                ).then_inc(s_ev3.h, 1)
            s_ev3.add()
            v_e1 = s_ev3.v

            sfl = p.flu["scalar"].add()

            @p.on("scalar")
            def _(s, g_idx=g_idx, v_e1=v_e1, sfl=sfl):
                s.wait_ge(s_ev3.h, v_e1)
                s.activation(av[g_idx % 2][:], t1[:], AF.Sigmoid
                             ).then_inc(p.flu["scalar"].h, 1)
                s.activation(ig[:], pi[g_idx % 2][:], AF.Sigmoid)
                s.wait_ge(p.flu["scalar"].h, sfl)
                s.activation(t1[:], av[g_idx % 2][:], AF.Square
                             ).then_inc(s_sg3.h, 1)
            v_sg = s_sg3.add()

            fl0 = fl.v

            @p.on("vector")
            def _(v, g_idx=g_idx, v_sg=v_sg, fl0=fl0):
                v.wait_ge(s_sg3.h, v_sg)
                v.tensor_scalar(t1[:], t1[:], -1.0, 1.0, ALU.mult, ALU.add
                                ).then_inc(fl.h, 1)
                v.wait_ge(fl.h, fl0 + 1)
                v.tensor_scalar(t1[:], t1[:], 1e-8, 0.0, ALU.max, ALU.add
                                ).then_inc(fl.h, 1)
                v.wait_ge(fl.h, fl0 + 2)
                v.tensor_copy(vv[g_idx % 2][:], pv[g_idx % 2][:]
                              ).then_inc(s_ev3.h, 1)
            fl.add(2)
            s_ev3.add()
            v_e2 = s_ev3.v

            @p.on("scalar")
            def _(s, v_e2=v_e2):
                s.wait_ge(s_ev3.h, v_e2)
                s.wait_ge(p.s_cpool.h, 16)
                s.activation(t1[:], t1[:], AF.Sqrt, bias=p.constf(0.0)
                             ).then_inc(s_sg3.h, 1)
            v_sg2 = s_sg3.add()

            fl0 = fl.v

            @p.on("vector")
            def _(v, g_idx=g_idx, v_sg2=v_sg2, fl0=fl0):
                v.wait_ge(s_sg3.h, v_sg2)
                v.tensor_tensor(vv[g_idx % 2][:], ig[:], vv[g_idx % 2][:],
                                ALU.mult).then_inc(fl.h, 1)
                v.wait_ge(fl.h, fl0 + 1)
                v.tensor_tensor(vv[g_idx % 2][:], vv[g_idx % 2][:], t1[:],
                                ALU.mult).then_inc(s_ev3.h, 1)
            fl.add(1)
            s_ev3.add()
            v_e3 = s_ev3.v

            for q in range(4):
                for which in range(2):
                    src = av if which == 0 else vv
                    kq = (g_idx * 4 + q) * 2 + which

                    @p.on("tensor")
                    def _(t, g_idx=g_idx, q=q, src=src, kq=kq, v_e3=v_e3):
                        t.wait_ge(s_ev3.h, v_e3)
                        if kq >= 2:
                            t.wait_ge(s_te3.h, kq - 1)
                        t.transpose(ptf3[kq % 2][:],
                                    src[g_idx % 2][:, q * 128:(q + 1) * 128],
                                    identf[:]).then_inc(s_tm3.h, 1)
                    v_tm = s_tm3.add()

                    @p.on("vector")
                    def _(v, kq=kq, v_tm=v_tm):
                        v.wait_ge(s_tm3.h, v_tm)
                        if kq >= 2:
                            v.wait_ge(s_st3.h, 16 * (kq - 1))
                        v.tensor_copy(etf[kq % 2][:], ptf3[kq % 2][:]
                                      ).then_inc(s_te3.h, 1)
                    v_te = s_te3.add()

                    @p.on("gpsimd")
                    def _(g, tt=tt, oc=oc, q=q, which=which, kq=kq, v_te=v_te):
                        g.wait_ge(s_te3.h, v_te)
                        if kq >= 1:
                            g.wait_ge(s_st3.h, 16 * kq)
                        dst = a_fm if which == 0 else u_fm
                        dr = (oc * 4 + q) * 128
                        g.dma_start(out=dst[dr:dr + 128,
                                            tt * 128:(tt + 1) * 128],
                                    in_=etf[kq % 2][:]).then_inc(s_st3.h, 16)
                    s_st3.add(16)
    v_st3 = s_st3.v
    sc3.close()

    scS = p.scope()
    SR = [p.sb_in(scS, [128, T], BF16) for _ in range(ND)]
    sc4 = p.scope()
    afs = [p.sb_in(sc4, [128, T], F32) for _ in range(2)]
    ufs = [p.sb_in(sc4, [128, T], F32) for _ in range(2)]
    ac = p.sb_in(sc4, [128, CHUNK], F32)
    cd = p.sb_in(sc4, [128, CHUNK], F32)
    cdc = p.sb_in(sc4, [128, CHUNK], F32)
    u2 = p.sb_in(sc4, [128, CHUNK], F32)
    ci = p.sb_in(sc4, [128, CHUNK], F32)
    stf = p.sb_in(sc4, [128, CHUNK], F32)
    st = p.sb_in(sc4, [128, 1], F32)
    stall = p.sb_in(sc4, [128, ND], F32)
    stin = p.sb_in(sc4, [128, 1], F32)
    dum4 = p.sb_in(sc4, [1, 1], F32)
    s_sl = p.sem()
    s_sc = p.sem()
    s_agd = p.sem()
    s_agc = p.sem()
    s_sti = p.sem()

    def scan_pass(pass_no, stin_vals=None):
        for c in range(ND):
            li = pass_no * ND + c

            @p.on("sync")
            def _(sy, c=c, li=li):
                if li == 0:
                    sy.wait_ge(s_st3.h, v_st3)
                if li >= 1:
                    sy.wait_ge(s_sl.h, 32 * li)
                if li >= 2:
                    sy.wait_ge(s_sc.h, li - 1)
                sy.dma_start(out=afs[li % 2][:],
                             in_=a_fm[c * 128:(c + 1) * 128, :]
                             ).then_inc(s_sl.h, 16)
                sy.dma_start(out=ufs[li % 2][:],
                             in_=u_fm[c * 128:(c + 1) * 128, :]
                             ).then_inc(s_sl.h, 16)
            s_sl.add(32)
            v_sl = s_sl.v

            fl0 = fl.v
            sv = stin_vals[c] if stin_vals else None

            @p.on("vector")
            def _(v, c=c, li=li, v_sl=v_sl, fl0=fl0, pass_no=pass_no, sv=sv):
                v.wait_ge(s_sl.h, v_sl)
                v.wait_ge(p.s_cpool.h, 16)
                if sv is not None:
                    v.wait_ge(s_sti.h, sv)
                k = [fl0]

                def st_(ins):
                    k[0] += 1
                    ins.then_inc(fl.h, 1)
                    v.wait_ge(fl.h, k[0])
                if pass_no == 0:
                    st_(v.memset(st[:], 0.0))
                else:
                    st_(v.tensor_tensor(st[:], stin[:], iso_pp[:], ALU.mult))
                a_, u_ = afs[li % 2], ufs[li % 2]
                for ch in range(NCH):
                    w0 = ch * CHUNK
                    st_(v.tensor_scalar(ac[:], a_[:, w0:w0 + CHUNK], 1e-10,
                                        0.0, ALU.max, ALU.add))
                    st_(v.tensor_tensor_scan(cd[:], ac[:], ac[:],
                                             p.constf(1.0), ALU.mult,
                                             ALU.bypass))
                    st_(v.tensor_scalar(cdc[:], cd[:], 1e-10, 0.0,
                                        ALU.max, ALU.add))
                    st_(v.reciprocal(cdc[:], cdc[:]))
                    st_(v.tensor_tensor(u2[:], u_[:, w0:w0 + CHUNK], cdc[:],
                                        ALU.mult))
                    st_(v.tensor_tensor_scan(ci[:], u2[:], u2[:],
                                             p.constf(0.0), ALU.add,
                                             ALU.bypass))
                    st_(v.scalar_tensor_tensor(stf[:], ci[:], st[:], cd[:],
                                               ALU.add, ALU.mult))
                    st_(v.tensor_copy(st[:], stf[:, CHUNK - 1:CHUNK]))
                    st_(v.tensor_copy(SR[c][:, w0:w0 + CHUNK], stf[:]))
                if pass_no == 0:
                    st_(v.tensor_copy(stall[:, c:c + 1], st[:]))
                v.memset(dum4[:], 0.0).then_inc(s_sc.h, 1)
            fl.add(1 + NCH * 9 + (1 if pass_no == 0 else 0))
            s_sc.add()

    scan_pass(0)
    v_p1 = s_sc.v

    for c in range(ND):
        @p.on("gpsimd")
        def _(g, c=c, v_p1=v_p1):
            if c == 0:
                g.wait_ge(s_sc.h, v_p1)
            else:
                g.wait_ge(s_agd.h, 16 * c)
            g.dma_start(out=ag_in[0:1, c * 128:(c + 1) * 128],
                        in_=stall[:, c:c + 1]).then_inc(s_agd.h, 16)
        s_agd.add(16)

    @p.on("gpsimd")
    def _(g, v_agd=s_agd.v):
        g.wait_ge(s_agd.h, v_agd)
        g.collective_compute(
            "AllGather", ALU.bypass, replica_groups=groups,
            ins=[ag_in.opt()], outs=[ag_out.opt()],
        ).then_inc(s_agc.h, 1)
    s_agc.add(1)

    stin_vals = []
    for c in range(ND):
        @p.on("gpsimd")
        def _(g, c=c, v0=s_sti.v):
            g.wait_ge(s_agc.h, 1)
            if c >= 1:
                g.wait_ge(s_sti.h, v0)
                g.wait_ge(s_sc.h, ND + c)
            g.dma_start(out=stin[:], in_=ag_out[0:1, c * 128:(c + 1) * 128]
                        ).then_inc(s_sti.h, 16)
        stin_vals.append(s_sti.add(16))
    scan_pass(1, stin_vals)
    v_scan = s_sc.v
    sc4.close()

    sc5 = p.scope()
    wo_sb = [p.sb_in(sc5, [128, D], BF16) for _ in range(ND)]
    s_w5 = emit_weight_load_bf16(p, woT_in, wo_sb, wait=[(s_sc, v_scan)])
    po = [p.ps_in(sc5, [128, 512], F32) for _ in range(2)]
    s_mm5 = p.sem()
    s_ev5 = p.sem()
    for tt in range(NT):
        for oc in range(2):
            g_idx = tt * 2 + oc

            @p.on("tensor")
            def _(t, tt=tt, oc=oc, g_idx=g_idx):
                if g_idx == 0:
                    t.wait_ge(s_sc.h, v_scan)
                    t.wait_ge(s_w5.h, 16 * ND)
                if g_idx >= 2:
                    t.wait_ge(s_ev5.h, g_idx - 1)
                ins = None
                for c in range(ND):
                    ins = t.matmul(po[g_idx % 2][:],
                                   SR[c][:, tt * 128:(tt + 1) * 128],
                                   wo_sb[c][:, oc * 512:(oc + 1) * 512],
                                   start=(c == 0), stop=(c == ND - 1))
                ins.then_inc(s_mm5.h, 1)
            v_mm = s_mm5.add()

            @p.on("vector")
            def _(v, tt=tt, oc=oc, g_idx=g_idx, v_mm=v_mm):
                v.wait_ge(s_mm5.h, v_mm)
                v.tensor_tensor(x_t[tt][:, oc * 512:(oc + 1) * 512],
                                x_t[tt][:, oc * 512:(oc + 1) * 512],
                                po[g_idx % 2][:], ALU.add
                                ).then_inc(s_ev5.h, 1)
            s_ev5.add()
    v_x2 = s_ev5.v
    if DEBUG_B1:
        s_db5 = p.sem()

        @p.on("gpsimd")
        def _(g):
            g.wait_ge(s_ev5.h, v_x2)
            g.dma_start(out=dbg_afm, in_=a_fm[0:128, :]).then_inc(s_db5.h, 16)
            g.wait_ge(s_db5.h, 16)
            g.dma_start(out=dbg_ufm, in_=u_fm[0:128, :]).then_inc(s_db5.h, 16)
            g.wait_ge(s_db5.h, 32)
            g.dma_start(out=dbg_sr, in_=SR[0][:]).then_inc(s_db5.h, 16)
            g.wait_ge(s_db5.h, 48)
            g.dma_start(out=dbg_x2, in_=x_t[0][:]).then_inc(s_db5.h, 16)
            g.wait_ge(s_db5.h, 64)
        s_db5.add(64)
    sc5.close()
    scS.close()
    scC.close()

    scH = p.scope()
    HN = [p.sb_in(scH, [128, T], BF16) for _ in range(ND)]
    sc6 = p.scope()
    xn6 = [p.sb_in(sc6, [128, D], F32) for _ in range(2)]
    ptf6 = [p.ps_in(sc6, [128, 128], BF16) for _ in range(2)]
    xnb6 = p.sb_in(sc6, [128, D], BF16)
    s_tm6 = p.sem()
    s_tp6 = p.sem()

    def at6(i, s_fin, fin_v):
        c_wm = chain(p, "vector", [(s_fin, fin_v)] +
                     ([(s_tp6, i * ND)] if i >= 1 else []), [
            lambda v, i=i: v.tensor_tensor(xnb6[:], xn6[i % 2][:],
                                           fnw_bc[:], ALU.mult)])
        for c in range(ND):
            k = i * ND + c

            @p.on("tensor")
            def _(t, i=i, c=c, k=k, c_wm=c_wm):
                t.wait_ge(c_wm[0].h, c_wm[1])
                if k >= 2:
                    t.wait_ge(s_tp6.h, k - 1)
                t.transpose(ptf6[k % 2][:], xnb6[:, c * 128:(c + 1) * 128],
                            identb[:]).then_inc(s_tm6.h, 1)
            v_tm = s_tm6.add()

            @p.on("vector")
            def _(v, i=i, c=c, k=k, v_tm=v_tm):
                v.wait_ge(s_tm6.h, v_tm)
                v.tensor_copy(HN[c][:, i * 128:(i + 1) * 128],
                              ptf6[k % 2][:]).then_inc(s_tp6.h, 1)
            s_tp6.add()

    s_ln6 = emit_ln(p, x_t, [xn6[i % 2] for i in range(NT)],
                    lambda i: [(s_ev5, v_x2)] +
                              ([(s_tp6, (i - 1) * ND)] if i >= 2 else []),
                    rms=True, eps=1e-6, sc=sc6, nb=1, after_tile=at6)
    v_hn = NT * ND
    sc6.close()

    sc7 = p.scope()
    wg7 = [p.sb_in(sc7, [128, 512], BF16) for _ in range(ND)]
    wu7 = [p.sb_in(sc7, [128, 512], BF16) for _ in range(ND)]
    pg7 = [p.ps_in(sc7, [128, 512], F32) for _ in range(2)]
    pu7 = [p.ps_in(sc7, [128, 512], F32) for _ in range(2)]
    ptb7 = [p.ps_in(sc7, [128, 128], BF16) for _ in range(2)]
    sgb = p.sb_in(sc7, [128, 512], F32)
    hbb = [p.sb_in(sc7, [128, 512], BF16) for _ in range(2)]
    etb7 = [p.sb_in(sc7, [128, 128], BF16) for _ in range(2)]
    s_wl7 = p.sem()
    s_mm7 = p.sem()
    s_sg7 = p.sem()
    s_hb7 = p.sem()
    s_tm7 = p.sem()
    s_te7 = p.sem()
    s_st7 = p.sem()
    NJ = FFNB // 512
    for j in range(NJ):
        @p.on("gpsimd")
        def _(sy, j=j):
            if j == 0:
                sy.wait_ge(s_tp6.h, v_hn)
            if j >= 1:
                sy.wait_ge(s_wl7.h, 16 * 2 * ND * j)
                sy.wait_ge(s_mm7.h, 2 * NT * j)
            for c in range(ND):
                if j or c:
                    sy.wait_ge(s_wl7.h, 16 * (2 * ND * j + c))
                sy.dma_start(out=wg7[c][:],
                             in_=wguT_in[c * 128:(c + 1) * 128,
                                         j * 512:(j + 1) * 512]
                             ).then_inc(s_wl7.h, 16)
            for c in range(ND):
                sy.wait_ge(s_wl7.h, 16 * (2 * ND * j + ND + c))
                sy.dma_start(out=wu7[c][:],
                             in_=wguT_in[c * 128:(c + 1) * 128,
                                         FFNB + j * 512:FFNB + (j + 1) * 512]
                             ).then_inc(s_wl7.h, 16)
        s_wl7.add(16 * 2 * ND)
        v_wl = s_wl7.v

        for tt in range(NT):
            g_idx = j * NT + tt

            @p.on("tensor")
            def _(t, j=j, tt=tt, g_idx=g_idx, v_wl=v_wl):
                if g_idx == 0:
                    t.wait_ge(s_tp6.h, v_hn)
                t.wait_ge(s_wl7.h, v_wl)
                if g_idx >= 2:
                    t.wait_ge(s_hb7.h, g_idx - 1)
                for wsb, ps in ((wg7, pg7), (wu7, pu7)):
                    ins = None
                    for c in range(ND):
                        ins = t.matmul(ps[g_idx % 2][:],
                                       HN[c][:, tt * 128:(tt + 1) * 128],
                                       wsb[c][:], start=(c == 0),
                                       stop=(c == ND - 1))
                    ins.then_inc(s_mm7.h, 1)
            v_mm = s_mm7.add(2)

            @p.on("scalar")
            def _(s, g_idx=g_idx, v_mm=v_mm):
                s.wait_ge(s_mm7.h, v_mm)
                s.activation(sgb[:], pg7[g_idx % 2][:], AF.Silu
                             ).then_inc(s_sg7.h, 1)
            v_sg = s_sg7.add()

            @p.on("vector")
            def _(v, g_idx=g_idx, v_sg=v_sg):
                v.wait_ge(s_sg7.h, v_sg)
                if g_idx >= 2:
                    v.wait_ge(s_te7.h, 4 * (g_idx - 1))
                v.tensor_tensor(hbb[g_idx % 2][:], sgb[:], pu7[g_idx % 2][:],
                                ALU.mult).then_inc(s_hb7.h, 1)
            v_hb = s_hb7.add()

            for q in range(4):
                kq = g_idx * 4 + q

                @p.on("tensor")
                def _(t, g_idx=g_idx, q=q, kq=kq, v_hb=v_hb):
                    t.wait_ge(s_hb7.h, v_hb)
                    if kq >= 2:
                        t.wait_ge(s_te7.h, kq - 1)
                    t.transpose(ptb7[kq % 2][:],
                                hbb[g_idx % 2][:, q * 128:(q + 1) * 128],
                                identb[:]).then_inc(s_tm7.h, 1)
                v_tm = s_tm7.add()

                @p.on("vector")
                def _(v, kq=kq, v_tm=v_tm):
                    v.wait_ge(s_tm7.h, v_tm)
                    if kq >= 2:
                        v.wait_ge(s_st7.h, 16 * (kq - 1))
                    v.tensor_copy(etb7[kq % 2][:], ptb7[kq % 2][:]
                                  ).then_inc(s_te7.h, 1)
                v_te = s_te7.add()

                @p.on("gpsimd")
                def _(g, j=j, tt=tt, q=q, kq=kq, v_te=v_te):
                    g.wait_ge(s_te7.h, v_te)
                    if kq >= 1:
                        g.wait_ge(s_st7.h, 16 * kq)
                    fr = (j * 4 + q) * 128
                    g.dma_start(out=hb_fm[fr:fr + 128,
                                          tt * 128:(tt + 1) * 128],
                                in_=etb7[kq % 2][:]).then_inc(s_st7.h, 16)
                s_st7.add(16)
    v_hbst = s_st7.v
    sc7.close()
    scH.close()

    sc8 = p.scope()
    NF7 = FFNB // 128
    wd7 = [p.sb_in(sc8, [128, D], BF16) for _ in range(NF7)]
    hbc = [p.sb_in(sc8, [128, 128], BF16) for _ in range(NF7)]
    pd7 = [p.ps_in(sc8, [128, 512], F32) for _ in range(2)]
    yt7 = [p.sb_in(sc8, [128, 512], F32) for _ in range(2)]
    xo8 = [p.sb_in(sc8, [128, D], F32) for _ in range(2)]
    s_hl8 = p.sem()
    s_xo8 = p.sem()
    s_mm8 = p.sem()
    s_ev8 = p.sem()
    s_yst = p.sem()
    s_wl8 = emit_weight_load_bf16(p, wdnT_in, wd7, wait=[(s_st7, v_hbst)])

    for tt in range(NT):
        @p.on("sync")
        def _(sy, tt=tt):
            if tt == 0:
                sy.wait_ge(s_st7.h, v_hbst)
            else:
                sy.wait_ge(s_hl8.h, 16 * NF7 * tt)
                sy.wait_ge(s_mm8.h, 2 * tt)
            for c in range(NF7):
                if c >= 1:
                    sy.wait_ge(s_hl8.h, 16 * (NF7 * tt + c))
                sy.dma_start(out=hbc[c][:],
                             in_=hb_fm[c * 128:(c + 1) * 128,
                                       tt * 128:(tt + 1) * 128]
                             ).then_inc(s_hl8.h, 16)
        s_hl8.add(16 * NF7)
        v_hl = s_hl8.v

        @p.on("sync")
        def _(sy, tt=tt):
            if tt >= 1:
                sy.wait_ge(s_xo8.h, 16 * tt)
            if tt >= 2:
                sy.wait_ge(s_yst.h, 32 * (tt - 1))
            sy.dma_start(out=xo8[tt % 2][:], in_=x_in[tt * 128:(tt + 1) * 128, :]
                         ).then_inc(s_xo8.h, 16)
        s_xo8.add(16)
        v_xo = s_xo8.v

        for dc in range(2):
            g_idx = tt * 2 + dc

            @p.on("tensor")
            def _(t, tt=tt, dc=dc, g_idx=g_idx, v_hl=v_hl):
                t.wait_ge(s_hl8.h, v_hl)
                if g_idx == 0:
                    t.wait_ge(s_wl8.h, 16 * NF7)
                if g_idx >= 2:
                    t.wait_ge(s_ev8.h, g_idx - 1)
                ins = None
                for c in range(NF7):
                    ins = t.matmul(pd7[dc][:], hbc[c][:],
                                   wd7[c][:, dc * 512:(dc + 1) * 512],
                                   start=(c == 0), stop=(c == NF7 - 1))
                ins.then_inc(s_mm8.h, 1)
            v_mm = s_mm8.add()

            @p.on("vector")
            def _(v, tt=tt, dc=dc, g_idx=g_idx, v_mm=v_mm):
                v.wait_ge(s_mm8.h, v_mm)
                if g_idx >= 2:
                    v.wait_ge(s_yst.h, 16 * (g_idx - 1))
                v.tensor_tensor(yt7[dc][:], pd7[dc][:],
                                x_t[tt][:, dc * 512:(dc + 1) * 512],
                                ALU.add).then_inc(s_ev8.h, 1)
            v_ev = s_ev8.add()

            gfl8 = p.flu["gpsimd"].add()

            @p.on("gpsimd")
            def _(g, tt=tt, dc=dc, g_idx=g_idx, v_ev=v_ev, v_xo=v_xo,
                  gfl8=gfl8):
                g.wait_ge(s_ev8.h, v_ev)
                g.wait_ge(s_xo8.h, v_xo)
                g.tensor_tensor(yt7[dc][:], yt7[dc][:],
                                xo8[tt % 2][:, dc * 512:(dc + 1) * 512],
                                ALU.add).then_inc(p.flu["gpsimd"].h, 1)
                g.wait_ge(p.flu["gpsimd"].h, gfl8)
                if g_idx >= 1:
                    g.wait_ge(s_yst.h, 16 * g_idx)
                g.dma_start(out=y_out[tt * 128:(tt + 1) * 128,
                                      dc * 512:(dc + 1) * 512],
                            in_=yt7[dc][:]).then_inc(s_yst.h, 16)
            s_yst.add(16)
    sc8.close()
    if DEBUG_B1:
        s_db8 = p.sem()

        @p.on("gpsimd")
        def _(g):
            g.wait_ge(s_yst.h, 32 * NT)
            g.dma_start(out=dbg_hbfm, in_=hb_fm[0:128, :]).then_inc(s_db8.h, 16)
            g.wait_ge(s_db8.h, 16)
        s_db8.add(16)

    @p.on("gpsimd")
    def _(g):
        g.wait_ge(s_yst.h, 32 * NT)




DBG_B0_SKIP = False
DBG_B0_PHASE = 0


def emit_b0(nc, p, io):
    x_in = io.inp("x", [T, D])
    xprev_in = io.inp("xprev", [63, D])
    iseven_in = io.inp("is_even", [1, 1])
    wdispT_in = io.inp("wdispT", [256, D])
    wfu_in = io.inp("wfu", [16, 32])
    bfu_in = io.inp("bfu", [1, 32])
    rpeT_in = io.inp("rpeT", [16, 64])
    wpos_in = io.inp("wpos", [16, 16])
    wbdm_in = io.inp("wbdm", [128, 8], BF16)
    bbond_in = io.inp("b_bond", [1, 1])
    bdmg_in = io.inp("b_dmg", [1, 1])
    wvT_in = io.inp("wvT", [D, D])
    bval_in = io.inp("b_val", [1, D])
    wcT_in = io.inp("wcT", [D, D])
    bcp_in = io.inp("b_cproj", [1, D])
    identb_in = io.inp("ident_bf", [128, 128], BF16)
    i2_in = io.inp("ident2", [128, 256], BF16)
    y_out = io.out("y", [T, D])

    GW = 512
    NFM = 4
    ash = io.internal("ash", [16 * 128 * 196 + 256], BF16)
    pfb_scr = io.internal("pfb_scr", [128, 64])
    fl = p.flu["vector"]
    gfl = p.flu["gpsimd"]
    sfl = p.flu["scalar"]

    identb = p.sb([128, 128], BF16)
    wbdm = p.sb([128, 8], BF16)
    bval_bc = p.sb([128, D], BF16)
    bcp_bc = p.sb([128, D], BF16)
    bb_pp = p.sb([128, 1], F32)
    bd_pp = p.sb([128, 1], F32)
    ise_pp = p.sb([128, 1], F32)
    PFB = p.sb([128, 64], F32)
    s_misc = p.sem()

    @p.on("gpsimd")
    def _(g):
        n = [0]
        def nx(ins):
            n[0] += 16
            ins.then_inc(s_misc.h, 16)
            g.wait_ge(s_misc.h, n[0])
        nx(g.dma_start(out=identb[:], in_=identb_in))
        nx(g.dma_start(out=wbdm[:], in_=wbdm_in))
        nx(g.dma_start(out=bval_bc[:], in_=bval_in.broadcast_to((128, D))))
        nx(g.dma_start(out=bcp_bc[:], in_=bcp_in.broadcast_to((128, D))))
        nx(g.dma_start(out=bb_pp[:], in_=bbond_in.broadcast_to((128, 1))))
        nx(g.dma_start(out=bd_pp[:], in_=bdmg_in.broadcast_to((128, 1))))
        nx(g.dma_start(out=ise_pp[:], in_=iseven_in.broadcast_to((128, 1))))
        if DBG_B0_SKIP:
            pass
        else:
            nx(g.dma_start(out=pfb_scr[16:32, :].rearrange("a b -> b a"),
                           in_=bfu_in[0:1, 16:32].broadcast_to((64, 16)),
                           allow_slow_non_contiguous=True))
        if not DBG_B0_SKIP:
            nx(g.dma_start(out=pfb_scr[48:64, :].rearrange("a b -> b a"),
                           in_=bfu_in[0:1, 16:32].broadcast_to((64, 16)),
                           allow_slow_non_contiguous=True))
        if not DBG_B0_SKIP:
            nx(g.dma_start(out=pfb_scr[80:96, :].rearrange("a b -> b a"),
                           in_=bfu_in[0:1, 16:32].broadcast_to((64, 16)),
                           allow_slow_non_contiguous=True))
        if not DBG_B0_SKIP:
            nx(g.dma_start(out=pfb_scr[112:128, :].rearrange("a b -> b a"),
                           in_=bfu_in[0:1, 16:32].broadcast_to((64, 16)),
                           allow_slow_non_contiguous=True))
    s_misc.add(16 * (11 - (4 if DBG_B0_SKIP else 0)))
    v_misc = s_misc.v

    sc0 = p.scope()
    rpe_sb = p.sb_in(sc0, [128, 64], F32)
    wpos_sb = p.sb_in(sc0, [16, 16], F32)
    bfub = p.sb_in(sc0, [16, 1], F32)
    pfev = p.sb_in(sc0, [128, 64], F32)
    pftp = p.sb_in(sc0, [16, 64], F32)
    identf0 = p.sb_in(sc0, [128, 128], F32)
    pp0 = p.ps_in(sc0, [128, 512], F32)
    pp1f = p.ps_in(sc0, [128, 512], F32)
    pp1 = pp1f[0:16, 0:128]
    s_p0 = p.sem()

    @p.on("sync")
    def _(sy):
        sy.dma_start(out=rpe_sb[0:16, :], in_=rpeT_in).then_inc(s_p0.h, 16)
        sy.wait_ge(s_p0.h, 16)
        sy.dma_start(out=wpos_sb[:], in_=wpos_in).then_inc(s_p0.h, 16)
        sy.wait_ge(s_p0.h, 32)
        sy.dma_start(out=bfub[:], in_=bfu_in[0:1, 0:16]).then_inc(s_p0.h, 16)
    s_p0.add(48)

    @p.on("gpsimd")
    def _(g):
        g.wait_ge(s_misc.h, v_misc)
        g.memset(identf0[:], 1.0).then_inc(gfl.h, 1)
    vg2 = gfl.add()

    s_idf = p.sem()

    @p.on("gpsimd")
    def _(g, vg2=vg2):
        g.wait_ge(gfl.h, vg2)
        g.affine_select(out=identf0[:], in_=identf0[:], pattern=[[-1, 128]],
                        compare_op=ALU.is_equal, fill=0.0, base=0,
                        channel_multiplier=1).then_inc(s_idf.h, 1)
    s_idf.add(1)

    s_pf = p.sem()

    @p.on("tensor")
    def _(t):
        t.wait_ge(s_p0.h, 48)
        t.wait_ge(s_idf.h, 1)
        t.matmul(pp0[0:64, :16], rpe_sb[0:16, :], wpos_sb[:],
                 start=True, stop=True).then_inc(s_pf.h, 1)
    v1 = s_pf.add()
    c_e0 = chain(p, "vector", [(s_pf, v1)], [
        lambda v: v.tensor_copy(pfev[0:64, :16], pp0[0:64, :16])])

    @p.on("tensor")
    def _(t, c_e0=c_e0):
        t.wait_ge(c_e0[0].h, c_e0[1])
        t.transpose(pp1, pfev[:, 0:16], identf0[:]).then_inc(s_pf.h, 1)
    v2 = s_pf.add()
    c_e1 = chain(p, "vector", [(s_pf, v2)], [
        lambda v: v.tensor_copy(pftp[:], pp1f[0:16, 0:64]),
        lambda v: v.scalar_tensor_tensor(pftp[:], pftp[:], bfub[:],
                                         pftp[:], ALU.add, ALU.bypass),
    ], flush=[True, True])

    s_pfd = p.sem()

    @p.on("gpsimd")
    def _(g, c_e1=c_e1):
        g.wait_ge(c_e1[0].h, c_e1[1])
        for hh in range(4):
            if hh >= 1:
                g.wait_ge(s_pfd.h, 16 * hh)
            g.dma_start(out=pfb_scr[hh * 32:hh * 32 + 16, :], in_=pftp[:]
                        ).then_inc(s_pfd.h, 16)
        g.wait_ge(s_pfd.h, 64)
        g.dma_start(out=PFB[:], in_=pfb_scr).then_inc(s_pfd.h, 16)
    s_pfd.add(80)
    v_pfb = s_pfd.v
    sc0.close()

    G_FM = [p.sb([128, 63 + T], BF16) for _ in range(NFM)]
    VALT = [p.sb([128, D], BF16) for _ in range(NT + 1)]
    VALB = [p.sb([128, D], BF16) for _ in range(NT + 1)]
    i2_sb = p.sb([128, 256], BF16)
    scX = p.scope()
    XFM = [p.sb_in(scX, [128, T], BF16) for _ in range(ND)]
    XPFM = [p.sb_in(scX, [128, 128], BF16) for _ in range(ND)]
    sc1 = p.scope()
    x_t = [p.sb_in(sc1, [128, D], F32) for _ in range(NT)]
    xpt = p.sb_in(sc1, [128, D], F32)
    xnr = [p.sb_in(sc1, [128, D], BF16) for _ in range(2)]
    ptb_f = [p.ps_in(sc1, [128, 1024], BF16) for _ in range(2)]
    ptb = [ptb_f[0][:, 0:128], ptb_f[1][:, 0:128]]
    s_ld = emit_load_tiles(p, x_in, x_t, waits=[(s_pfd, v_pfb)])
    s_lp = p.sem()
    c_zp = chain(p, "vector", [(s_pfd, v_pfb)], [
        lambda v: v.memset(xpt[:], 0.0)])

    @p.on("sync")
    def _(sy, c_zp=c_zp):
        sy.wait_ge(c_zp[0].h, c_zp[1])
        sy.dma_start(out=xpt[65:128, :], in_=xprev_in).then_inc(s_lp.h, 16)
    s_lp.add(16)

    s_tm1 = p.sem()
    s_tp1 = p.sem()
    s_lnp = emit_ln(p, [xpt], [xpt], lambda i: [(s_lp, 16)], rms=False,
                    eps=1e-5, n_tiles=1, sc=sc1, nb=1)
    c_cvp = chain(p, "vector", [(s_lnp, 1)], [
        lambda v: v.tensor_copy(xnr[0][:], xpt[:])])
    for c in range(ND):
        @p.on("tensor")
        def _(t, c=c, c_cvp=c_cvp):
            if c == 0:
                t.wait_ge(c_cvp[0].h, c_cvp[1])
                t.wait_ge(s_misc.h, 16)
            if c >= 2:
                t.wait_ge(s_tp1.h, c - 1)
            t.transpose(ptb[c % 2][:], xnr[0][:, c * 128:(c + 1) * 128],
                        identb[:]).then_inc(s_tm1.h, 1)
        v_tm = s_tm1.add()

        @p.on("vector")
        def _(v, c=c, v_tm=v_tm):
            v.wait_ge(s_tm1.h, v_tm)
            v.tensor_copy(XPFM[c][:], ptb[c % 2][:]).then_inc(s_tp1.h, 1)
        s_tp1.add()
    base1 = s_tp1.v

    def atX(i, s_fin, fin_v):
        for c in range(ND):
            k = base1 + i * ND + c

            @p.on("tensor")
            def _(t, i=i, c=c, k=k, s_fin=s_fin, fin_v=fin_v):
                t.wait_ge(s_fin.h, fin_v)
                if k >= 2:
                    t.wait_ge(s_tp1.h, k - 1)
                t.transpose(ptb[k % 2][:], xnr[i % 2][:, c * 128:(c + 1) * 128],
                            identb[:]).then_inc(s_tm1.h, 1)
            v_tm = s_tm1.add()

            @p.on("vector")
            def _(v, i=i, c=c, k=k, v_tm=v_tm):
                v.wait_ge(s_tm1.h, v_tm)
                v.tensor_copy(XFM[c][:, i * 128:(i + 1) * 128],
                              ptb[k % 2][:]).then_inc(s_tp1.h, 1)
            s_tp1.add()

    s_ln1 = emit_ln(p, x_t, [xnr[i % 2] for i in range(NT)],
                    lambda i: [(s_ld, 16 * (i + 1))] +
                              ([(s_tp1, base1 + (i - 1) * ND)]
                               if i >= 2 else []),
                    rms=False, eps=1e-5, sc=sc1, nb=1, after_tile=atX)
    v_xfm = base1 + NT * ND
    sc1.close()

    sc2 = p.scope()
    wdisp_sb = [p.sb_in(sc2, [16, 128], F32) for _ in range(2)]
    wfu2 = p.sb_in(sc2, [16, 32], F32)
    wdf = [p.sb_in(sc2, [128, GW], BF16) for _ in range(ND)]
    wv_sb = [p.sb_in(sc2, [128, D], BF16) for _ in range(ND)]
    gtok = p.sb_in(sc2, [128, GW], BF16)
    pw2f = p.ps_in(sc2, [128, 512], F32)
    pw2 = pw2f[:, 0:32]
    pg2_ = p.ps_in(sc2, [128, GW], F32)
    pg2 = [pg2_, pg2_]
    pv2 = [p.ps_in(sc2, [128, 512], F32) for _ in range(4)]
    ptg_f = [p.ps_in(sc2, [128, 1024], BF16) for _ in range(2)]
    ptg = [ptg_f[0][:, 0:128], ptg_f[1][:, 0:128]]
    s_w2 = p.sem()
    s_wm = p.sem()
    s_wev = p.sem()
    s_gm = p.sem()
    s_gev = p.sem()
    s_gtm = p.sem()
    s_gtp = p.sem()
    s_vm = p.sem()
    s_vev = p.sem()

    @p.on("sync")
    def _(sy):
        sy.wait_ge(s_tp1.h, v_xfm)
        sy.dma_start(out=wfu2[:], in_=wfu_in).then_inc(s_w2.h, 16)
        sy.wait_ge(s_w2.h, 16)
        sy.dma_start(out=i2_sb[:], in_=i2_in).then_inc(s_w2.h, 16)
    s_w2.add(32)
    for h in range(16):
        for ic in range(ND):
            k = h * ND + ic

            @p.on("sync")
            def _(sy, h=h, ic=ic, k=k):
                if k == 0:
                    sy.wait_ge(s_tp1.h, v_xfm)
                sy.wait_ge(s_w2.h, 16 * (k + 2))
                if k >= 2:
                    sy.wait_ge(s_wm.h, k - 1)
                sy.dma_start(out=wdisp_sb[k % 2][:],
                             in_=wdispT_in[h * 16:(h + 1) * 16,
                                           ic * 128:(ic + 1) * 128]
                             ).then_inc(s_w2.h, 16)
            v_l = s_w2.add(16)
            if True:
                pass

            @p.on("tensor")
            def _(t, h=h, ic=ic, k=k, v_l=v_l):
                t.wait_ge(s_w2.h, v_l)
                if k >= 1:
                    t.wait_ge(s_wev.h, k)
                t.matmul(pw2, wdisp_sb[k % 2][:], wfu2[:],
                         start=True, stop=True).then_inc(s_wm.h, 1)
            v_m = s_wm.add()

            @p.on("vector")
            def _(v, h=h, ic=ic, v_m=v_m):
                v.wait_ge(s_wm.h, v_m)
                v.tensor_copy(wdf[ic][:, h * 32:(h + 1) * 32], pw2
                              ).then_inc(s_wev.h, 1)
            s_wev.add()
    v_wdf = s_wev.v
    s_wv = emit_weight_load_bf16(p, wvT_in, wv_sb,
                                 wait=[(s_misc, v_misc), (s_tp1, v_xfm)])

    for gi in range(NT + 1):
        @p.on("tensor")
        def _(t, gi=gi):
            if gi == 0:
                t.wait_ge(s_tp1.h, v_xfm)
                t.wait_ge(s_wev.h, v_wdf)
                t.wait_ge(s_wv.h, 16 * ND)
            if gi >= 1:
                t.wait_ge(s_gev.h, gi)
            src = XPFM if gi == 0 else XFM
            off = 0 if gi == 0 else (gi - 1) * 128
            ins = None
            for c in range(ND):
                sl = src[c][:, :] if gi == 0 else src[c][:, off:off + 128]
                ins = t.matmul(pg2[gi % 2][:], sl, wdf[c][:],
                               start=(c == 0), stop=(c == ND - 1))
            ins.then_inc(s_gm.h, 1)
        v_gm = s_gm.add()

        @p.on("vector")
        def _(v, gi=gi, v_gm=v_gm):
            v.wait_ge(s_gm.h, v_gm)
            if gi >= 2:
                v.wait_ge(s_gtm.h, (gi - 1) * NFM)
            v.tensor_copy(gtok[:], pg2[gi % 2][:]).then_inc(s_gev.h, 1)
        v_ge = s_gev.add()

        for fc in range(NFM):
            k = gi * NFM + fc

            @p.on("tensor")
            def _(t, gi=gi, fc=fc, k=k, v_ge=v_ge):
                t.wait_ge(s_gev.h, v_ge)
                if k >= 2:
                    t.wait_ge(s_gtp.h, k - 1)
                t.transpose(ptg[k % 2], gtok[:, fc * 128:(fc + 1) * 128],
                            identb[:]).then_inc(s_gtm.h, 1)
            v_tm = s_gtm.add()

            @p.on("vector")
            def _(v, gi=gi, fc=fc, k=k, v_tm=v_tm):
                v.wait_ge(s_gtm.h, v_tm)
                if gi == 0:
                    v.tensor_copy(G_FM[fc][:, 0:63],
                                  ptg_f[k % 2][:, 65:128]).then_inc(s_gtp.h, 1)
                else:
                    v.tensor_copy(G_FM[fc][:, 63 + (gi - 1) * 128:
                                           63 + gi * 128],
                                  ptg[k % 2]).then_inc(s_gtp.h, 1)
            s_gtp.add()

        for oc in range(2):
            @p.on("tensor")
            def _(t, gi=gi, oc=oc):
                if gi == 0:
                    t.wait_ge(s_wev.h, v_wdf)
                if gi * 2 + oc >= 4:
                    t.wait_ge(s_vev.h, gi * 2 + oc - 3)
                src = XPFM if gi == 0 else XFM
                off = 0 if gi == 0 else (gi - 1) * 128
                ins = None
                for c in range(ND):
                    sl = src[c][:, :] if gi == 0 else src[c][:, off:off + 128]
                    ins = t.matmul(pv2[(gi * 2 + oc) % 4][:], sl,
                                   wv_sb[c][:, oc * 512:(oc + 1) * 512],
                                   start=(c == 0), stop=(c == ND - 1))
                ins.then_inc(s_vm.h, 1)
            v_vm = s_vm.add()

            @p.on("vector")
            def _(v, gi=gi, oc=oc, v_vm=v_vm):
                v.wait_ge(s_vm.h, v_vm)
                v.tensor_tensor(VALT[gi][:, oc * 512:(oc + 1) * 512],
                                pv2[(gi * 2 + oc) % 4][:],
                                bval_bc[:, oc * 512:(oc + 1) * 512],
                                ALU.add).then_inc(s_vev.h, 1)
            s_vev.add()
    v_gfm = s_gtp.v
    v_val = s_vev.v

    s_bm = p.sem()
    s_bev = p.sem()
    for i in range(NT + 1):
        for nc2 in range(2):
            kb = i * 2 + nc2

            @p.on("tensor")
            def _(t, i=i, nc2=nc2, kb=kb):
                if kb <= 1:
                    t.wait_ge(s_vev.h, v_val)
                if kb == 0:
                    t.wait_ge(s_w2.h, 32)
                if kb >= 4:
                    t.wait_ge(s_bev.h, kb - 3)
                ins = t.matmul(pv2[kb % 4][:], i2_sb[:, 128:256],
                               VALT[i][:, nc2 * 512:(nc2 + 1) * 512],
                               start=True, stop=(i == NT))
                if i < NT:
                    ins = t.matmul(pv2[kb % 4][:], i2_sb[:, 0:128],
                                   VALT[i + 1][:, nc2 * 512:(nc2 + 1) * 512],
                                   start=False, stop=True)
                ins.then_inc(s_bm.h, 1)
            v_bm = s_bm.add()

            @p.on("vector")
            def _(v, i=i, nc2=nc2, v_bm=v_bm):
                v.wait_ge(s_bm.h, v_bm)
                if i == NT:
                    v.tensor_copy(VALB[i][0:64, nc2 * 512:(nc2 + 1) * 512],
                                  pv2[(i * 2 + nc2) % 4][0:64, :]
                                  ).then_inc(s_bev.h, 1)
                else:
                    v.tensor_copy(VALB[i][:, nc2 * 512:(nc2 + 1) * 512],
                                  pv2[(i * 2 + nc2) % 4][:]
                                  ).then_inc(s_bev.h, 1)
            s_bev.add()
    v_vb = s_bev.v
    if DBG_B0_PHASE == 2:
        s_dy = p.sem()

        @p.on("gpsimd")
        def _(g):
            g.wait_ge(s_bev.h, v_vb)
            g.wait_ge(s_gtp.h, v_gfm)
            for i in range(NT):
                if i:
                    g.wait_ge(s_dy.h, 16 * i)
                g.dma_start(out=y_out[i * 128:(i + 1) * 128, :],
                            in_=VALT[i + 1][:]).then_inc(s_dy.h, 16)
            g.wait_ge(s_dy.h, 16 * NT)
        s_dy.add(16 * NT)
        sc2.close()
        scX.close()
        return
    sc2.close()
    scX.close()

    zfill = io.inp("zrow", [1, 256])
    sc3 = p.scope()
    dif = [p.sb_in(sc3, [128, 128], F32) for _ in range(2)]
    gel = [p.sb_in(sc3, [128, 128], BF16) for _ in range(2)]
    S_ALL = p.sb_in(sc3, [128, 1024], F32)
    DMG = p.sb_in(sc3, [128, 1024], F32)
    M0 = p.sb_in(sc3, [128, 64], F32)
    m16 = p.sb_in(sc3, [128, 16], F32)
    den16 = p.sb_in(sc3, [128, 16], F32)
    A_E = p.sb_in(sc3, [128, 1024], F32)
    A_N = p.sb_in(sc3, [128, 1024], BF16)
    asht = [p.sb_in(sc3, [128, 16 * 195], BF16) for _ in range(2)]
    at0 = [p.sb_in(sc3, [128, 128], BF16) for _ in range(2)]
    at1 = [p.sb_in(sc3, [128, 128], BF16) for _ in range(2)]
    out_t = [p.sb_in(sc3, [128, D], F32) for _ in range(2)]
    obfb = p.sb_in(sc3, [128, D], BF16)
    ofm8 = [p.sb_in(sc3, [128, 128], BF16) for _ in range(ND)]
    dum3 = p.sb_in(sc3, [1, 1], F32)
    wc_sb = [p.sb_in(sc3, [128, D], BF16) for _ in range(ND)]
    xr3 = [p.sb_in(sc3, [128, D], F32) for _ in range(2)]
    yt3 = [p.sb_in(sc3, [128, 512], F32) for _ in range(2)]
    psg = [p.ps_in(sc3, [128, 512], F32) for _ in range(2)]
    pat = [p.ps_in(sc3, [128, 1024], BF16) for _ in range(2)]
    pta = [pat[0][:, 0:128], pat[1][:, 0:128]]
    pvvf = p.ps_in(sc3, [128, 512], F32)
    pvv = [pvvf[:, 0:64], pvvf[:, 64:128]]
    ptof = p.ps_in(sc3, [128, 1024], BF16)
    pto = [ptof[:, 0:128], ptof[:, 128:256]]
    pcp = [p.ps_in(sc3, [128, 512], F32) for _ in range(2)]
    s_wc = emit_weight_load_bf16(p, wcT_in, wc_sb, wait=[(s_vev, v_val)])
    s_df = p.sem()
    s_gl = p.sem()
    s_dot = p.sem()
    s_sev = p.sem()
    s_sm = p.sem()
    s_spr = p.sem()
    s_arl = p.sem()
    s_atm = p.sem()
    s_ate = p.sem()
    s_vvm = p.sem()
    s_ovev = p.sem()
    s_otm = p.sem()
    s_otp = p.sem()
    s_cm = p.sem()
    s_cev = p.sem()
    s_xr3 = p.sem()
    s_yst = p.sem()
    s_zf = p.sem()
    s_m0 = p.sem()

    @p.on("gpsimd")
    def _(g):
        g.wait_ge(s_pfd.h, v_pfb)
        g.dma_start(out=ash[0:16 * 128 * 196].rearrange("(t q) -> t q", q=196),
                    in_=zfill[0:1, 0:196].broadcast_to((16 * 128, 196))
                    ).then_inc(s_zf.h, 16)
        g.memset(M0[:], 0.0).then_inc(gfl.h, 1)
    vgm = gfl.add()

    @p.on("gpsimd")
    def _(g, vgm=vgm):
        g.wait_ge(gfl.h, vgm)
        g.affine_select(out=M0[:], in_=M0[:], pattern=[[1, 64]],
                        compare_op=ALU.is_ge, fill=-1e30, base=-63,
                        channel_multiplier=1).then_inc(s_m0.h, 1)
    s_zf.add(16)
    s_m0.add(1)
    v_zf = s_zf.v

    NJ0 = 64
    for tt in range(NT):
        for fm in range(NFM):
            grp = tt * NFM + fm
            for j in range(NJ0):
                it = grp * NJ0 + j

                @p.on("gpsimd")
                def _(g, tt=tt, fm=fm, j=j, it=it):
                    if it == 0:
                        g.wait_ge(s_gtp.h, v_gfm)
                        g.wait_ge(s_zf.h, v_zf)
                    if it >= 2:
                        g.wait_ge(s_gl.h, it - 1)
                    g.tensor_tensor(dif[it % 2][:],
                                    G_FM[fm][:, tt * 128 + j:tt * 128 + j + 128],
                                    G_FM[fm][:, 63 + tt * 128:63 + tt * 128 + 128],
                                    ALU.subtract).then_inc(s_df.h, 1)
                v_df = s_df.add()

                @p.on("scalar")
                def _(s, j=j, it=it, v_df=v_df):
                    s.wait_ge(s_df.h, v_df)
                    if it >= 2:
                        s.wait_ge(s_dot.h, it - 1)
                    s.activation(gel[it % 2][:], dif[it % 2][:], AF.Gelu,
                                 bias=PFB[:, j:j + 1]).then_inc(s_gl.h, 1)
                v_gl = s_gl.add()

                @p.on("tensor")
                def _(t, j=j, it=it, grp=grp, v_gl=v_gl):
                    t.wait_ge(s_gl.h, v_gl)
                    if j == 0 and grp >= 2:
                        t.wait_ge(s_sev.h, 2 * (grp - 1))
                    t.matmul(psg[grp % 2][:, j * 8:j * 8 + 8],
                             gel[it % 2][:], wbdm[:],
                             start=(j == 0), stop=(j == NJ0 - 1)
                             ).then_inc(s_dot.h, 1)
                v_dt = s_dot.add()

            @p.on("vector")
            def _(v, tt=tt, fm=fm, grp=grp, v_dt=s_dot.v):
                v.wait_ge(s_dot.h, v_dt)
                if fm == 0 and tt >= 1:
                    v.wait_ge(s_sm.h, (tt - 1) * 4 + 3)
                pv = psg[grp % 2][:].rearrange("p (j g) -> p g j", g=8)
                v.tensor_copy(S_ALL[:, fm * 256:(fm + 1) * 256].rearrange(
                    "p (hh j) -> p hh j", hh=4), pv[:, 0:4, :])
                v.tensor_copy(DMG[:, fm * 256:(fm + 1) * 256].rearrange(
                    "p (hh j) -> p hh j", hh=4), pv[:, 4:8, :]
                              ).then_inc(s_sev.h, 2)
            s_sev.add(2)
        v_sc = s_sev.v

        if DBG_B0_PHASE == 4:
            @p.on("gpsimd")
            def _(g, tt=tt, v_sc=v_sc):
                g.wait_ge(s_sev.h, v_sc)
                if tt:
                    g.wait_ge(s_yst.h, 16 * tt)
                g.dma_start(out=y_out[tt * 128:(tt + 1) * 128, :],
                            in_=S_ALL[:]).then_inc(s_yst.h, 16)
                g.wait_ge(s_yst.h, 16 * (tt + 1))
            s_yst.add(16)
            @p.on("vector")
            def _(v, tt=tt):
                v.wait_ge(s_yst.h, 16 * (tt + 1))
                v.memset(dum3[:], 0.0).then_inc(s_sm.h, 4)
            s_sm.add(4)
            continue

        fl0 = fl.v

        @p.on("scalar")
        def _(s, tt=tt, v_sc=v_sc):
            s.wait_ge(s_sev.h, v_sc)
            s.activation(DMG[:], DMG[:], AF.Sigmoid, bias=bd_pp[:]
                         ).then_inc(s_sm.h, 1)
        s_sm.add()
        v_sg = s_sm.v

        @p.on("vector")
        def _(v, tt=tt, v_sg=v_sg, fl0=fl0):
            v.wait_ge(s_sm.h, v_sg)
            if tt == 0:
                v.wait_ge(s_m0.h, 1)
            v.scalar_tensor_tensor(S_ALL[:], DMG[:], p.constf(-10.0),
                                   S_ALL[:], ALU.mult, ALU.add
                                   ).then_inc(fl.h, 1)
            v.wait_ge(fl.h, fl0 + 1)
            v.scalar_tensor_tensor(S_ALL[:], S_ALL[:], bb_pp[:], S_ALL[:],
                                   ALU.add, ALU.bypass).then_inc(fl.h, 1)
            v.wait_ge(fl.h, fl0 + 2)
            if tt == 0:
                for h in range(16):
                    v.scalar_tensor_tensor(
                        S_ALL[:, h * 64:(h + 1) * 64], M0[:], ise_pp[:],
                        S_ALL[:, h * 64:(h + 1) * 64], ALU.mult, ALU.add)
            v.memset(dum3[:], 0.0).then_inc(fl.h, 1)
            v.wait_ge(fl.h, fl0 + 3)
            v.tensor_reduce(m16[:].rearrange("p (g o) -> p g o", o=1),
                            S_ALL[:].rearrange("p (g k) -> p g k", g=16),
                            AX.X, ALU.max).then_inc(fl.h, 1)
            v.wait_ge(fl.h, fl0 + 4)
            v.tensor_scalar(m16[:], m16[:], -1.0, 0.0, ALU.mult, ALU.add
                            ).then_inc(s_sm.h, 1)
        fl.add(4)
        s_sm.add()
        v_ng = s_sm.v

        @p.on("scalar")
        def _(s, tt=tt, v_ng=v_ng):
            s.wait_ge(s_sm.h, v_ng)
            ins = None
            for h in range(16):
                ins = s.activation(A_E[:, h * 64:(h + 1) * 64],
                                   S_ALL[:, h * 64:(h + 1) * 64], AF.Exp,
                                   bias=m16[:, h:h + 1],
                                   accum_out=den16[:, h:h + 1])
            ins.then_inc(s_sm.h, 1)
        s_sm.add()
        v_ex = s_sm.v

        fl0 = fl.v

        @p.on("vector")
        def _(v, tt=tt, v_ex=v_ex, fl0=fl0):
            v.wait_ge(s_sm.h, v_ex)
            if tt >= 1:
                v.wait_ge((s_yst if DBG_B0_PHASE == 3 else s_spr).h, 16 * tt)
            v.reciprocal(den16[:], den16[:]).then_inc(fl.h, 1)
            v.wait_ge(fl.h, fl0 + 1)
            ins = None
            for h in range(16):
                ins = v.scalar_tensor_tensor(A_N[:, h * 64:(h + 1) * 64],
                                             A_E[:, h * 64:(h + 1) * 64],
                                             den16[:, h:h + 1],
                                             A_E[:, h * 64:(h + 1) * 64],
                                             ALU.mult, ALU.bypass)
            ins.then_inc(s_sm.h, 1)
        fl.add(1)
        s_sm.add()
        v_an = s_sm.v

        if DBG_B0_PHASE == 3:
            @p.on("gpsimd")
            def _(g, tt=tt, v_an=v_an):
                g.wait_ge(s_sm.h, v_an)
                if tt:
                    g.wait_ge(s_yst.h, 16 * tt)
                g.dma_start(out=y_out[tt * 128:(tt + 1) * 128, :],
                            in_=A_N[:]).then_inc(s_yst.h, 16)
            s_yst.add(16)
            v_dummy_sp = s_spr.add(16)

            @p.on("gpsimd")
            def _(g, tt=tt):
                g.then_inc_dummy if False else None
                g.wait_ge(s_yst.h, 16 * (tt + 1))
            continue_marker = True
            continue
        @p.on("gpsimd")
        def _(g, tt=tt, v_an=v_an):
            g.wait_ge(s_sm.h, v_an)
            if tt >= 2:
                g.wait_ge(s_arl.h, 16 * tt)
            g.dma_start(
                out=ash[0:16 * 128 * 196].rearrange("(h tq) -> h tq", h=16
                ).rearrange("h (t q) -> t h q", q=196)[:, :, 1:65],
                in_=A_N[:].rearrange("p (h j) -> p h j", h=16)
            ).then_inc(s_spr.h, 16)
        s_spr.add(16)
        v_sp = s_spr.v

        @p.on("sync")
        def _(sy, tt=tt, v_sp=v_sp):
            sy.wait_ge(s_spr.h, v_sp)
            if tt >= 2:
                sy.wait_ge(s_atm.h, (tt - 1) * 16)
            sy.dma_start(
                out=asht[tt % 2][:],
                in_=ash[0:16 * 128 * 196].rearrange("(h tq) -> h tq", h=16
                )[:, 0:195 * 128].rearrange("h (t q) -> t h q", q=195)
            ).then_inc(s_arl.h, 16)
        s_arl.add(16)
        v_rl = s_arl.v

        for h in range(16):
            kq = tt * 16 + h

            @p.on("tensor")
            def _(t, tt=tt, h=h, kq=kq, v_rl=v_rl):
                t.wait_ge(s_arl.h, v_rl)
                if kq >= 2:
                    t.wait_ge(s_ate.h, kq - 1)
                t.transpose(pta[kq % 2],
                            asht[tt % 2][:, h * 195:h * 195 + 128],
                            identb[:])
                t.transpose(pat[kq % 2][0:67, 128:256] if False else
                            pat[kq % 2][:, 128:256][0:67, :],
                            asht[tt % 2][:, h * 195 + 128:h * 195 + 195],
                            identb[:]).then_inc(s_atm.h, 1)
            v_tm = s_atm.add()

            @p.on("vector")
            def _(v, h=h, kq=kq, v_tm=v_tm):
                v.wait_ge(s_atm.h, v_tm)
                if kq >= 2:
                    v.wait_ge(s_vvm.h, kq - 1)
                v.tensor_copy(at0[kq % 2][:], pta[kq % 2])
                v.tensor_copy(at1[kq % 2][0:67, :], pat[kq % 2][0:67, 128:256]
                              ).then_inc(s_ate.h, 1)
            v_te = s_ate.add()

            @p.on("tensor")
            def _(t, tt=tt, h=h, kq=kq, v_te=v_te):
                t.wait_ge(s_ate.h, v_te)
                if kq == 0:
                    t.wait_ge(s_bev.h, v_vb)
                if kq >= 2:
                    t.wait_ge(s_ovev.h, kq - 1)
                t.matmul(pvv[kq % 2], at0[kq % 2][:],
                         VALB[tt][:, h * 64:(h + 1) * 64],
                         start=True, stop=False)
                t.matmul(pvv[kq % 2], at1[kq % 2][0:64, :],
                         VALB[tt + 1][0:64, h * 64:(h + 1) * 64],
                         start=False, stop=True).then_inc(s_vvm.h, 1)
            v_vv = s_vvm.add()

            @p.on("vector")
            def _(v, tt=tt, h=h, kq=kq, v_vv=v_vv):
                v.wait_ge(s_vvm.h, v_vv)
                if h == 0 and tt >= 2:
                    v.wait_ge(s_otp.h, (tt - 1) * ND)
                v.tensor_copy(out_t[tt % 2][:, h * 64:(h + 1) * 64],
                              pvv[kq % 2]).then_inc(s_ovev.h, 1)
            s_ovev.add()
        v_out = s_ovev.v

        obf = out_t[tt % 2]

        @p.on("sync")
        def _(sy, tt=tt):
            if tt >= 1:
                sy.wait_ge(s_xr3.h, 16 * tt)
            if tt >= 2:
                sy.wait_ge(s_yst.h, 32 * (tt - 1))
            sy.dma_start(out=xr3[tt % 2][:], in_=x_in[tt * 128:(tt + 1) * 128, :]
                         ).then_inc(s_xr3.h, 16)
        s_xr3.add(16)
        v_xr = s_xr3.v

        fl0 = fl.v

        @p.on("vector")
        def _(v, tt=tt, v_out=v_out, fl0=fl0):
            v.wait_ge(s_ovev.h, v_out)
            if tt >= 1:
                v.wait_ge(s_otm.h, tt * ND)
            v.tensor_copy(obfb[:], out_t[tt % 2][:]).then_inc(fl.h, 1)
        fl.add(1)
        v_ob = fl.v

        for c in range(ND):
            k3 = tt * ND + c

            @p.on("tensor")
            def _(t, tt=tt, c=c, k3=k3, v_ob=v_ob):
                t.wait_ge(fl.h, v_ob)
                if k3 >= 2:
                    t.wait_ge(s_otp.h, k3 - 1)
                t.transpose(pto[k3 % 2],
                            obfb[:, c * 128:(c + 1) * 128],
                            identb[:]).then_inc(s_otm.h, 1)
            v_tm = s_otm.add()

            @p.on("vector")
            def _(v, c=c, k3=k3, v_tm=v_tm):
                v.wait_ge(s_otm.h, v_tm)
                if k3 >= 2:
                    v.wait_ge(s_cm.h, max(0, (k3 // ND - 1) * 2))
                v.tensor_copy(ofm8[c][:], pto[k3 % 2]).then_inc(s_otp.h, 1)
            s_otp.add()
        v_ofm = s_otp.v

        for oc in range(2):
            g3 = tt * 2 + oc

            @p.on("tensor")
            def _(t, tt=tt, oc=oc, g3=g3, v_ofm=v_ofm):
                t.wait_ge(s_otp.h, v_ofm)
                if g3 == 0:
                    t.wait_ge(s_wc.h, 16 * ND)
                if g3 >= 2:
                    t.wait_ge(s_cev.h, g3 - 1)
                ins = None
                for c in range(ND):
                    ins = t.matmul(pcp[g3 % 2][:], ofm8[c][:],
                                   wc_sb[c][:, oc * 512:(oc + 1) * 512],
                                   start=(c == 0), stop=(c == ND - 1))
                ins.then_inc(s_cm.h, 1)
            v_cm = s_cm.add()

            fl0 = fl.v

            @p.on("vector")
            def _(v, tt=tt, oc=oc, g3=g3, v_cm=v_cm, v_xr=v_xr, fl0=fl0):
                v.wait_ge(s_cm.h, v_cm)
                v.wait_ge(s_xr3.h, v_xr)
                if g3 >= 2:
                    v.wait_ge(s_yst.h, 16 * (g3 - 1))
                v.tensor_tensor(yt3[oc][:], pcp[g3 % 2][:],
                                bcp_bc[:, oc * 512:(oc + 1) * 512], ALU.add
                                ).then_inc(fl.h, 1)
                v.wait_ge(fl.h, fl0 + 1)
                v.tensor_tensor(yt3[oc][:], yt3[oc][:],
                                xr3[tt % 2][:, oc * 512:(oc + 1) * 512],
                                ALU.add).then_inc(s_cev.h, 1)
            fl.add(1)
            v_ce = s_cev.add()

            @p.on("gpsimd")
            def _(g, tt=tt, oc=oc, g3=g3, v_ce=v_ce):
                g.wait_ge(s_cev.h, v_ce)
                if g3 >= 1:
                    g.wait_ge(s_yst.h, 16 * g3)
                g.dma_start(out=y_out[tt * 128:(tt + 1) * 128,
                                      oc * 512:(oc + 1) * 512],
                            in_=yt3[oc][:]).then_inc(s_yst.h, 16)
            s_yst.add(16)
    sc3.close()
    fin_yst = (16 * NT) if DBG_B0_PHASE in (3, 4) else 32 * NT

    @p.on("gpsimd")
    def _(g, fin_yst=fin_yst):
        g.wait_ge(s_yst.h, fin_yst)

_CACHE = {}


def _get_prog(key, emit):
    if key not in _CACHE:
        _CACHE[key] = build_program(emit)
    return _CACHE[key]


def shard_x(x_full):
    out = []
    for c in range(NCORE):
        b, h = c // 2, c % 2
        out.append(np.ascontiguousarray(x_full[b, h * T:(h + 1) * T, :]))
    return out


def unshard_x(shards):
    x = np.empty((4, 4096, D), np.float32)
    for c in range(NCORE):
        b, h = c // 2, c % 2
        x[b, h * T:(h + 1) * T, :] = shards[c]
    return x


def run_f0(xs, W, b, lora_down, lora_up, down_aux, up_aux):
    nc, cvals = _get_prog("f0", emit_f0)
    base = {
        "cpool": cpool_array(cvals),
        "WT": np.ascontiguousarray(W.T),
        "b": np.ascontiguousarray(b.reshape(1, D)),
        "lora_down": lora_down, "lora_up": lora_up,
        "lora_downT": np.ascontiguousarray(lora_down.T),
        "lora_upT": np.ascontiguousarray(lora_up.T),
        "down_aux": down_aux,
        "up_auxT": np.ascontiguousarray(up_aux.T),
        "ident_bf": IDENT_BF,
        "ones1": np.ones((1, 128), np.float32),
    }
    in_maps = [dict(base, x=xs[c]) for c in range(NCORE)]
    res = run_bass_kernel_spmd(nc, in_maps, core_ids=list(range(NCORE)))
    return [np.asarray(r["y"]) for r in res.results]


def split_qkv_l0(W_qkv, b_qkv):
    cols = np.arange(3072).reshape(16, 3, 64)
    Wq = np.ascontiguousarray(W_qkv[:, cols[:, 0, :].reshape(-1)])
    Wk = np.ascontiguousarray(W_qkv[:, cols[:, 1, :].reshape(-1)])
    Wv = np.ascontiguousarray(W_qkv[:, cols[:, 2, :].reshape(-1)])
    bq = np.ascontiguousarray(b_qkv[cols[:, 0, :].reshape(-1)].reshape(1, -1))
    bk = np.ascontiguousarray(b_qkv[cols[:, 1, :].reshape(-1)].reshape(1, -1))
    bv = np.ascontiguousarray(b_qkv[cols[:, 2, :].reshape(-1)].reshape(1, -1))
    return Wq, Wk, Wv, bq, bk, bv


def run_attn_a(xs, variant, **w):
    nc, cvals = _get_prog(f"attn{variant}",
                          lambda nc, p, io: emit_attn_a(nc, p, io, variant))
    if variant == 0:
        Wq, Wk, Wv, bq, bk, bv = split_qkv_l0(w["W_qkv"], w["b_qkv"])
        Wo, bo = w["W_out"], w["b_out"].reshape(1, D)
    else:
        Wq, Wk, Wv, Wo = w["Wq"], w["Wk"], w["Wv"], w["Wo"]
        bq, bk, bv = (w["bq"].reshape(1, D), w["bk"].reshape(1, D),
                      w["bv"].reshape(1, D))
        bo = w["bo"].reshape(1, D)
    base = {
        "cpool": cpool_array(cvals),
        "Wq": Wq, "Wk": Wk, "Wv": Wv, "Wo": Wo,
        "bq": bq, "bk": bk, "bv": bv, "bo": bo,
        "ident_bf": IDENT_BF, "ident_f": np.eye(128, dtype=np.float32),
        "ones1": np.ones((1, 128), np.float32),
    }
    in_maps = []
    for c in range(NCORE):
        pair = c ^ 1
        in_maps.append(dict(base, x=xs[c], xp=xs[pair]))
    res = run_bass_kernel_spmd(nc, in_maps, core_ids=list(range(NCORE)))
    return [np.asarray(r["y"]) for r in res.results]


def run_f1(xs, w_gate, w_up, w_down):
    nc, cvals = _get_prog("f1", lambda nc, p, io: emit_f1(nc, p, io))
    base = {"cpool": cpool_array(cvals), "w_gate": w_gate, "w_up": w_up,
            "w_down": w_down, "ident_bf": IDENT_BF}
    in_maps = [dict(base, x=xs[c]) for c in range(NCORE)]
    res = run_bass_kernel_spmd(nc, in_maps, core_ids=list(range(NCORE)))
    return [np.asarray(r["y"]) for r in res.results]


def run_b1(xs, w):
    nc, cvals = _get_prog("b1", lambda nc, p, io: emit_b1(nc, p, io))
    base = {
        "cpool": cpool_array(cvals),
        "pnw": w["pre_norm_w"].reshape(1, D),
        "cw4": np.ascontiguousarray(w["conv_w"][:, 0, :].T),
        "cb": w["conv_b"].reshape(1, D),
        "waT": np.ascontiguousarray(w["w_a"].T),
        "wiT": np.ascontiguousarray(w["w_i"].T),
        "wvT": np.ascontiguousarray(w["w_v"].T),
        "db": w["decay_bias"].reshape(1, D),
        "woT": np.ascontiguousarray(w["out_proj_w"].T),
        "fnw": w["ffn_norm_w"].reshape(1, D),
        "wguT": np.ascontiguousarray(w["w_gate_up"].T),
        "wdnT": np.ascontiguousarray(w["w_down"].T),
        "ident_bf": IDENT_BF, "ident_f": np.eye(128, dtype=np.float32),
    }
    zeros3 = np.zeros((3, D), np.float32)
    in_maps = []
    for c in range(NCORE):
        xl3 = np.ascontiguousarray(xs[c - 1][T - 3:]) if c % 2 else zeros3
        in_maps.append(dict(base, x=xs[c], xl3=xl3,
                            is_odd=np.array([[float(c % 2)]], np.float32)))
    res = run_bass_kernel_spmd(nc, in_maps, core_ids=list(range(NCORE)))
    return [np.asarray(r["y"]) for r in res.results]


IDENT2 = None


def _ident2():
    global IDENT2
    if IDENT2 is None:
        a = np.zeros((128, 256), ml_dtypes.bfloat16)
        for q in range(128):
            a[q, q + 64] = 1.0
        IDENT2 = a
    return IDENT2


def run_b0(xs, w):
    nc, cvals = _get_prog("b0", lambda nc, p, io: emit_b0(nc, p, io))
    wbdm = np.zeros((128, 8), np.float32)
    for hh in range(4):
        wbdm[hh * 32:hh * 32 + 16, hh] = w["W_bond"][:, 0]
        wbdm[hh * 32 + 16:hh * 32 + 32, 4 + hh] = w["W_dmg"][:, 0]
    base = {
        "cpool": cpool_array(cvals),
        "wdispT": np.ascontiguousarray(w["W_disp"].T),
        "wfu": w["W_fused"],
        "bfu": w["b_fused"].reshape(1, 32),
        "rpeT": np.ascontiguousarray(w["rel_pos_emb"].T),
        "wpos": w["W_pos"],
        "wbdm": wbdm.astype(ml_dtypes.bfloat16),
        "b_bond": w["b_bond"].reshape(1, 1),
        "b_dmg": w["b_dmg"].reshape(1, 1),
        "wvT": w["W_val"],
        "b_val": w["b_val"].reshape(1, D),
        "wcT": w["W_cproj"],
        "b_cproj": w["b_cproj"].reshape(1, D),
        "ident_bf": IDENT_BF,
        "ident2": _ident2(),
        "zrow": np.zeros((1, 256), np.float32),
    }
    zeros63 = np.zeros((63, D), np.float32)
    in_maps = []
    for c in range(NCORE):
        xprev = np.ascontiguousarray(xs[c - 1][T - 63:]) if c % 2 else zeros63
        in_maps.append(dict(base, x=xs[c], xprev=xprev,
                            is_even=np.array([[float(1 - c % 2)]], np.float32)))
    res = run_bass_kernel_spmd(nc, in_maps, core_ids=list(range(NCORE)))
    return [np.asarray(r["y"]) for r in res.results]


def _attn_base(variant, g):
    if variant == 0:
        Wq, Wk, Wv, bq, bk, bv = split_qkv_l0(g["L0_attn_a_W_qkv"],
                                              g["L0_attn_a_b_qkv"])
        Wo, bo = g["L0_attn_a_W_out"], g["L0_attn_a_b_out"].reshape(1, D)
    else:
        Wq, Wk, Wv, Wo = (g["L1_attn_a_Wq"], g["L1_attn_a_Wk"],
                          g["L1_attn_a_Wv"], g["L1_attn_a_Wo"])
        bq = g["L1_attn_a_bq"].reshape(1, D)
        bk = g["L1_attn_a_bk"].reshape(1, D)
        bv = g["L1_attn_a_bv"].reshape(1, D)
        bo = g["L1_attn_a_bo"].reshape(1, D)
    return {"Wq": Wq, "Wk": Wk, "Wv": Wv, "Wo": Wo,
            "bq": bq, "bk": bk, "bv": bv, "bo": bo,
            "ident_bf": IDENT_BF, "ident_f": np.eye(128, dtype=np.float32),
            "ones1": np.ones((1, 128), np.float32)}


def _f0_base(g):
    W, b = g["L0_ffn_W"], g["L0_ffn_b"]
    return {"WT": np.ascontiguousarray(W.T),
            "b": np.ascontiguousarray(b.reshape(1, D)),
            "lora_down": g["L0_ffn_lora_down"], "lora_up": g["L0_ffn_lora_up"],
            "lora_downT": np.ascontiguousarray(g["L0_ffn_lora_down"].T),
            "lora_upT": np.ascontiguousarray(g["L0_ffn_lora_up"].T),
            "down_aux": g["L0_ffn_down_aux"],
            "up_auxT": np.ascontiguousarray(g["L0_ffn_up_aux"].T),
            "ident_bf": IDENT_BF, "ones1": np.ones((1, 128), np.float32)}


def _f1_base(g):
    return {"w_gate": g["L1_ffn_w_gate"], "w_up": g["L1_ffn_w_up"],
            "w_down": g["L1_ffn_w_down"], "ident_bf": IDENT_BF}


def _b0_base(g):
    w = {k[len("L0_attn_b_"):]: np.asarray(v) for k, v in g.items()
         if k.startswith("L0_attn_b_")}
    wbdm = np.zeros((128, 8), np.float32)
    for hh in range(4):
        wbdm[hh * 32:hh * 32 + 16, hh] = w["W_bond"][:, 0]
        wbdm[hh * 32 + 16:hh * 32 + 32, 4 + hh] = w["W_dmg"][:, 0]
    return {"wdispT": np.ascontiguousarray(w["W_disp"].T), "wfu": w["W_fused"],
            "bfu": w["b_fused"].reshape(1, 32),
            "rpeT": np.ascontiguousarray(w["rel_pos_emb"].T),
            "wpos": w["W_pos"], "wbdm": wbdm.astype(ml_dtypes.bfloat16),
            "b_bond": w["b_bond"].reshape(1, 1),
            "b_dmg": w["b_dmg"].reshape(1, 1),
            "wvT": w["W_val"], "b_val": w["b_val"].reshape(1, D),
            "wcT": w["W_cproj"], "b_cproj": w["b_cproj"].reshape(1, D),
            "ident_bf": IDENT_BF, "ident2": _ident2(),
            "zrow": np.zeros((1, 256), np.float32)}


def _b1_base(g):
    w = {k[len("L1_attn_b_"):]: np.asarray(v) for k, v in g.items()
         if k.startswith("L1_attn_b_")}
    return {"pnw": w["pre_norm_w"].reshape(1, D),
            "cw4": np.ascontiguousarray(w["conv_w"][:, 0, :].T),
            "cb": w["conv_b"].reshape(1, D),
            "waT": np.ascontiguousarray(w["w_a"].T),
            "wiT": np.ascontiguousarray(w["w_i"].T),
            "wvT": np.ascontiguousarray(w["w_v"].T),
            "db": w["decay_bias"].reshape(1, D),
            "woT": np.ascontiguousarray(w["out_proj_w"].T),
            "fnw": w["ffn_norm_w"].reshape(1, D),
            "wguT": np.ascontiguousarray(w["w_gate_up"].T),
            "wdnT": np.ascontiguousarray(w["w_down"].T),
            "ident_bf": IDENT_BF, "ident_f": np.eye(128, dtype=np.float32)}


def _get_prog_pair(key, emit_a, emit_b):
    if key not in _CACHE:
        _CACHE[key] = build_program_pair(emit_a, emit_b)
    return _CACHE[key]


def run_pair(key, emit_a, emit_b, base_a, base_b, percore_a):
    nc, cvals = _get_prog_pair(key, emit_a, emit_b)
    base = {"cpool": cpool_array(cvals)}
    base.update({"a_" + k: v for k, v in base_a.items()})
    base.update({"b_" + k: v for k, v in base_b.items()})
    in_maps = []
    for c in range(NCORE):
        m = dict(base)
        m.update({"a_" + k: v for k, v in percore_a(c).items()})
        in_maps.append(m)
    res = run_bass_kernel_spmd(nc, in_maps, core_ids=list(range(NCORE)))
    return [np.asarray(r["b_y"]) for r in res.results]


def kernel(**inputs):
    g = inputs
    zeros63 = np.zeros((63, D), np.float32)
    xs = shard_x(np.asarray(g["L0_attn_a_X"], np.float32))
    xs = run_pair("p_a0f0", lambda nc, p, io: emit_attn_a(nc, p, io, 0),
                  emit_f0, _attn_base(0, g), _f0_base(g),
                  lambda c: {"x": xs[c], "xp": xs[c ^ 1]})
    xs = run_pair("p_b0f0", emit_b0, emit_f0, _b0_base(g), _f0_base(g),
                  lambda c: {"x": xs[c],
                             "xprev": (np.ascontiguousarray(xs[c - 1][T - 63:])
                                       if c % 2 else zeros63),
                             "is_even": np.array([[float(1 - c % 2)]],
                                                 np.float32)})
    xs = run_pair("p_a1f1", lambda nc, p, io: emit_attn_a(nc, p, io, 1),
                  emit_f1, _attn_base(1, g), _f1_base(g),
                  lambda c: {"x": xs[c], "xp": xs[c ^ 1]})
    xs = run_b1(xs, {k[len("L1_attn_b_"):]: np.asarray(v) for k, v in g.items()
                     if k.startswith("L1_attn_b_")})
    xs = run_f1(xs, g["L1_ffn_w_gate"], g["L1_ffn_w_up"], g["L1_ffn_w_down"])
    return unshard_x([np.asarray(s, np.float32) for s in xs])


def kernel_v1(**inputs):
    g = inputs
    xs = shard_x(np.asarray(g["L0_attn_a_X"], np.float32))
    xs = run_attn_a(xs, 0, W_qkv=g["L0_attn_a_W_qkv"], b_qkv=g["L0_attn_a_b_qkv"],
                    W_out=g["L0_attn_a_W_out"], b_out=g["L0_attn_a_b_out"])
    xs = run_f0(xs, g["L0_ffn_W"], g["L0_ffn_b"], g["L0_ffn_lora_down"],
                g["L0_ffn_lora_up"], g["L0_ffn_down_aux"], g["L0_ffn_up_aux"])
    xs = run_b0(xs, {k[len("L0_attn_b_"):]: np.asarray(v) for k, v in g.items()
                     if k.startswith("L0_attn_b_")})
    xs = run_f0(xs, g["L0_ffn_W"], g["L0_ffn_b"], g["L0_ffn_lora_down"],
                g["L0_ffn_lora_up"], g["L0_ffn_down_aux"], g["L0_ffn_up_aux"])
    xs = run_pair("p_a1f1", lambda nc, p, io: emit_attn_a(nc, p, io, 1),
                  emit_f1, _attn_base(1, g), _f1_base(g),
                  lambda c: {"x": xs[c], "xp": xs[c ^ 1]})
    xs = run_b1(xs, {k[len("L1_attn_b_"):]: np.asarray(v) for k, v in g.items()
                     if k.startswith("L1_attn_b_")})
    xs = run_f1(xs, g["L1_ffn_w_gate"], g["L1_ffn_w_up"], g["L1_ffn_w_down"])
    return unshard_x([np.asarray(s, np.float32) for s in xs])
